# version 29 (final)
# baseline (speedup 1.0000x reference)
import numpy as np
import ml_dtypes
from contextlib import ExitStack
import concourse.bass as bass
import concourse.mybir as mybir
from concourse.bass_utils import run_bass_kernel_spmd

F32 = mybir.dt.float32
BF16 = mybir.dt.bfloat16
I32 = mybir.dt.int32
U32 = mybir.dt.uint32
AF = mybir.ActivationFunctionType
ALU = mybir.AluOpType
AX = mybir.AxisListType

PHASES = (1, 2, 3, 4)
NBATCH = 2
A2_LIMIT = 99
B_PER_CORE = 2
T = 2048
D = 1024
NCORE = 8
EPS_RMS = 1.1920929e-07
CONST_VALS = [0.0, 1.0, -1.0, 0.5, 1e-5, EPS_RMS, 240.0, 0.125, 1.0 / 1024.0, 339.41125496954285]


ENGINES = ("sync", "scalar", "vector", "tensor", "gpsimd")


def _is_psum(x):
    try:
        sp = x.space
        sp = sp() if callable(sp) else sp
        return "PSUM" in str(sp).upper()
    except Exception:
        return False


def _tname(x):
    if hasattr(x, "tensor"):
        return x.tensor.name
    if hasattr(x, "name"):
        return x.name
    return str(x)


class Op:
    __slots__ = ("engine", "fn", "deps", "amount", "signaled", "val", "unsafe", "tag", "sem_key")

    def __init__(self, engine, fn, amount, unsafe, tag):
        self.engine = engine
        self.fn = fn
        self.deps = []
        self.amount = amount
        self.signaled = False
        self.val = None
        self.unsafe = unsafe
        self.tag = tag
        self.sem_key = engine


class Prog:
    def __init__(self, nc):
        self.nc = nc
        self.streams = {e: [] for e in ENGINES}
        self.last_writer = {}
        self.readers = {}
        self.n_ops = 0
        self.fence_toks = []

    def fence(self):
        toks = {}
        for e in ENGINES:
            for o in self.streams[e]:
                if o.sem_key.startswith("cc_"):
                    continue
                toks[(e, o.sem_key)] = o
        self.fence_toks = list(toks.values())

    def do(self, engine, fn, reads=(), writes=(), deps=(), dma=False, unsafe=False, tag="",
           skey=None):
        o = Op(engine, fn, 16 if dma else 1, unsafe, tag)
        if dma:
            assert len(writes) >= 1
            if skey is None:
                skey = _tname(writes[0])
                for pre in ("a_", "b_", "c_", "d_"):
                    if skey.startswith(pre):
                        skey = skey[2:]
                        break
            o.sem_key = "dma_" + skey
            o.signaled = True
        deps = set(d for d in deps if d is not None)
        deps.update(self.fence_toks)
        for r in reads:
            if r is None:
                continue
            t = _tname(r)
            lw = self.last_writer.get(t)
            if lw is not None:
                deps.add(lw)
            if _is_psum(r):
                for rd in self.readers.get(t, ()):
                    if rd.engine != engine:
                        deps.add(rd)
        for w in writes:
            if w is None:
                continue
            t = _tname(w)
            lw = self.last_writer.get(t)
            if lw is not None:
                deps.add(lw)
            for rd in self.readers.get(t, ()):
                deps.add(rd)
        deps.discard(o)
        o.deps = [d for d in deps]
        for d in o.deps:
            d.signaled = True
        self.streams[engine].append(o)
        self.n_ops += 1
        for r in reads:
            if r is None:
                continue
            self.readers.setdefault(_tname(r), []).append(o)
        for w in writes:
            if w is None:
                continue
            t = _tname(w)
            self.last_writer[t] = o
            self.readers[t] = []
        return o

    def emit(self, block, stack):
        keys = set()
        counters = {}
        for e in ENGINES:
            for o in self.streams[e]:
                if o.signaled:
                    keys.add(o.sem_key)
                    c = counters.get(o.sem_key, 0) + o.amount
                    counters[o.sem_key] = c
                    o.val = c
        assert len(keys) <= 100, f"too many semaphores: {len(keys)}"
        sems = {k: stack.enter_context(self.nc.semaphore(f"s_{k}"[:40])) for k in sorted(keys)}

        def make_body(ename):
            ops = self.streams[ename]

            def body(eng):
                watermark = {}
                for o in ops:
                    need = {}
                    for d in o.deps:
                        if (d.engine == ename == "tensor" and not d.unsafe
                                and d.sem_key == ename):
                            continue
                        assert d.val is not None, f"dep of {o.tag} not signaled"
                        need[d.sem_key] = max(need.get(d.sem_key, 0), d.val)
                    for s, v in sorted(need.items()):
                        if watermark.get(s, 0) < v:
                            eng.wait_ge(sems[s], v)
                            watermark[s] = v
                    inst = o.fn(eng)
                    if o.signaled:
                        if inst is None:
                            eng.sem_inc(sems[o.sem_key], o.amount)
                        else:
                            inst.then_inc(sems[o.sem_key], o.amount)
            return body

        for ename in ENGINES:
            if self.streams[ename]:
                getattr(block, ename)(make_body(ename))



def host_constants():
    c = {}
    c["iden_f"] = np.eye(128, dtype=np.float32)
    mw = np.zeros((128, 896), np.float32)
    mw[:, 0:384] = -1e30
    mw[:, 384:512] = np.where(np.arange(128)[None, :] >= np.arange(128)[:, None], 0.0, -1e30)
    c["diagmask4"] = mw
    R = np.zeros((8, 64, 128), np.float32)
    for i in range(8):
        for p_ in range(128):
            R[i, 8 * i + p_ // 16, p_] = 1.0
    c["Rrep"] = np.concatenate([R[i] for i in range(8)], axis=1)
    RB = np.zeros((16, 128), np.float32)
    for p_ in range(128):
        RB[p_ % 16, p_] = 1.0
    c["RB"] = RB
    G = np.transpose(R, (0, 2, 1))
    c["Grep"] = np.concatenate([G[i] for i in range(8)], axis=1)
    c["ones_col"] = np.ones((128, 1), np.float32)
    c["ones_row"] = np.ones((1, 128), np.float32)
    c["cvals"] = np.broadcast_to(np.array(CONST_VALS, np.float32), (128, len(CONST_VALS))).copy()
    return c


HOST_CONST_SHAPES = {k: v.shape for k, v in host_constants().items()}



def prep_wT(p, st, W_ap, F, Din, iden_b, name, psA, psB):
    nc = p.nc
    nchunk = -(-Din // 128)
    wT = [st.enter_context(nc.sbuf_tensor(f"{name}_T{c}", [min(128, Din - 128 * c), F], BF16))
          for c in range(nchunk)]
    nrow = -(-F // 128)
    tmp = [st.enter_context(nc.sbuf_tensor(f"{name}_tmpp{i}", [min(128, F), Din], BF16)) for i in range(2)]
    for r in range(nrow):
        buf = tmp[r % 2]
        rows = min(128, F - 128 * r)
        p.do("gpsimd", lambda g, r=r, buf=buf, rows=rows: g.dma_start(
            out=buf[0:rows, :], in_=W_ap[r * 128:r * 128 + rows, :]),
            reads=[W_ap], writes=[buf], dma=True, tag=f"{name}ld")
        for c in range(nchunk):
            cols = min(128, Din - 128 * c)
            ps = psA if (r * nchunk + c) % 2 == 0 else psB
            p.do("tensor", lambda t, buf=buf, c=c, ps=ps, rows=rows, cols=cols: t.transpose(
                ps.bitcast(BF16)[0:cols, 0:rows], buf[0:rows, c * 128:c * 128 + cols], iden_b[:]),
                reads=[buf, iden_b], writes=[ps], tag=f"{name}tr")
            p.do("vector", lambda v, ps=ps, c=c, r=r, rows=rows, cols=cols: v.tensor_copy(
                out=wT[c][0:cols, r * 128:r * 128 + rows], in_=ps.bitcast(BF16)[0:cols, 0:rows]),
                reads=[ps], writes=[wT[c]], tag=f"{name}ev")
    return wT



def KK_A2L():
    return A2_LIMIT


def build_l0_attn(p, st, io):
    nc = p.nc
    pfx = io.get("pfx", "")
    NT, NC, NQC = T // 128, D // 128, T // 512
    x_ap, out_ap, w = io["x"], io["out"], io

    def sb(st_, name, shape, dt):
        return st_.enter_context(nc.sbuf_tensor(pfx + name, shape, dt))

    if not io.get("cvals_ready"):
        cvals_s = sb(st, "cvals_s", [128, len(CONST_VALS)], F32)
        p.do("gpsimd", lambda g: g.dma_start(out=cvals_s[:], in_=w["cvals"]),
             reads=[w["cvals"]], writes=[cvals_s], dma=True, tag="cvals")
        for j, v in enumerate(CONST_VALS):
            nc.const_aps.register(F32, float(v), cvals_s[:, j:j + 1])
        p.fence()
    iden_f = sb(st, "idenf_s", [128, 128], F32)
    iden_b = sb(st, "idenb_s", [128, 128], BF16)
    diagm = sb(st, "diagm_s", [128, 896], BF16)
    p.do("gpsimd", lambda g: g.dma_start(out=iden_f[:], in_=w["iden_f"]), reads=[w["iden_f"]], writes=[iden_f], dma=True)
    p.do("gpsimd", lambda g: g.dma_start(out=iden_b[:], in_=w["iden_f"]), reads=[w["iden_f"]], writes=[iden_b], dma=True)
    p.do("gpsimd", lambda g: g.dma_start(out=diagm[:], in_=w["diagmask4"]), reads=[w["diagmask4"]], writes=[diagm], dma=True)

    psb = io.get("psb") or [st.enter_context(nc.psum_tensor(f"psb{i}", [128, 512], F32)) for i in range(7)]
    psO = io.get("psO") or st.enter_context(nc.psum_tensor("psO", [65, 512], F32))
    psb = psb[:7]

    wtmp = [sb(st, f"wtmp{i}", [128, D], BF16) for i in range(2)]

    def prep(W_ap, F, Din, name, st_):
        nchunk = -(-Din // 128)
        wT = [sb(st_, f"{name}_T{c}", [min(128, Din - 128 * c), F], BF16) for c in range(nchunk)]
        nrow = -(-F // 128)
        for r in range(nrow):
            buf = wtmp[r % 2]
            rows = min(128, F - 128 * r)
            p.do("gpsimd", lambda g, r=r, buf=buf, rows=rows, W_ap=W_ap, Din=Din: g.dma_start(
                out=buf[0:rows, 0:Din], in_=W_ap[r * 128:r * 128 + rows, :]),
                reads=[W_ap], writes=[buf], dma=True, tag=f"{name}ld")
            for c in range(nchunk):
                cols = min(128, Din - 128 * c)
                ps = psb[(r * nchunk + c) % 2]
                p.do("tensor", lambda t, buf=buf, c=c, ps=ps, rows=rows, cols=cols: t.transpose(
                    ps.bitcast(BF16)[0:cols, 0:rows], buf[0:rows, c * 128:c * 128 + cols],
                    iden_b[0:rows, 0:rows]),
                    reads=[buf, iden_b], writes=[ps], tag=f"{name}tr")
                p.do("vector", lambda v, ps=ps, c=c, r=r, rows=rows, cols=cols, wt=wT[c]: v.tensor_copy(
                    out=wt[0:cols, r * 128:r * 128 + rows], in_=ps.bitcast(BF16)[0:cols, 0:rows]),
                    reads=[ps], writes=[wT[c]], tag=f"{name}ev")
        return wT

    inT = prep(w["in_w"], 64, D, "inw", st)
    gaT = prep(w["gate_w"], 64, D, "gaw", st)
    ouT = prep(w["out_w"], D, 64, "ouw", st)
    dtT = prep(w["dt_w"], 64, 64, "dtw", st)
    bpT = prep(w["Bp_w"], 16, 64, "bpw", st)
    cpT = prep(w["Cp_w"], 16, 64, "cpw", st)

    n1c = sb(st, "n1c", [128, NC], F32)
    n2c = sb(st, "n2c", [128, NC], F32)
    dtb = sb(st, "dtb", [64, 1], F32)
    negA = sb(st, "negA", [128, 8], F32)
    for dst, src_ in ((n1c, "n1_cols"), (n2c, "n2_cols"), (dtb, "dtb_col"), (negA, "A_cols")):
        p.do("gpsimd", lambda g, dst=dst, src_=src_: g.dma_start(out=dst[:], in_=w[src_]),
             reads=[w[src_]], writes=[dst], dma=True)
    p.do("scalar", lambda s: s.activation(out=negA[:], in_=negA[:], func=AF.Exp),
         reads=[negA], writes=[negA])
    p.do("vector", lambda v: v.tensor_scalar(out=negA[:], in0=negA[:], scalar1=-1.0, scalar2=None,
                                             op0=ALU.mult), reads=[negA], writes=[negA])
    for c in range(NC):
        p.do("scalar", lambda s, c=c: s.activation(out=inT[c][:], in_=inT[c][:], func=AF.Identity,
                                                   scale=n2c[:, c:c + 1]),
             reads=[inT[c], n2c], writes=[inT[c]], tag="fold2")
        p.do("scalar", lambda s, c=c: s.activation(out=gaT[c][:], in_=gaT[c][:], func=AF.Identity,
                                                   scale=n2c[:, c:c + 1]),
             reads=[gaT[c], n2c], writes=[gaT[c]], tag="fold3")
    Rrep = sb(st, "Rrep_s", [64, 8 * 128], BF16)
    RBt = sb(st, "RBt", [16, 128], BF16)
    Grep = sb(st, "Grep_s", [128, 8 * 64], BF16)
    onecol_b = sb(st, "onecol_b", [128, 1], BF16)
    onerow_b = sb(st, "onerow_b", [1, 128], BF16)
    for dst, src_ in ((Rrep, "Rrep"), (RBt, "RB"), (Grep, "Grep"), (onecol_b, "ones_col"), (onerow_b, "ones_row")):
        p.do("gpsimd", lambda g, dst=dst, src_=src_: g.dma_start(out=dst[:], in_=w[src_]),
             reads=[w[src_]], writes=[dst], dma=True)

    u_fm = [sb(st, f"u_fm{c}", [128, T], F32) for c in range(NC)]
    h_fm = [sb(st, f"h_fm{c}", [128, T], BF16) for c in range(NC)]
    final_stores = []

    for b in range(NBATCH):
        xb = x_ap[b * T:(b + 1) * T, :]
        ob = out_ap[b * T:(b + 1) * T, :]
        bst = ExitStack()
        xt = [sb(bst, f"xt{b}_{i}", [128, D], F32) for i in range(2)]
        ot = [sb(bst, f"ot{b}_{i}", [128, D], F32) for i in range(2)]

        a1 = ExitStack()
        ut = [sb(a1, f"ut{b}_{i}", [128, D], F32) for i in range(2)]
        ht2 = [sb(a1, f"ht{b}_{i}", [128, D], BF16) for i in range(2)]
        sqs_a = [sb(a1, f"sqs{b}_{i}", [128, 12], F32) for i in range(2)]
        stat = [sb(a1, f"stat{b}_{i}", [128, 8], F32) for i in range(2)]

        def a1_head(tt):
            i = tt % 2
            X, U, ST, SQ = xt[i], ut[i], stat[i], sqs_a[i]
            p.do("sync", lambda s, tt=tt, X=X, xb=xb: s.dma_start(
                out=X[:], in_=xb[tt * 128:(tt + 1) * 128, :]), reads=[xb], writes=[X], dma=True, tag="ldx")
            emit_ln_tile(p, X, U, ST, SQ)
            p.do("vector", lambda v, U=U, SQ=SQ: v.bn_stats(
                out=SQ[:, 0:6], in_=U[:, 0:512]), reads=[U], writes=[SQ], tag="bns2")
            p.do("vector", lambda v, U=U, SQ=SQ: v.bn_stats(
                out=SQ[:, 6:12], in_=U[:, 512:1024]), reads=[U], writes=[SQ], tag="bns2")
            p.do("vector", lambda v, ST=ST, SQ=SQ: v.bn_aggr(
                out=ST[:, 5:7], in_=SQ[:, 0:12].rearrange("p (a b) -> p a b", b=6)),
                reads=[SQ], writes=[ST], tag="bna2")
            p.do("vector", lambda v, ST=ST: v.tensor_tensor(
                out=ST[:, 7:8], in0=ST[:, 5:6], in1=ST[:, 5:6], op=ALU.mult),
                reads=[ST], writes=[ST], tag="musq")
            p.do("vector", lambda v, ST=ST: v.tensor_tensor(
                out=ST[:, 7:8], in0=ST[:, 7:8], in1=ST[:, 6:7], op=ALU.add),
                reads=[ST], writes=[ST])
            p.do("scalar", lambda s, ST=ST: s.activation(
                out=ST[:, 6:7], in_=ST[:, 7:8], func=AF.Sqrt, bias=EPS_RMS),
                reads=[ST], writes=[ST])
            p.do("vector", lambda v, ST=ST: v.reciprocal(
                out=ST[:, 5:6], in_=ST[:, 6:7]), reads=[ST], writes=[ST])
            p.do("scalar", lambda s, U=U, ST=ST, tt=tt: s.activation(
                out=ht2[tt % 2][:], in_=U[:], func=AF.Identity, scale=ST[:, 5:6]),
                reads=[U, ST], writes=[ht2[tt % 2]], tag="ht")

        a1_head(0)
        for tt in range(NT):
            i = tt % 2
            U, ht = ut[i], ht2[i]
            if tt + 1 < NT:
                a1_head(tt + 1)
            for c in range(NC):
                pu, ph = psb[c % 2], psb[2 + c % 2]
                p.do("tensor", lambda t, U=U, c=c, pu=pu: t.transpose(
                    pu[:, 0:128], U[:, c * 128:(c + 1) * 128], iden_f[:]),
                    reads=[U, iden_f], writes=[pu], tag="trU")
                p.do("vector", lambda v, tt=tt, c=c, pu=pu: v.tensor_copy(
                    out=u_fm[c][:, tt * 128:(tt + 1) * 128], in_=pu[:, 0:128]),
                    reads=[pu], writes=[u_fm[c]], tag="evU")
                p.do("tensor", lambda t, c=c, ph=ph, ht=ht: t.transpose(
                    ph.bitcast(BF16)[:, 0:128], ht[:, c * 128:(c + 1) * 128], iden_b[:]),
                    reads=[ht, iden_b], writes=[ph], tag="trH")
                p.do("scalar", lambda s, tt=tt, c=c, ph=ph: s.activation(
                    out=h_fm[c][:, tt * 128:(tt + 1) * 128], in_=ph.bitcast(BF16)[:, 0:128],
                    func=AF.Identity, scale=n1c[:, c:c + 1]),
                    reads=[ph, n1c], writes=[h_fm[c]], tag="evH")
        if io.get("dbg") is not None and b == 0:
            dbg = io["dbg"]
            for tt_ in (2,):
                pass
            p.do("sync", lambda s: s.dma_start(out=dbg[0:128, 0:8], in_=stat[0][:]),
                 reads=[stat[0]], writes=[dbg], dma=True, tag="dbgstat")
            p.do("sync", lambda s: s.dma_start(out=dbg[0:128, 8:8 + T], in_=u_fm[0][:]),
                 reads=[u_fm[0]], writes=[dbg], dma=True, tag="dbgu")
            p.do("sync", lambda s: s.dma_start(out=dbg[0:128, 8 + T:8 + T + D], in_=ut[0][:]),
                 reads=[ut[0]], writes=[dbg], dma=True, tag="dbgut")
        p.fence()
        a1.close()

        a2 = ExitStack()
        DO2 = 2 in PHASES
        attn_fm = [sb(a2, f"at{b}_fm{c}", [128, T], BF16) for c in range(NC)]
        Qh = [[sb(a2, f"Qh{b}_{pp}_{h}", [128, T], BF16) for h in range(2)] for pp in range(2)]
        Kh = [[sb(a2, f"Kh{b}_{pp}_{h}", [128, T], BF16) for h in range(2)] for pp in range(2)]
        Vh = [[sb(a2, f"Vh{b}_0_{h}", [128, 16 * 65], BF16) for h in range(2)]] * 2
        pairT = [[sb(a2, f"pT{b}_0_{c}", [128, 384], BF16) for c in range(NC)]] * 2
        qa = sb(a2, f"qa{b}", [64, 512], BF16)
        kmb = sb(a2, f"kmb{b}", [64, 1], F32)
        kmbb = sb(a2, f"kmbb{b}", [64, 1], BF16)
        p_sb = [sb(a2, f"p_sb{b}_{i}", [128, 512], BF16) for i in range(4)]
        o_sb = sb(a2, f"o_sb{b}", [64, 512], BF16)
        rdenf = psb[6][0:1, :]
        rden = sb(a2, f"rden{b}", [1, 512], BF16)
        for hp in (range(8) if DO2 else []):
            pp = hp % 2
            QhP, KhP, VhP, pT = Qh[pp], Kh[pp], Vh[pp], pairT[pp]
            for j, rowbase in enumerate((hp * 128, 1024 + hp * 128, 2048 + hp * 128)):
                buf = wtmp[j % 2]
                p.do("gpsimd", lambda g, buf=buf, rowbase=rowbase: g.dma_start(
                    out=buf[:], in_=w["qkv_w"][rowbase:rowbase + 128, :]),
                    reads=[w["qkv_w"]], writes=[buf], dma=True, tag="qkvld")
                for c in range(NC):
                    ps = psb[2 + (j * NC + c) % 2]
                    p.do("tensor", lambda t, buf=buf, c=c, ps=ps: t.transpose(
                        ps.bitcast(BF16)[:, 0:128], buf[:, c * 128:(c + 1) * 128], iden_b[:]),
                        reads=[buf, iden_b], writes=[ps], tag="qkvtr")
                    p.do("vector", lambda v, c=c, j=j, ps=ps, pT=pT: v.tensor_copy(
                        out=pT[c][:, j * 128:(j + 1) * 128], in_=ps.bitcast(BF16)[:, 0:128]),
                        reads=[ps], writes=[pT[c]], tag="qkvev")
            for which, cb in ((("q", 0), ("k", 128)) if KK_A2L() >= 0.5 else ()):
                dst0, dst1 = (QhP[0], QhP[1]) if which == "q" else (KhP[0], KhP[1])
                for qc in range(NQC):
                    ps = psb[2 + qc % 2]
                    for c in range(NC):
                        p.do("tensor", lambda t, c=c, qc=qc, cb=cb, ps=ps, pT=pT: t.matmul(
                            ps[:], pT[c][:, cb:cb + 128], h_fm[c][:, qc * 512:(qc + 1) * 512],
                            start=(c == 0), stop=(c == NC - 1)),
                            reads=[pT[c], h_fm[c]], writes=[ps], tag="proj")
                    p.do("vector", lambda v, qc=qc, ps=ps, dst0=dst0: v.tensor_copy(
                        out=dst0[0:64, qc * 512:(qc + 1) * 512], in_=ps[0:64, :]),
                        reads=[ps], writes=[dst0], tag="pe0")
                    p.do("scalar", lambda s, qc=qc, ps=ps, dst1=dst1: s.copy(
                        out=dst1[0:64, qc * 512:(qc + 1) * 512], in_=ps[64:128, :]),
                        reads=[ps], writes=[dst1], tag="pe1")
            for h in (range(2) if KK_A2L() >= 2 else []):
                Q, K = QhP[h], KhP[h]
                p.do("vector", lambda v, K=K: v.tensor_reduce(
                    out=kmb[:, 0:1], in_=K[0:64, :], axis=AX.X, op=ALU.max,
                    apply_absolute_value=True), reads=[K], writes=[kmb], tag="km")
                p.do("vector", lambda v: v.tensor_copy(out=kmbb[:], in_=kmb[:, 0:1]),
                     reads=[kmb], writes=[kmbb])
                for qc in range(NQC):
                    p.do("scalar", lambda s, Q=Q, qc=qc: s.activation(
                        out=qa[:], in_=Q[0:64, qc * 512:(qc + 1) * 512], func=AF.Abs),
                        reads=[Q], writes=[qa], tag="qa")
                    p.do("tensor", lambda t, qc=qc: t.matmul(
                        psb[6][0:1, :], kmbb[:, 0:1], qa[:],
                        start=True, stop=True), reads=[kmbb, qa], writes=[psb[6]], tag="mhat")
                    p.do("scalar", lambda s, qc=qc, Q=Q: s.activation(
                        out=Q[64:65, qc * 512:(qc + 1) * 512], in_=psb[6][0:1, :],
                        func=AF.Identity, scale=-1.0, bias=240.0),
                        reads=[psb[6]], writes=[Q], tag="mrow")
            for tt in (range(NT) if KK_A2L() >= 0.8 else []):
                ps = psb[2 + tt % 2]
                for c in range(NC):
                    p.do("tensor", lambda t, c=c, tt=tt, ps=ps, pT=pT: t.matmul(
                        ps[:, 0:128], h_fm[c][:, tt * 128:(tt + 1) * 128],
                        pT[c][:, 256:384],
                        start=(c == 0), stop=(c == NC - 1)),
                        reads=[h_fm[c], pT[c]], writes=[ps], tag="projv")
                p.do("vector", lambda v, tt=tt, ps=ps, VhP=VhP: v.tensor_copy(
                    out=VhP[0][:, tt * 65:tt * 65 + 64], in_=ps[:, 0:64]),
                    reads=[ps], writes=[VhP[0]], tag="ve0")
                p.do("scalar", lambda s, tt=tt, ps=ps, VhP=VhP: s.copy(
                    out=VhP[1][:, tt * 65:tt * 65 + 64], in_=ps[:, 64:128]),
                    reads=[ps], writes=[VhP[1]], tag="ve1")
            for h in (range(2) if KK_A2L() >= 1.5 else []):
                p.do("vector", lambda v, h=h, VhP=VhP: v.memset(
                    VhP[h][:].rearrange("p (t j) -> p t j", j=65)[:, :, 64:65], 1.0),
                    reads=[], writes=[VhP[h]], unsafe=True, tag="vone")
                p.do("vector", lambda v, h=h, KhP=KhP: v.memset(KhP[h][64:65, :], 1.0),
                     reads=[], writes=[KhP[h]], unsafe=True, tag="kone")
            psAcc = [psO, psb[4][0:65, :]]
            for qc in (range(NQC) if KK_A2L() >= 3 else []):
                nkt = 4 * qc + 4

                def emit_S2(kt, h, qc=qc, QhP=QhP, KhP=KhP):
                    psS = psb[h]
                    P = p_sb[(2 * kt + h) % 4]
                    Q, K = QhP[h], KhP[h]
                    diag = kt >= 4 * qc
                    if diag:
                        v_i = kt - 4 * qc
                        p.do("tensor", lambda t, v_i=v_i, psS=psS: t.matmul(
                            psS[:], iden_b[:],
                            diagm[:, (3 - v_i) * 128:(3 - v_i) * 128 + 512],
                            start=True, stop=False), reads=[iden_b, diagm], writes=[psS], tag="Smask")
                    p.do("tensor", lambda t, kt=kt, qc=qc, Q=Q, K=K, psS=psS, diag=diag: t.matmul(
                        psS[:], K[0:65, kt * 128:(kt + 1) * 128], Q[0:65, qc * 512:(qc + 1) * 512],
                        start=not diag, stop=True), reads=[K, Q], writes=[psS], tag="S")
                    p.do("scalar", lambda s, psS=psS, P=P: s.activation(
                        out=P[:], in_=psS[:], func=AF.Exp, scale=0.125),
                        reads=[psS], writes=[P], tag="exp")

                def emit_PV2(kt, h, nkt=nkt, VhP=VhP):
                    P = p_sb[(2 * kt + h) % 4]
                    acc = psAcc[h]
                    p.do("tensor", lambda t, kt=kt, h=h, P=P, nkt=nkt, acc=acc, VhP=VhP: t.matmul(
                        acc[:], VhP[h][:, kt * 65:(kt + 1) * 65], P[:],
                        start=(kt == 0), stop=(kt == nkt - 1)),
                        reads=[VhP[h], P], writes=[acc], tag="PV")

                for kt in range(nkt):
                    emit_S2(kt, 0)
                    emit_S2(kt, 1)
                    if KK_A2L() >= 4 and kt >= 1:
                        emit_PV2(kt - 1, 0)
                        emit_PV2(kt - 1, 1)
                if KK_A2L() >= 4:
                    emit_PV2(nkt - 1, 0)
                    emit_PV2(nkt - 1, 1)
                if KK_A2L() < 5:
                    continue
                for h in range(2):
                    acc = psAcc[h]
                    p.do("scalar", lambda s, acc=acc: s.activation(out=rdenf[:], in_=acc[64:65, :],
                                                                  func=AF.Ln),
                         reads=[acc], writes=[rdenf], tag="rdl")
                    p.do("scalar", lambda s: s.activation(out=rden[:], in_=rdenf[:],
                                                          func=AF.Exp, scale=-1.0),
                         reads=[rdenf], writes=[rden], tag="rde")
                    p.do("tensor", lambda t: t.matmul(
                        psb[5][0:64, :], onerow_b[0:1, 0:64], rden[:], start=True, stop=True),
                        reads=[onerow_b, rden], writes=[psb[5]], tag="bcast")
                    p.do("vector", lambda v, acc=acc: v.tensor_copy(out=o_sb[:], in_=acc[0:64, :]),
                         reads=[acc], writes=[o_sb], tag="oev")
                    hrow = h * 64
                    p.do("vector", lambda v, hp=hp, hrow=hrow, qc=qc: v.tensor_tensor(
                        out=attn_fm[hp][hrow:hrow + 64, qc * 512:(qc + 1) * 512],
                        in0=o_sb[:], in1=psb[5][0:64, :], op=ALU.mult),
                        reads=[o_sb, psb[5]], writes=[attn_fm[hp]], tag="an")
        oslice = [pairT[1][c] for c in range(NC)]
        for dc in (range(NC) if (DO2 and KK_A2L() >= 6) else []):
            buf = wtmp[dc % 2]
            p.do("gpsimd", lambda g, buf=buf, dc=dc: g.dma_start(
                out=buf[:], in_=w["o_w"][dc * 128:(dc + 1) * 128, :]),
                reads=[w["o_w"]], writes=[buf], dma=True, tag="owld")
            for c in range(NC):
                ps = psb[c % 2]
                p.do("tensor", lambda t, buf=buf, c=c, ps=ps: t.transpose(
                    ps.bitcast(BF16)[:, 0:128], buf[:, c * 128:(c + 1) * 128], iden_b[:]),
                    reads=[buf, iden_b], writes=[ps], tag="owtr")
                p.do("vector", lambda v, c=c, ps=ps: v.tensor_copy(
                    out=oslice[c][:, 0:128], in_=ps.bitcast(BF16)[:, 0:128]),
                    reads=[ps], writes=[oslice[c]], tag="owev")
            for qc in range(NQC):
                ps = psb[4 + qc % 2]
                for c in range(NC):
                    p.do("tensor", lambda t, qc=qc, c=c, ps=ps: t.matmul(
                        ps[:], oslice[c][:, 0:128], attn_fm[c][:, qc * 512:(qc + 1) * 512],
                        start=(c == 0), stop=(c == NC - 1)),
                        reads=[oslice[c], attn_fm[c]], writes=[ps], tag="oproj")
                p.do("vector", lambda v, dc=dc, qc=qc, ps=ps: v.tensor_tensor(
                    out=u_fm[dc][:, qc * 512:(qc + 1) * 512],
                    in0=u_fm[dc][:, qc * 512:(qc + 1) * 512], in1=ps[:], op=ALU.add),
                    reads=[u_fm[dc], ps], writes=[u_fm[dc]], tag="oev2")
        p.fence()
        a2.close()

        a3 = ExitStack()
        DO3 = 3 in PHASES
        z_sb = sb(a3, f"z{b}", [64, T], BF16)
        gate_sb = sb(a3, f"ga{b}", [64, T], BF16)
        dtbf = sb(a3, f"dtbf{b}", [64, T], BF16)
        dtzb = sb(a3, f"dtzb{b}", [64, T], BF16)
        Bi_sb = sb(a3, f"Bi{b}", [16, T], BF16)
        Ci_sb = sb(a3, f"Ci{b}", [16, T], BF16)
        sp1 = sb(a3, f"sp1{b}", [64, 512], F32)
        sp2 = sb(a3, f"sp2{b}", [64, 512], F32)
        a_sc_ = [sb(a3, f"asc{b}_0", [128, T], F32)] * 2
        u_sc_ = [sb(a3, f"usc{b}_0", [128, T], F32)] * 2
        h_sc_ = [sb(a3, f"hsc{b}_{i}", [128, T], F32) for i in range(2)]
        w_sc_ = [sb(a3, f"wsc{b}_0", [128, T], BF16)] * 2
        yg_sb = sb(a3, f"yg{b}", [64, T], BF16)
        usq = sb(a3, f"usq{b}", [128, 512], BF16)
        r2row = sb(a3, f"r2r{b}", [1, T], F32)
        r2rowb = sb(a3, f"r2rb{b}", [1, T], BF16)
        for qc in (range(NQC) if DO3 else []):
            for c in range(NC):
                p.do("scalar", lambda s, c=c, qc=qc: s.activation(
                    out=usq[:], in_=u_fm[c][:, qc * 512:(qc + 1) * 512], func=AF.Square),
                    reads=[u_fm[c]], writes=[usq], tag="usq")
                p.do("tensor", lambda t, c=c: t.matmul(
                    psb[6][0:1, :], onecol_b[:, 0:1], usq[:],
                    start=(c == 0), stop=(c == NC - 1)), reads=[onecol_b, usq], writes=[psb[6]], tag="s2")
            p.do("scalar", lambda s, qc=qc: s.activation(
                out=r2row[:, qc * 512:(qc + 1) * 512], in_=psb[6][0:1, :], func=AF.Sqrt,
                scale=1.0 / D, bias=EPS_RMS), reads=[psb[6]], writes=[r2row], tag="s2ev")
        if not DO3:
            p.do("vector", lambda v: v.memset(r2row[:], 1.0), reads=[], writes=[r2row], unsafe=True)
        p.do("scalar", lambda s: s.activation(out=r2row[:], in_=r2row[:], func=AF.Ln),
             reads=[r2row], writes=[r2row], tag="r2l")
        p.do("scalar", lambda s: s.activation(out=r2rowb[:], in_=r2row[:], func=AF.Exp, scale=-1.0),
             reads=[r2row], writes=[r2rowb], tag="r2e")
        for qc in (range(NQC) if DO3 else []):
            p.do("tensor", lambda t, qc=qc: t.matmul(
                psb[6][:], onerow_b[0:1, :], r2rowb[:, qc * 512:(qc + 1) * 512],
                start=True, stop=True), reads=[onerow_b, r2rowb], writes=[psb[6]], tag="r2bc")
            for c in range(NC):
                p.do("vector", lambda v, c=c, qc=qc: v.tensor_tensor(
                    out=h_fm[c][:, qc * 512:(qc + 1) * 512],
                    in0=u_fm[c][:, qc * 512:(qc + 1) * 512], in1=psb[6][:], op=ALU.mult),
                    reads=[u_fm[c], psb[6]], writes=[h_fm[c]], tag="h2")
        for qc in (range(NQC) if DO3 else []):
            for c in range(NC):
                p.do("tensor", lambda t, c=c, qc=qc: t.matmul(
                    psb[0][0:64, :], inT[c][:, 0:64], h_fm[c][:, qc * 512:(qc + 1) * 512],
                    start=(c == 0), stop=(c == NC - 1)), reads=[inT[c], h_fm[c]], writes=[psb[0]], tag="zmm")
            p.do("vector", lambda v, qc=qc: v.tensor_copy(
                out=z_sb[:, qc * 512:(qc + 1) * 512], in_=psb[0][0:64, :]),
                reads=[psb[0]], writes=[z_sb], tag="zev")
            for c in range(NC):
                p.do("tensor", lambda t, c=c, qc=qc: t.matmul(
                    psb[1][0:64, :], gaT[c][:, 0:64], h_fm[c][:, qc * 512:(qc + 1) * 512],
                    start=(c == 0), stop=(c == NC - 1)), reads=[gaT[c], h_fm[c]], writes=[psb[1]], tag="gmm")
            p.do("scalar", lambda s, qc=qc: s.activation(
                out=gate_sb[:, qc * 512:(qc + 1) * 512], in_=psb[1][0:64, :], func=AF.Silu),
                reads=[psb[1]], writes=[gate_sb], tag="gev")
            p.do("tensor", lambda t, qc=qc: t.matmul(
                psb[2][0:64, :], dtT[0][0:64, 0:64], z_sb[:, qc * 512:(qc + 1) * 512],
                start=True, stop=True), reads=[dtT[0], z_sb], writes=[psb[2]], tag="dtmm")
            p.do("scalar", lambda s: s.activation(
                out=sp1[:], in_=psb[2][0:64, :], func=AF.Abs, bias=dtb[0:64, :]),
                reads=[psb[2], dtb], writes=[sp1])
            p.do("scalar", lambda s: s.activation(
                out=sp1[:], in_=sp1[:], func=AF.Exp, scale=-1.0), reads=[sp1], writes=[sp1])
            p.do("scalar", lambda s: s.activation(
                out=sp1[:], in_=sp1[:], func=AF.Ln, bias=1.0), reads=[sp1], writes=[sp1])
            p.do("scalar", lambda s: s.activation(
                out=sp2[:], in_=psb[2][0:64, :], func=AF.Relu, bias=dtb[0:64, :]),
                reads=[psb[2], dtb], writes=[sp2])
            p.do("vector", lambda v, qc=qc: v.tensor_tensor(
                out=dtbf[:, qc * 512:(qc + 1) * 512], in0=sp1[:], in1=sp2[:], op=ALU.add),
                reads=[sp1, sp2], writes=[dtbf], tag="dt")
            p.do("vector", lambda v, qc=qc: v.tensor_tensor(
                out=dtzb[:, qc * 512:(qc + 1) * 512], in0=dtbf[:, qc * 512:(qc + 1) * 512],
                in1=z_sb[:, qc * 512:(qc + 1) * 512], op=ALU.mult),
                reads=[dtbf, z_sb], writes=[dtzb], tag="dtz")
            p.do("tensor", lambda t, qc=qc: t.matmul(
                psb[3][0:16, :], bpT[0][0:64, 0:16], z_sb[:, qc * 512:(qc + 1) * 512],
                start=True, stop=True), reads=[bpT[0], z_sb], writes=[psb[3]], tag="bimm")
            p.do("vector", lambda v, qc=qc: v.tensor_copy(
                out=Bi_sb[:, qc * 512:(qc + 1) * 512], in_=psb[3][0:16, :]),
                reads=[psb[3]], writes=[Bi_sb], tag="biev")
            p.do("tensor", lambda t, qc=qc: t.matmul(
                psb[4][0:16, :], cpT[0][0:64, 0:16], z_sb[:, qc * 512:(qc + 1) * 512],
                start=True, stop=True), reads=[cpT[0], z_sb], writes=[psb[4]], tag="cimm")
            p.do("scalar", lambda s, qc=qc: s.copy(
                out=Ci_sb[:, qc * 512:(qc + 1) * 512], in_=psb[4][0:16, :]),
                reads=[psb[4]], writes=[Ci_sb], tag="ciev")
        for i in (range(8) if DO3 else []):
            a_sc, u_sc, h_sc, w_sc = a_sc_[i % 2], u_sc_[i % 2], h_sc_[i % 2], w_sc_[i % 2]
            for qc in range(NQC):
                p.do("tensor", lambda t, i=i, qc=qc: t.matmul(
                    psb[0][:], Rrep[:, i * 128:(i + 1) * 128], dtbf[:, qc * 512:(qc + 1) * 512],
                    start=True, stop=True), reads=[Rrep, dtbf], writes=[psb[0]], tag="dtrep")
                p.do("scalar", lambda s, i=i, qc=qc: s.activation(
                    out=a_sc[:, qc * 512:(qc + 1) * 512], in_=psb[0][:], func=AF.Identity,
                    scale=negA[:, i:i + 1], bias=1.0), reads=[psb[0], negA], writes=[a_sc], tag="a")
                p.do("tensor", lambda t, i=i, qc=qc: t.matmul(
                    psb[1][:], Rrep[:, i * 128:(i + 1) * 128], dtzb[:, qc * 512:(qc + 1) * 512],
                    start=True, stop=True), reads=[Rrep, dtzb], writes=[psb[1]], tag="dtzrep")
                p.do("scalar", lambda s, qc=qc: s.copy(
                    out=u_sc[:, qc * 512:(qc + 1) * 512], in_=psb[1][:]),
                    reads=[psb[1]], writes=[u_sc], tag="dtrev")
                p.do("tensor", lambda t, qc=qc: t.matmul(
                    psb[2][:], RBt[:, :], Bi_sb[:, qc * 512:(qc + 1) * 512],
                    start=True, stop=True), reads=[RBt, Bi_sb], writes=[psb[2]], tag="birep")
                p.do("vector", lambda v, qc=qc: v.tensor_tensor(
                    out=u_sc[:, qc * 512:(qc + 1) * 512], in0=u_sc[:, qc * 512:(qc + 1) * 512],
                    in1=psb[2][:], op=ALU.mult), reads=[u_sc, psb[2]], writes=[u_sc], tag="usc")
            p.do("vector", lambda v: v.tensor_tensor_scan(
                out=h_sc[:], data0=a_sc[:], data1=u_sc[:], initial=0.0,
                op0=ALU.mult, op1=ALU.add), reads=[a_sc, u_sc], writes=[h_sc], tag="scan")
            for qc in range(NQC):
                p.do("tensor", lambda t, qc=qc: t.matmul(
                    psb[3][:], RBt[:, :], Ci_sb[:, qc * 512:(qc + 1) * 512],
                    start=True, stop=True), reads=[RBt, Ci_sb], writes=[psb[3]], tag="cirep")
                p.do("vector", lambda v, qc=qc: v.tensor_tensor(
                    out=w_sc[:, qc * 512:(qc + 1) * 512], in0=h_sc[:, qc * 512:(qc + 1) * 512],
                    in1=psb[3][:], op=ALU.mult), reads=[h_sc, psb[3]], writes=[w_sc], tag="wsc")
                yps = [psb[4], psb[5], psb[6], psO][qc]
                p.do("tensor", lambda t, i=i, qc=qc, yps=yps: t.matmul(
                    yps[0:64, 0:512], Grep[:, i * 64:(i + 1) * 64],
                    w_sc[:, qc * 512:(qc + 1) * 512],
                    start=(i == 0), stop=(i == 7)), reads=[Grep, w_sc],
                    writes=[yps], tag="yacc")
        for qc in (range(NQC) if DO3 else []):
            yps = [psb[4], psb[5], psb[6], psO][qc]
            p.do("vector", lambda v, qc=qc, yps=yps: v.tensor_tensor(
                out=yg_sb[:, qc * 512:(qc + 1) * 512], in0=yps[0:64, 0:512],
                in1=gate_sb[:, qc * 512:(qc + 1) * 512], op=ALU.mult),
                reads=[yps, gate_sb], writes=[yg_sb], tag="yg")
        for dc in (range(NC) if DO3 else []):
            for qc in range(NQC):
                ps = psb[(dc * NQC + qc) % 4]
                p.do("tensor", lambda t, dc=dc, qc=qc, ps=ps: t.matmul(
                    ps[:], ouT[0][0:64, dc * 128:(dc + 1) * 128], yg_sb[:, qc * 512:(qc + 1) * 512],
                    start=True, stop=True), reads=[ouT[0], yg_sb], writes=[ps], tag="sproj")
                p.do("vector", lambda v, dc=dc, qc=qc, ps=ps: v.tensor_tensor(
                    out=u_fm[dc][:, qc * 512:(qc + 1) * 512],
                    in0=u_fm[dc][:, qc * 512:(qc + 1) * 512], in1=ps[:], op=ALU.add),
                    reads=[u_fm[dc], ps], writes=[u_fm[dc]], tag="sev")
        p.fence()
        a3.close()

        for tt in range(NT):
            i = tt % 2
            X, O = xt[i], ot[i]
            p.do("sync", lambda s, tt=tt, X=X, xb=xb: s.dma_start(
                out=X[:], in_=xb[tt * 128:(tt + 1) * 128, :]), reads=[xb], writes=[X], dma=True, tag="ldx2")
            for c in range(NC):
                pu = psb[4 + c % 2]
                p.do("tensor", lambda t, tt=tt, c=c, pu=pu: t.transpose(
                    pu[:, 0:128], u_fm[c][:, tt * 128:(tt + 1) * 128], iden_f[:]),
                    reads=[u_fm[c], iden_f], writes=[pu], tag="trO")
                p.do("vector", lambda v, c=c, X=X, O=O, pu=pu: v.tensor_tensor(
                    out=O[:, c * 128:(c + 1) * 128], in0=X[:, c * 128:(c + 1) * 128],
                    in1=pu[:, 0:128], op=ALU.add), reads=[X, pu], writes=[O], tag="oadd")
            stx = p.do("sync", lambda s, tt=tt, O=O, ob=ob: s.dma_start(
                out=ob[tt * 128:(tt + 1) * 128, :], in_=O[:]), reads=[O], writes=[ob], dma=True, tag="stx")
            final_stores.append(stx)
        p.fence()
        bst.close()
    return final_stores



def make_launch_a():
    nc = bass.Bass("TRN2", target_bir_lowering=False)
    st = ExitStack()
    io = {}
    io["x"] = nc.dram_tensor("x", [B_PER_CORE * T, D], F32, kind="ExternalInput").ap()
    io["out"] = nc.dram_tensor("out", [B_PER_CORE * T, D], F32, kind="ExternalOutput").ap()
    io["dbg"] = nc.dram_tensor("dbg", [128, 8 + T + D], F32, kind="ExternalOutput").ap()
    for name, shape in [("qkv_w", [3072, D]), ("o_w", [D, D]),
                        ("in_w", [64, D]), ("out_w", [D, 64]),
                        ("Bp_w", [16, 64]), ("Cp_w", [16, 64]),
                        ("dt_w", [64, 64]), ("gate_w", [64, D]),
                        ("n1_cols", [128, 8]), ("n2_cols", [128, 8]),
                        ("A_cols", [128, 8]), ("dtb_col", [64, 1])]:
        io[name] = nc.dram_tensor(name, shape, F32, kind="ExternalInput").ap()
    for name, shape in HOST_CONST_SHAPES.items():
        io[name] = nc.dram_tensor(name, list(shape), F32, kind="ExternalInput").ap()
    p = Prog(nc)
    with st:
        finals = build_l0_attn(p, st, io)
        p.do("gpsimd", lambda g: g.engine_nop(), deps=finals, tag="fence")
        with nc.Block() as block:
            p.emit(block, st)
    return nc


def l0_attn_maps(inputs):
    consts = host_constants()
    xs = inputs["L0_attn_x"].reshape(16, T, D)
    in_maps = []
    for c in range(NCORE):
        m = {"x": np.ascontiguousarray(xs[2 * c:2 * c + 2].reshape(B_PER_CORE * T, D))}
        for k in ["qkv_w", "o_w", "in_w", "out_w", "Bp_w", "Cp_w", "dt_w", "gate_w"]:
            m[k] = inputs["L0_attn_" + k]
        m["n1_cols"] = np.ascontiguousarray(inputs["L0_attn_norm1_w"].reshape(8, 128).T)
        m["n2_cols"] = np.ascontiguousarray(inputs["L0_attn_norm2_w"].reshape(8, 128).T)
        m["A_cols"] = np.ascontiguousarray(inputs["L0_attn_A_log"].reshape(8, 128).T)
        m["dtb_col"] = np.ascontiguousarray(inputs["L0_attn_dt_b"].reshape(64, 1))
        m.update(consts)
        in_maps.append(m)
    return in_maps


def _gather_out(res):
    return np.concatenate([np.asarray(r["out"]).reshape(B_PER_CORE, T, D)
                           for r in res.results], 0)


def kernel_unfused(**inputs):
    cores = list(range(NCORE))
    x1 = _gather_out(run_bass_kernel_spmd(make_launch_a(), l0_attn_maps(inputs), core_ids=cores))
    x2 = _gather_out(run_bass_kernel_spmd(make_launch_b(), l0_ffn_maps(inputs, x1), core_ids=cores))
    x3 = _gather_out(run_bass_kernel_spmd(make_launch_c(), l1_attn_maps(inputs, x2), core_ids=cores))
    x4 = _gather_out(run_bass_kernel_spmd(make_launch_d(), l1_ffn_maps(inputs, x3), core_ids=cores))
    return x4


def kernel(**inputs):
    nc = make_fused()
    res = run_bass_kernel_spmd(nc, fused_maps(inputs), core_ids=list(range(NCORE)))
    return _gather_out(res)



def emit_ln_tile(p, X, U, ST, sqs):
    p.do("vector", lambda v: v.bn_stats(out=sqs[:, 0:6], in_=X[:, 0:512]),
         reads=[X], writes=[sqs], tag="bns")
    p.do("vector", lambda v: v.bn_stats(out=sqs[:, 6:12], in_=X[:, 512:1024]),
         reads=[X], writes=[sqs], tag="bns")
    p.do("vector", lambda v: v.bn_aggr(out=ST[:, 0:2],
                                       in_=sqs[:, 0:12].rearrange("p (a b) -> p a b", b=6)),
         reads=[sqs], writes=[ST], tag="bna")
    p.do("scalar", lambda s: s.activation(out=ST[:, 2:3], in_=ST[:, 1:2], func=AF.Sqrt, bias=1e-5),
         reads=[ST], writes=[ST])
    p.do("vector", lambda v: v.reciprocal(out=ST[:, 3:4], in_=ST[:, 2:3]), reads=[ST], writes=[ST])
    p.do("vector", lambda v: v.scalar_tensor_tensor(out=ST[:, 4:5], in0=ST[:, 0:1], scalar=-1.0,
                                                    in1=ST[:, 3:4], op0=ALU.mult, op1=ALU.mult),
         reads=[ST], writes=[ST])
    p.do("scalar", lambda s: s.activation(out=U[:], in_=X[:], func=AF.Identity, scale=ST[:, 3:4],
                                          bias=ST[:, 4:5]), reads=[X, ST], writes=[U], tag="u")


def emit_fp4_quant(p, U, DEQ, scr, cols, pool_adds=False, defer_deq=False):
    nb = cols // 16
    bm, xn, m = scr["bm"], scr["xn"], scr["m"]
    U3 = U[:].rearrange("p (b i) -> p b i", i=16)
    p.do("vector", lambda v: v.tensor_reduce(out=bm[:, 0:nb], in_=U3, axis=AX.X, op=ALU.max,
                                             apply_absolute_value=True),
         reads=[U], writes=[bm], tag="bmax")
    p.do("vector", lambda v: v.tensor_scalar(out=bm[:, 0:nb], in0=bm[:, 0:nb], scalar1=6e-12,
                                             scalar2=1.0 / 6.0, op0=ALU.max, op1=ALU.mult),
         reads=[bm], writes=[bm], tag="bscale")
    p.do("vector", lambda v: v.reciprocal(out=bm[:, nb:2 * nb], in_=bm[:, 0:nb]),
         reads=[bm], writes=[bm], tag="binv")
    inv3 = bm[:, nb:2 * nb].unsqueeze(2).broadcast_to([128, nb, 16])
    scl3 = bm[:, 0:nb].unsqueeze(2).broadcast_to([128, nb, 16])
    p.do("vector", lambda v: v.tensor_tensor(out=xn[:].rearrange("p (b i) -> p b i", i=16),
                                             in0=U3, in1=inv3, op=ALU.mult),
         reads=[U, bm], writes=[xn], tag="xn")
    p.do("vector", lambda v: v.tensor_scalar(out=m[:].bitcast(I32), in0=xn[:].bitcast(I32),
                                             scalar1=0x7F800000, scalar2=None, op0=ALU.bitwise_and),
         reads=[xn], writes=[m], tag="emask")
    p.do("vector", lambda v: v.tensor_scalar(out=m[:], in0=m[:], scalar1=6291456.0,
                                             scalar2=6291456.0, op0=ALU.mult, op1=ALU.max),
         reads=[m], writes=[m], tag="magic")
    eng_add = "gpsimd" if pool_adds else "vector"
    p.do(eng_add, lambda v: v.tensor_tensor(out=xn[:], in0=xn[:], in1=m[:], op=ALU.add),
         reads=[xn, m], writes=[xn], tag="madd")
    p.do(eng_add, lambda v: v.tensor_tensor(out=xn[:], in0=xn[:], in1=m[:], op=ALU.subtract),
         reads=[xn, m], writes=[xn], tag="msub")
    def deq():
        p.do("vector", lambda v: v.tensor_tensor(out=DEQ[:].rearrange("p (b i) -> p b i", i=16),
                                                 in0=xn[:].rearrange("p (b i) -> p b i", i=16),
                                                 in1=scl3, op=ALU.mult),
             reads=[xn, bm], writes=[DEQ], tag="deq")
    if defer_deq:
        return deq
    deq()



I_FF = 2816
NIT = 22
WG_TPC = 3


def build_l1_ffn_wquant(p, st, io):
    nc = p.nc
    pfx = io.get("pfx", "")

    def sb(st_, name, shape, dt):
        return st_.enter_context(nc.sbuf_tensor(pfx + name, shape, dt))

    w = io
    wg_in = nc.dram_tensor(pfx + "wg_in", [WG_TPC * 128, D], BF16).ap()
    wu_in = nc.dram_tensor(pfx + "wu_in", [WG_TPC * 128, D], BF16).ap()
    wd_in = nc.dram_tensor(pfx + "wd_in", [128, I_FF], BF16).ap()
    wg_out = nc.dram_tensor(pfx + "wg_out", [8 * WG_TPC * 128, D], BF16, addr_space="Shared").ap()
    wu_out = nc.dram_tensor(pfx + "wu_out", [8 * WG_TPC * 128, D], BF16, addr_space="Shared").ap()
    wd_out = nc.dram_tensor(pfx + "wd_out", [8 * 128, I_FF], BF16, addr_space="Shared").ap()
    wq = ExitStack()
    wtile = sb(wq, "wtile", [128, I_FF], F32)
    wdeq = sb(wq, "wdeq", [128, I_FF], BF16)
    scr = {"bm": sb(wq, "q_bm", [128, 2 * (I_FF // 16)], F32),
           "xn": sb(wq, "q_xn", [128, I_FF], F32),
           "m": sb(wq, "q_m", [128, I_FF], F32)}
    for name, src_rows, cin in (("wg", "wg_rows", wg_in), ("wu", "wu_rows", wu_in)):
        for j in range(WG_TPC):
            p.do("sync", lambda s, j=j, src_rows=src_rows: s.dma_start(
                out=wtile[:, 0:D], in_=w[src_rows][j * 128:(j + 1) * 128, :]),
                reads=[w[src_rows]], writes=[wtile], dma=True, tag="wld")
            scr_d = {"bm": _View(scr["bm"], 0, 2 * (D // 16)), "xn": _View(scr["xn"], 0, D),
                     "m": _View(scr["m"], 0, D)}
            emit_fp4_quant(p, _View(wtile, 0, D), _View(wdeq, 0, D), scr_d, D)
            p.do("sync", lambda s, j=j, cin=cin: s.dma_start(
                out=cin[j * 128:(j + 1) * 128, :], in_=wdeq[:, 0:D]),
                reads=[wdeq], writes=[cin], dma=True, tag="wst")
    p.do("sync", lambda s: s.dma_start(out=wtile[:], in_=w["wd_rows"][:]),
         reads=[w["wd_rows"]], writes=[wtile], dma=True, tag="wldd")
    emit_fp4_quant(p, _View(wtile, 0, I_FF), _View(wdeq, 0, I_FF), scr, I_FF)
    p.do("sync", lambda s: s.dma_start(out=wd_in[:], in_=wdeq[:]),
         reads=[wdeq], writes=[wd_in], dma=True, tag="wstd")
    for cin, cout in ((wg_in, wg_out), (wu_in, wu_out), (wd_in, wd_out)):
        cc = p.do("gpsimd", lambda g, cin=cin, cout=cout: g.collective_compute(
            "AllGather", ALU.bypass, replica_groups=[list(range(8))],
            ins=[cin.opt()], outs=[cout.opt()]),
            reads=[cin], writes=[cout], tag="ag")
        cc.amount = 1
        cc.sem_key = "cc_" + _tname(cout)
    wq.close()
    return {"wg_out": wg_out, "wu_out": wu_out, "wd_out": wd_out}


def build_l1_ffn(p, st, io):
    nc = p.nc
    pfx = io.get("pfx", "")
    NT_ALL = (B_PER_CORE * T) // 128
    NC = D // 128
    x_ap, out_ap, w = io["x"], io["out"], io

    def sb(st_, name, shape, dt):
        return st_.enter_context(nc.sbuf_tensor(pfx + name, shape, dt))

    if not io.get("cvals_ready"):
        cvals_s = sb(st, "cvals_s", [128, len(CONST_VALS)], F32)
        p.do("gpsimd", lambda g: g.dma_start(out=cvals_s[:], in_=w["cvals"]),
             reads=[w["cvals"]], writes=[cvals_s], dma=True)
        for j, v in enumerate(CONST_VALS):
            nc.const_aps.register(F32, float(v), cvals_s[:, j:j + 1])
        p.fence()
    iden_b = sb(st, "idenb_s", [128, 128], BF16)
    p.do("gpsimd", lambda g: g.dma_start(out=iden_b[:], in_=w["iden_f"]),
         reads=[w["iden_f"]], writes=[iden_b], dma=True)
    psb = io.get("psb") or [st.enter_context(nc.psum_tensor(f"psb{i}", [128, 512], F32)) for i in range(8)]

    outs = io.get("wq_outs")
    if outs is None:
        outs = build_l1_ffn_wquant(p, st, io)
    wg_out, wu_out, wd_out = outs["wg_out"], outs["wu_out"], outs["wd_out"]

    wgT = [sb(st, f"wgT{c}", [128, I_FF], BF16) for c in range(NC)]
    wuT = [sb(st, f"wuT{c}", [128, I_FF], BF16) for c in range(NC)]
    wdT = [sb(st, f"wdT{c}", [128, D], BF16) for c in range(NIT)]
    wl = ExitStack()
    wrow = [sb(wl, f"wrow{i}", [128, I_FF], BF16) for i in range(2)]
    for t in range(NIT):
        r, j = t % 8, t // 8
        buf = wrow[t % 2]
        for which, pool, dstT in (("g", wg_out, wgT), ("u", wu_out, wuT)):
            p.do("sync", lambda s, buf=buf, pool=pool, r=r, j=j: s.dma_start(
                out=buf[:, 0:D], in_=pool[r * WG_TPC * 128 + j * 128: r * WG_TPC * 128 + (j + 1) * 128, :]),
                reads=[pool], writes=[buf], dma=True, tag="wgld")
            for c in range(NC):
                ps = psb[c % 2]
                p.do("tensor", lambda tt_, buf=buf, c=c, ps=ps: tt_.transpose(
                    ps.bitcast(BF16)[:, 0:128], buf[:, c * 128:(c + 1) * 128], iden_b[:]),
                    reads=[buf, iden_b], writes=[ps], tag="wgtr")
                p.do("vector", lambda v, ps=ps, c=c, t=t, dstT=dstT: v.tensor_copy(
                    out=dstT[c][:, t * 128:(t + 1) * 128], in_=ps.bitcast(BF16)[:, 0:128]),
                    reads=[ps], writes=[dstT[c]], tag="wgev")
    for r in range(8):
        buf = wrow[r % 2]
        p.do("sync", lambda s, buf=buf, r=r: s.dma_start(
            out=buf[:], in_=wd_out[r * 128:(r + 1) * 128, :]),
            reads=[wd_out], writes=[buf], dma=True, tag="wdld")
        for c in range(NIT):
            ps = psb[c % 2]
            p.do("tensor", lambda tt_, buf=buf, c=c, ps=ps: tt_.transpose(
                ps.bitcast(BF16)[:, 0:128], buf[:, c * 128:(c + 1) * 128], iden_b[:]),
                reads=[buf, iden_b], writes=[ps], tag="wdtr")
            p.do("vector", lambda v, ps=ps, c=c, r=r: v.tensor_copy(
                out=wdT[c][:, r * 128:(r + 1) * 128], in_=ps.bitcast(BF16)[:, 0:128]),
                reads=[ps], writes=[wdT[c]], tag="wdev")
    p.fence()
    wl.close()

    act = ExitStack()
    xt = [sb(act, f"xt{i}", [128, D], F32) for i in range(2)]
    ut = [sb(act, f"ut{i}", [128, D], F32) for i in range(2)]
    sqs = [sb(act, f"sqs{i}", [128, 12], F32) for i in range(2)]
    stat = [sb(act, f"stat{i}", [128, 8], F32) for i in range(2)]
    uq = [sb(act, f"uq{i}", [128, D], BF16) for i in range(2)]
    xq_fm = [[sb(act, f"xq{i}_{c}", [128, 128], BF16) for c in range(NC)] for i in range(2)]
    g_sb = [sb(act, f"g_sb{i}", [128, 512], BF16) for i in range(3)]
    h_seg = [sb(act, f"h_seg{i}", [128, 512], F32) for i in range(3)]
    hq = [sb(act, f"hq{i}", [128, I_FF], BF16) for i in range(2)]
    hq_fm = [[sb(act, f"hq{i}_{c}", [128, 128], BF16) for c in range(NIT)] for i in range(2)]
    scr2 = [{"bm": sb(act, f"q2bm{i}", [128, 128], F32),
             "xn": sb(act, f"q2xn{i}", [128, 512], F32),
             "m": sb(act, f"q2m{i}", [128, 512], F32)} for i in range(2)]
    scru = {"bm": sb(act, "qubm", [128, 128], F32),
            "xn": sb(act, "quxn", [128, D], F32),
            "m": sb(act, "qum", [128, D], F32)}
    ot = ut
    IC = [(0, 512), (512, 512), (1024, 512), (1536, 512), (2048, 512), (2560, 256)]
    final_stores = []

    def head(tt):
        i2 = tt % 2
        X, UT, UQ = xt[i2], ut[i2], uq[i2]
        p.do("sync", lambda s, tt=tt, X=X: s.dma_start(
            out=X[:], in_=x_ap[tt * 128:(tt + 1) * 128, :]), reads=[x_ap], writes=[X], dma=True, tag="ldx")
        emit_ln_tile(p, X, UT, stat[i2], sqs[i2])
        return emit_fp4_quant(p, _View(UT, 0, D), _View(UQ, 0, D),
                              {"bm": _View(scru["bm"], 0, 2 * (D // 16)), "xn": _View(scru["xn"], 0, D),
                               "m": _View(scru["m"], 0, D)}, D, pool_adds=True, defer_deq=True)

    u_deq = head(0)
    u_deq()
    u_deq = None
    for tt in range(NT_ALL):
        i2 = tt % 2
        X, UQ, HQ, OT = xt[i2], uq[i2], hq[i2], ot[i2]
        XQ, HQF = xq_fm[i2], hq_fm[i2]
        if tt + 1 < NT_ALL:
            u_deq = head(tt + 1)
        for c in range(NC):
            ps = psb[6 + c % 2]
            p.do("tensor", lambda t_, c=c, ps=ps, UQ=UQ: t_.transpose(
                ps.bitcast(BF16)[:, 0:128], UQ[:, c * 128:(c + 1) * 128], iden_b[:]),
                reads=[UQ, iden_b], writes=[ps], tag="xqtr")
            p.do("scalar", lambda s, c=c, ps=ps, XQ=XQ: s.copy(
                out=XQ[c][:], in_=ps.bitcast(BF16)[:, 0:128]),
                reads=[ps], writes=[XQ[c]], tag="xqev")
        def emit_hqtr(chunks):
            hbanks = (6, 7, 0, 1)
            for c in chunks:
                ps = psb[hbanks[c % 4]]
                p.do("tensor", lambda t_, c=c, ps=ps, HQ=HQ: t_.transpose(
                    ps.bitcast(BF16)[:, 0:128], HQ[:, c * 128:(c + 1) * 128], iden_b[:]),
                    reads=[HQ, iden_b], writes=[ps], tag="hqtr")
                p.do("scalar", lambda s, c=c, ps=ps, HQF=HQF: s.copy(
                    out=HQF[c][:], in_=ps.bitcast(BF16)[:, 0:128]),
                    reads=[ps], writes=[HQF[c]], tag="hqev")

        pend_deq = None
        pend_chunks = None
        for io_, (c0, cw) in enumerate(IC):
            pg = psb[io_ % 3]
            pu_ = psb[3 + io_ % 3]
            GS = g_sb[io_ % 3]
            SC2s = scr2[io_ % 2]
            for c in range(NC):
                p.do("tensor", lambda t_, c=c, c0=c0, cw=cw, pg=pg, XQ=XQ: t_.matmul(
                    pg[:, 0:cw], XQ[c][:], wgT[c][:, c0:c0 + cw],
                    start=(c == 0), stop=(c == NC - 1)), reads=[XQ[c], wgT[c]], writes=[pg], tag="gmm")
            p.do("scalar", lambda s, cw=cw, pg=pg, GS=GS: s.activation(
                out=GS[:, 0:cw], in_=pg[0:128, 0:cw], func=AF.Silu),
                reads=[pg], writes=[GS], tag="silu")
            for c in range(NC):
                p.do("tensor", lambda t_, c=c, c0=c0, cw=cw, pu_=pu_, XQ=XQ: t_.matmul(
                    pu_[:, 0:cw], XQ[c][:], wuT[c][:, c0:c0 + cw],
                    start=(c == 0), stop=(c == NC - 1)), reads=[XQ[c], wuT[c]], writes=[pu_], tag="umm")
            HS = h_seg[io_ % 3]
            p.do("vector", lambda v, cw=cw, pu_=pu_, HS=HS, GS=GS: v.tensor_tensor(
                out=HS[:, 0:cw], in0=pu_[0:128, 0:cw], in1=GS[:, 0:cw], op=ALU.mult),
                reads=[pu_, GS], writes=[HS], tag="hmul")
            d_ = emit_fp4_quant(p, _View(HS, 0, cw), _View(HQ, c0, cw),
                                {"bm": _View(SC2s["bm"], 0, 2 * (cw // 16)),
                                 "xn": _View(SC2s["xn"], 0, cw), "m": _View(SC2s["m"], 0, cw)},
                                cw, pool_adds=True, defer_deq=True)
            if io_ == 0 and u_deq is not None:
                u_deq()
                u_deq = None
            if pend_deq is not None:
                pend_deq()
                if io_ == 5:
                    emit_hqtr(range(0, 16))
            pend_deq = d_
        pend_deq()
        emit_hqtr(range(16, NIT))
        for oc in range(2):
            ps = psb[6 + oc]
            for c in range(NIT):
                p.do("tensor", lambda t_, c=c, oc=oc, ps=ps, HQF=HQF: t_.matmul(
                    ps[:], HQF[c][:], wdT[c][:, oc * 512:(oc + 1) * 512],
                    start=(c == 0), stop=(c == NIT - 1)), reads=[HQF[c], wdT[c]], writes=[ps], tag="dmm")
            p.do("vector", lambda v, oc=oc, X=X, ps=ps, OT=OT: v.tensor_tensor(
                out=OT[:, oc * 512:(oc + 1) * 512], in0=X[:, oc * 512:(oc + 1) * 512],
                in1=ps[:], op=ALU.add), reads=[X, ps], writes=[OT], tag="oadd")
        stx = p.do("sync", lambda s, tt=tt, OT=OT: s.dma_start(
            out=out_ap[tt * 128:(tt + 1) * 128, :], in_=OT[:]), reads=[OT], writes=[out_ap], dma=True, tag="stx")
        final_stores.append(stx)
    p.fence()
    act.close()
    return final_stores


class _View:
    def __init__(self, t, c0, n):
        self.t, self.c0, self.n = t, c0, n
        self.tensor = t[:].tensor

    def __getitem__(self, sl):
        return self.t[:, self.c0:self.c0 + self.n][sl]


def make_launch_d():
    nc = bass.Bass("TRN2", target_bir_lowering=False)
    st = ExitStack()
    io = {}
    io["x"] = nc.dram_tensor("x", [B_PER_CORE * T, D], F32, kind="ExternalInput").ap()
    io["out"] = nc.dram_tensor("out", [B_PER_CORE * T, D], F32, kind="ExternalOutput").ap()
    for name, shape in [("wg_rows", [WG_TPC * 128, D]), ("wu_rows", [WG_TPC * 128, D]),
                        ("wd_rows", [128, I_FF])]:
        io[name] = nc.dram_tensor(name, shape, F32, kind="ExternalInput").ap()
    for name in ("iden_f", "cvals"):
        io[name] = nc.dram_tensor(name, list(HOST_CONST_SHAPES[name]), F32, kind="ExternalInput").ap()
    p = Prog(nc)
    with st:
        finals = build_l1_ffn(p, st, io)
        p.do("gpsimd", lambda g: g.engine_nop(), deps=finals, tag="fence")
        with nc.Block() as block:
            p.emit(block, st)
    return nc


def l1_ffn_maps(inputs, x_full):
    consts = host_constants()
    xs = x_full.reshape(16, T, D)
    wg = inputs["L1_ffn_w_gate"]
    wu = inputs["L1_ffn_w_up"]
    wd = inputs["L1_ffn_w_down"]
    in_maps = []
    for c in range(NCORE):
        m = {"x": np.ascontiguousarray(xs[2 * c:2 * c + 2].reshape(B_PER_CORE * T, D))}
        rows = []
        for j in range(WG_TPC):
            t = c + 8 * j
            if t < NIT:
                rows.append(wg[t * 128:(t + 1) * 128])
            else:
                rows.append(np.zeros((128, D), np.float32))
        m["wg_rows"] = np.ascontiguousarray(np.concatenate(rows, 0))
        rows = []
        for j in range(WG_TPC):
            t = c + 8 * j
            if t < NIT:
                rows.append(wu[t * 128:(t + 1) * 128])
            else:
                rows.append(np.zeros((128, D), np.float32))
        m["wu_rows"] = np.ascontiguousarray(np.concatenate(rows, 0))
        m["wd_rows"] = np.ascontiguousarray(wd[c * 128:(c + 1) * 128])
        m["iden_f"] = consts["iden_f"]
        m["cvals"] = consts["cvals"]
        in_maps.append(m)
    return in_maps



def build_l1_attn(p, st, io):
    nc = p.nc
    pfx = io.get("pfx", "")
    NT, NC, NQC = T // 128, D // 128, T // 512
    x_ap, out_ap, w = io["x"], io["out"], io

    def sb(st_, name, shape, dt):
        return st_.enter_context(nc.sbuf_tensor(pfx + name, shape, dt))

    if not io.get("cvals_ready"):
        cvals_s = sb(st, "cvals_s", [128, len(CONST_VALS)], F32)
        p.do("gpsimd", lambda g: g.dma_start(out=cvals_s[:], in_=w["cvals"]),
             reads=[w["cvals"]], writes=[cvals_s], dma=True)
        for j, v in enumerate(CONST_VALS):
            nc.const_aps.register(F32, float(v), cvals_s[:, j:j + 1])
        p.fence()
    iden_f = sb(st, "idenf_s", [128, 128], F32)
    iden_b = sb(st, "idenb_s", [128, 128], BF16)
    diagm = sb(st, "diagm_s", [128, 896], BF16)
    for dst, src_ in ((iden_f, "iden_f"), (iden_b, "iden_f"), (diagm, "diagmask4")):
        p.do("gpsimd", lambda g, dst=dst, src_=src_: g.dma_start(out=dst[:], in_=w[src_]),
             reads=[w[src_]], writes=[dst], dma=True)
    psb = io.get("psb") or [st.enter_context(nc.psum_tensor(f"psb{i}", [128, 512], F32)) for i in range(8)]
    wtmp = [sb(st, f"wtmp{i}", [128, D], BF16) for i in range(2)]

    WT = {}
    for name in ("Wq", "Wk", "Wv"):
        buf = wtmp[0]
        p.do("gpsimd", lambda g, name=name, buf=buf: g.dma_start(out=buf[:], in_=w[name]),
             reads=[w[name]], writes=[buf], dma=True, tag="wld")
        wt = [sb(st, f"{name}T{c}", [128, 128], BF16) for c in range(NC)]
        for c in range(NC):
            ps = psb[c % 2]
            p.do("tensor", lambda t, buf=buf, c=c, ps=ps: t.transpose(
                ps.bitcast(BF16)[:, 0:128], buf[:, c * 128:(c + 1) * 128], iden_b[:]),
                reads=[buf, iden_b], writes=[ps], tag="wtr")
            p.do("vector", lambda v, ps=ps, c=c, wt=wt: v.tensor_copy(
                out=wt[c][:], in_=ps.bitcast(BF16)[:, 0:128]),
                reads=[ps], writes=[wt[c]], tag="wev")
        WT[name] = wt
    WoT = sb(st, "WoT", [128, D], BF16)
    for r in range(NC):
        buf = wtmp[1]
        p.do("gpsimd", lambda g, r=r, buf=buf: g.dma_start(
            out=buf[:, 0:128], in_=w["Wo"][r * 128:(r + 1) * 128, :]),
            reads=[w["Wo"]], writes=[buf], dma=True, tag="wold")
        ps = psb[2]
        p.do("tensor", lambda t, buf=buf, ps=ps: t.transpose(
            ps.bitcast(BF16)[:, 0:128], buf[:, 0:128], iden_b[:]),
            reads=[buf, iden_b], writes=[ps], tag="wotr")
        p.do("vector", lambda v, ps=ps, r=r: v.tensor_copy(
            out=WoT[:, r * 128:(r + 1) * 128], in_=ps.bitcast(BF16)[:, 0:128]),
            reads=[ps], writes=[WoT], tag="woev")
    onecol_cb = sb(st, "onecol_cb", [128, 1], BF16)
    p.do("gpsimd", lambda g: g.dma_start(out=onecol_cb[:], in_=w["ones_col"]),
         reads=[w["ones_col"]], writes=[onecol_cb], dma=True)
    bq_c = sb(st, "bq_c", [128, 1], F32)
    bk_c = sb(st, "bk_c", [128, 1], F32)
    bv_r = sb(st, "bv_r", [1, 128], BF16)
    bo_c = sb(st, "bo_c", [128, NC], F32)
    onerow_b = sb(st, "onerow_c", [1, 128], BF16)
    for dst, src_ in ((bq_c, "bq_col"), (bk_c, "bk_col"), (bv_r, "bv_row"), (bo_c, "bo_cols"),
                      (onerow_b, "ones_row")):
        p.do("gpsimd", lambda g, dst=dst, src_=src_: g.dma_start(out=dst[:], in_=w[src_]),
             reads=[w[src_]], writes=[dst], dma=True)

    SCL = 0.08838834764831845
    MR = 339.41125496954285
    final_stores = []
    for b in range(B_PER_CORE):
        xb = x_ap[b * T:(b + 1) * T, :]
        ob = out_ap[b * T:(b + 1) * T, :]
        bst = ExitStack()
        xt = [sb(bst, f"xt{b}_{i}", [128, D], F32) for i in range(2)]
        ut = [sb(bst, f"ut{b}_{i}", [128, D], F32) for i in range(2)]
        sqs = [sb(bst, f"sqs{b}_{i}", [128, 12], F32) for i in range(2)]
        stat = [sb(bst, f"stat{b}_{i}", [128, 8], F32) for i in range(2)]
        u_fm = [sb(bst, f"cu{b}_fm{c}", [128, T], BF16) for c in range(NC)]
        Q = sb(bst, f"Q{b}", [128, T], BF16)
        K = sb(bst, f"K{b}", [128, T], BF16)
        Vt = sb(bst, f"V{b}", [128, 16 * 128], BF16)
        qa = sb(bst, f"qa{b}", [128, T], BF16)
        mrow = sb(bst, f"mrow{b}", [1, T], BF16)
        kmb = sb(bst, f"km{b}", [128, 1], F32)
        kmbb = sb(bst, f"kmb{b}", [128, 1], BF16)
        p_sb = [sb(bst, f"p{b}_{i}", [128, 512], BF16) for i in range(3)]
        o_sb = sb(bst, f"o_sb{b}", [128, 512], BF16)
        rdenf = sb(bst, f"rdf{b}", [1, 512], F32)
        rden = sb(bst, f"rd{b}", [1, 512], BF16)
        attn_n = sb(bst, f"an{b}", [128, T], BF16)
        upd_fm = [sb(bst, f"up{b}_{c}", [128, T], BF16) for c in range(NC)]
        ot = [sb(bst, f"ot{b}_{i}", [128, D], F32) for i in range(2)]
        def c_head(tt):
            X, UT = xt[tt % 2], ut[tt % 2]
            p.do("sync", lambda s, tt=tt, X=X, xb=xb: s.dma_start(
                out=X[:], in_=xb[tt * 128:(tt + 1) * 128, :]), reads=[xb], writes=[X], dma=True, tag="ldx")
            emit_ln_tile(p, X, UT, stat[tt % 2], sqs[tt % 2])

        c_head(0)
        for tt in range(NT):
            UT = ut[tt % 2]
            if tt + 1 < NT:
                c_head(tt + 1)
            for c in range(NC):
                ps = psb[c % 2]
                p.do("tensor", lambda t, c=c, ps=ps, UT=UT: t.transpose(
                    ps[:, 0:128], UT[:, c * 128:(c + 1) * 128], iden_f[:]),
                    reads=[UT, iden_f], writes=[ps], tag="utr")
                p.do("vector", lambda v, tt=tt, c=c, ps=ps: v.tensor_copy(
                    out=u_fm[c][:, tt * 128:(tt + 1) * 128], in_=ps[:, 0:128]),
                    reads=[ps], writes=[u_fm[c]], tag="uev")
        for which, dst, bias in (("Wq", Q, bq_c), ("Wk", K, bk_c)):
            for qc in range(NQC):
                ps = psb[qc % 4]
                for c in range(NC):
                    p.do("tensor", lambda t, c=c, qc=qc, which=which, ps=ps: t.matmul(
                        ps[:], WT[which][c][:], u_fm[c][:, qc * 512:(qc + 1) * 512],
                        start=(c == 0), stop=(c == NC - 1)),
                        reads=[WT[which][c], u_fm[c]], writes=[ps], tag="qkproj")
                p.do("scalar", lambda s, qc=qc, ps=ps, dst=dst, bias=bias: s.activation(
                    out=dst[:, qc * 512:(qc + 1) * 512], in_=ps[:], func=AF.Identity, bias=bias[:]),
                    reads=[ps, bias], writes=[dst], tag="qkev")
        for tt in range(NT):
            ps = psb[4 + tt % 2]
            for c in range(NC):
                p.do("tensor", lambda t, c=c, tt=tt, ps=ps: t.matmul(
                    ps[:, 0:128], u_fm[c][:, tt * 128:(tt + 1) * 128], WT["Wv"][c][:],
                    start=(c == 0), stop=False), reads=[u_fm[c], WT["Wv"][c]], writes=[ps], tag="vproj")
            p.do("tensor", lambda t, tt=tt, ps=ps: t.matmul(
                ps[:, 0:128], onerow_b[:], bv_r[:], start=False, stop=True),
                reads=[onerow_b, bv_r], writes=[ps], tag="vbias")
            p.do("vector", lambda v, tt=tt, ps=ps: v.tensor_copy(
                out=Vt[:, tt * 128:(tt + 1) * 128], in_=ps[:, 0:128]),
                reads=[ps], writes=[Vt], tag="vev")
        p.do("scalar", lambda s: s.activation(out=qa[:], in_=Q[:], func=AF.Abs),
             reads=[Q], writes=[qa], tag="qa")
        p.do("vector", lambda v: v.tensor_reduce(out=kmb[:], in_=K[:], axis=AX.X, op=ALU.max,
                                                 apply_absolute_value=True), reads=[K], writes=[kmb])
        p.do("vector", lambda v: v.tensor_copy(out=kmbb[:], in_=kmb[:]), reads=[kmb], writes=[kmbb])
        for qc in range(NQC):
            p.do("tensor", lambda t, qc=qc: t.matmul(
                psb[6][0:1, :], kmbb[:], qa[:, qc * 512:(qc + 1) * 512], start=True, stop=True),
                reads=[kmbb, qa], writes=[psb[6]], tag="mhat")
            p.do("scalar", lambda s, qc=qc: s.activation(
                out=mrow[:, qc * 512:(qc + 1) * 512], in_=psb[6][0:1, :], func=AF.Identity,
                scale=-1.0, bias=MR), reads=[psb[6]], writes=[mrow], tag="mrow")
        def c_emit_S(kt, qc):
            psS = psb[kt % 3]
            P = p_sb[kt % 3]
            diag = kt >= 4 * qc
            if diag:
                v_i = kt - 4 * qc
                p.do("tensor", lambda t, v_i=v_i, psS=psS: t.matmul(
                    psS[:], iden_b[:],
                    diagm[:, (3 - v_i) * 128:(3 - v_i) * 128 + 512],
                    start=True, stop=False), reads=[iden_b, diagm], writes=[psS], tag="Smask")
            p.do("tensor", lambda t, kt=kt, qc=qc, psS=psS, diag=diag: t.matmul(
                psS[:], K[:, kt * 128:(kt + 1) * 128], Q[:, qc * 512:(qc + 1) * 512],
                start=not diag, stop=False), reads=[K, Q], writes=[psS], tag="S")
            p.do("tensor", lambda t, qc=qc, psS=psS: t.matmul(
                psS[:], onerow_b[:], mrow[:, qc * 512:(qc + 1) * 512],
                start=False, stop=True), reads=[onerow_b, mrow], writes=[psS], tag="Sm")
            p.do("scalar", lambda s, psS=psS, P=P: s.activation(
                out=P[:], in_=psS[:], func=AF.Exp, scale=SCL),
                reads=[psS], writes=[P], tag="exp")

        def c_emit_PV(kt, qc, nkt):
            P = p_sb[kt % 3]
            pvb = psb[4 + 2 * (qc % 2)]
            dnb = psb[5 + 2 * (qc % 2)]
            p.do("tensor", lambda t, kt=kt, P=P, nkt=nkt, pvb=pvb: t.matmul(
                pvb[:], Vt[:, kt * 128:(kt + 1) * 128].opt(), P[:],
                start=(kt == 0), stop=(kt == nkt - 1)), reads=[Vt, P], writes=[pvb], tag="PV")
            p.do("tensor", lambda t, kt=kt, P=P, nkt=nkt, dnb=dnb: t.matmul(
                dnb[0:1, :], onecol_cb[:], P[:],
                start=(kt == 0), stop=(kt == nkt - 1)), reads=[onecol_cb, P], writes=[dnb], tag="den")

        def c_make_tail(qc, nkt):
            pvb = psb[4 + 2 * (qc % 2)]
            dnb = psb[5 + 2 * (qc % 2)]
            bcb = psb[(nkt - 1) % 3]

            def tail():
                p.do("scalar", lambda s, dnb=dnb: s.activation(out=rdenf[:], in_=dnb[0:1, :], func=AF.Ln),
                     reads=[dnb], writes=[rdenf], tag="rdl")
                p.do("scalar", lambda s: s.activation(out=rden[:], in_=rdenf[:], func=AF.Exp, scale=-1.0),
                     reads=[rdenf], writes=[rden], tag="rde")
                p.do("tensor", lambda t, bcb=bcb: t.matmul(
                    bcb[:], onerow_b[:], rden[:], start=True, stop=True),
                    reads=[onerow_b, rden], writes=[bcb], tag="bcast")
                p.do("scalar", lambda s, pvb=pvb: s.copy(out=o_sb[:], in_=pvb[:]),
                     reads=[pvb], writes=[o_sb], tag="oev")
                p.do("vector", lambda v, qc=qc, bcb=bcb: v.tensor_tensor(
                    out=attn_n[:, qc * 512:(qc + 1) * 512], in0=o_sb[:], in1=bcb[:], op=ALU.mult),
                    reads=[o_sb, bcb], writes=[attn_n], tag="an")
            return tail

        pend_tail = None
        for qc in range(NQC):
            nkt = 4 * qc + 4
            for kt in range(nkt):
                c_emit_S(kt, qc)
                if kt == 2 and pend_tail is not None:
                    pend_tail()
                    pend_tail = None
                if kt >= 2:
                    c_emit_PV(kt - 2, qc, nkt)
            c_emit_PV(nkt - 2, qc, nkt)
            c_emit_PV(nkt - 1, qc, nkt)
            pend_tail = c_make_tail(qc, nkt)
        pend_tail()
        for dc in range(NC):
            for qc in range(NQC):
                ps = psb[4 + (dc * NQC + qc) % 2]
                p.do("tensor", lambda t, dc=dc, qc=qc, ps=ps: t.matmul(
                    ps[:], WoT[:, dc * 128:(dc + 1) * 128], attn_n[:, qc * 512:(qc + 1) * 512],
                    start=True, stop=True), reads=[WoT, attn_n], writes=[ps], tag="oproj")
                p.do("scalar", lambda s, dc=dc, qc=qc, ps=ps: s.activation(
                    out=upd_fm[dc][:, qc * 512:(qc + 1) * 512], in_=ps[:], func=AF.Identity,
                    bias=bo_c[:, dc:dc + 1]), reads=[ps, bo_c], writes=[upd_fm[dc]], tag="opev")
        for tt in range(NT):
            i = tt % 2
            X, O = xt[i], ot[i]
            p.do("sync", lambda s, tt=tt, X=X, xb=xb: s.dma_start(
                out=X[:], in_=xb[tt * 128:(tt + 1) * 128, :]), reads=[xb], writes=[X], dma=True, tag="ldx2")
            for c in range(NC):
                pu = psb[6 + c % 2]
                p.do("tensor", lambda t, tt=tt, c=c, pu=pu: t.transpose(
                    pu.bitcast(BF16)[:, 0:128], upd_fm[c][:, tt * 128:(tt + 1) * 128], iden_b[:]),
                    reads=[upd_fm[c], iden_b], writes=[pu], tag="trO")
                p.do("vector", lambda v, c=c, X=X, O=O, pu=pu: v.tensor_tensor(
                    out=O[:, c * 128:(c + 1) * 128], in0=X[:, c * 128:(c + 1) * 128],
                    in1=pu.bitcast(BF16)[:, 0:128], op=ALU.add), reads=[X, pu], writes=[O], tag="oadd")
            stx = p.do("sync", lambda s, tt=tt, O=O, ob=ob: s.dma_start(
                out=ob[tt * 128:(tt + 1) * 128, :], in_=O[:]), reads=[O], writes=[ob], dma=True, tag="stx")
            final_stores.append(stx)
        p.fence()
        bst.close()
    return final_stores





def make_launch_c():
    nc = bass.Bass("TRN2", target_bir_lowering=False)
    st = ExitStack()
    io = {}
    io["x"] = nc.dram_tensor("x", [B_PER_CORE * T, D], F32, kind="ExternalInput").ap()
    io["out"] = nc.dram_tensor("out", [B_PER_CORE * T, D], F32, kind="ExternalOutput").ap()
    for name, shape in [("Wq", [128, D]), ("Wk", [128, D]), ("Wv", [128, D]), ("Wo", [D, 128]),
                        ("bq_col", [128, 1]), ("bk_col", [128, 1]), ("bv_row", [1, 128]),
                        ("bo_cols", [128, 8])]:
        io[name] = nc.dram_tensor(name, shape, F32, kind="ExternalInput").ap()
    for name in ("iden_f", "cvals", "diagmask4", "ones_row", "ones_col"):
        io[name] = nc.dram_tensor(name, list(HOST_CONST_SHAPES[name]), F32, kind="ExternalInput").ap()
    p = Prog(nc)
    with st:
        finals = build_l1_attn(p, st, io)
        p.do("gpsimd", lambda g: g.engine_nop(), deps=finals, tag="fence")
        with nc.Block() as block:
            p.emit(block, st)
    return nc


def l1_attn_maps(inputs, x_full):
    consts = host_constants()
    xs = x_full.reshape(16, T, D)
    in_maps = []
    for c in range(NCORE):
        m = {"x": np.ascontiguousarray(xs[2 * c:2 * c + 2].reshape(B_PER_CORE * T, D))}
        for k in ("Wq", "Wk", "Wv", "Wo"):
            m[k] = inputs["L1_attn_" + k]
        m["bq_col"] = np.ascontiguousarray(inputs["L1_attn_bq"].reshape(128, 1))
        m["bk_col"] = np.ascontiguousarray(inputs["L1_attn_bk"].reshape(128, 1))
        m["bv_row"] = np.ascontiguousarray(inputs["L1_attn_bv"].reshape(1, 128))
        m["bo_cols"] = np.ascontiguousarray(inputs["L1_attn_bo"].reshape(8, 128).T)
        for k in ("iden_f", "cvals", "diagmask4", "ones_row", "ones_col"):
            m[k] = consts[k]
        in_maps.append(m)
    return in_maps



CAP = 5121
RTILES = 41
FEXP = 4096


def build_l0_ffn(p, st, io):
    nc = p.nc
    pfx = io.get("pfx", "")
    NT_ALL = (B_PER_CORE * T) // 128
    NC = D // 128
    x_ap, out_ap, w = io["x"], io["out"], io

    def sb(st_, name, shape, dt):
        return st_.enter_context(nc.sbuf_tensor(pfx + name, shape, dt))

    if not io.get("cvals_ready"):
        cvals_s = sb(st, "cvals_s", [128, len(CONST_VALS)], F32)
        p.do("gpsimd", lambda g: g.dma_start(out=cvals_s[:], in_=w["cvals"]),
             reads=[w["cvals"]], writes=[cvals_s], dma=True)
        for j, v in enumerate(CONST_VALS):
            nc.const_aps.register(F32, float(v), cvals_s[:, j:j + 1])
        p.fence()
    iden_f = sb(st, "idenf_s", [128, 128], F32)
    iden_b = sb(st, "idenb_s", [128, 128], BF16)
    my1 = sb(st, "my1_s", [8, 1], F32)
    io8 = sb(st, "io8_s", [128, 8], F32)
    ST8 = sb(st, "ST8_s", [8, 8], F32)
    rep8 = sb(st, "rep8_s", [8, 8 * 128], F32)
    ones_row_f = sb(st, "ones_row_f", [1, 128], F32)
    for dst, src_ in ((iden_f, "iden_f"), (iden_b, "iden_f"), (my1, "my1_col"), (io8, "iota8m8"),
                      (ST8, "st8"), (rep8, "rep8"), (ones_row_f, "ones_row")):
        p.do("gpsimd", lambda g, dst=dst, src_=src_: g.dma_start(out=dst[:], in_=w[src_]),
             reads=[w[src_]], writes=[dst], dma=True)
    psb = io.get("psb") or [st.enter_context(nc.psum_tensor(f"psb{i}", [128, 512], F32)) for i in range(8)]

    lsort = io["lsort"]
    pool = io["pool"]
    cnt_in = io["cnt_in"]
    cnt_out = io["cnt_out"]
    ybuf = io["ybuf"]
    ypool = io["ypool"]
    ustage = io["ustage"]

    wgc = [sb(st, f"wgc{c}", [128, 8], BF16) for c in range(NC)]
    for c in range(NC):
        p.do("gpsimd", lambda g, c=c: g.dma_start(out=wgc[c][:], in_=w["wg"][c * 128:(c + 1) * 128, :]),
             reads=[w["wg"]], writes=[wgc[c]], dma=True, tag="wgld", skey="wgc")
    p.do("gpsimd", lambda g: g.engine_nop(), reads=[], writes=wgc, tag="wgcbar")

    cw_all = sb(st, "cw_all", [128, NT_ALL], F32)
    t1_all = sb(st, "t1_all", [128, NT_ALL], F32)
    rk_all = sb(st, "rk_all", [128, NT_ALL], F32)

    rph = ExitStack()
    xt = [sb(rph, f"xt{i}", [128, D], F32) for i in range(2)]
    ut_ = [sb(rph, f"ut{i}", [128, D], F32) for i in range(2)]
    ub_ = [sb(rph, f"ub{i}", [128, D], BF16) for i in range(2)]
    sqs_ = [sb(rph, f"sqs{i}", [128, 12], F32) for i in range(2)]
    stat_ = [sb(rph, f"stat{i}", [128, 8], F32) for i in range(2)]
    ufc_ = [sb(rph, f"ufc{i}", [128, 128], BF16) for i in range(2)]
    mask_ = [sb(rph, f"mask{i}", [128, 8], F32) for i in range(2)]
    et_ = [sb(rph, f"et{i}", [128, 8], F32) for i in range(2)]
    tmp8_ = [sb(rph, f"tmp8{i}", [128, 8], F32) for i in range(2)]
    maskT = sb(rph, "maskT", [8, NT_ALL * 128], F32)
    exclT = sb(rph, "exclT", [8, NT_ALL * 128], F32)
    zer8 = sb(rph, "zer8", [8, NT_ALL * 128], F32)
    rloc_all = sb(rph, "rloc_all", [128, NT_ALL], F32)
    p.do("vector", lambda v: v.memset(zer8[:], 1.0), reads=[], writes=[zer8], unsafe=True)

    def r_head(tt):
        X = xt[tt % 2]
        ut, ub, sqs, stat = ut_[tt % 2], ub_[tt % 2], sqs_[tt % 2], stat_[tt % 2]
        p.do("sync", lambda s, tt=tt, X=X: s.dma_start(
            out=X[:], in_=x_ap[tt * 128:(tt + 1) * 128, :]), reads=[x_ap], writes=[X], dma=True, tag="ldx")
        emit_ln_tile(p, X, ut, stat, sqs)
        p.do("scalar", lambda s, ut=ut, ub=ub: s.activation(out=ub[:], in_=ut[:], func=AF.Identity),
             reads=[ut], writes=[ub], tag="ub")

    r_head(0)
    for tt in range(NT_ALL):
        ut, ub, sqs, stat = ut_[tt % 2], ub_[tt % 2], sqs_[tt % 2], stat_[tt % 2]
        mask, et, tmp8 = mask_[tt % 2], et_[tt % 2], tmp8_[tt % 2]
        if tt + 1 < NT_ALL:
            r_head(tt + 1)
        for c in range(NC):
            ps7 = psb[6 + c % 2]
            ufc = ufc_[c % 2]
            p.do("tensor", lambda t_, c=c, ub=ub, ps7=ps7: t_.transpose(
                ps7.bitcast(BF16)[:, 0:128], ub[:, c * 128:(c + 1) * 128], iden_b[:]),
                reads=[ub, iden_b], writes=[ps7], tag="rtr")
            p.do("vector", lambda v, c=c, ps7=ps7, ufc=ufc: v.tensor_copy(out=ufc[:], in_=ps7.bitcast(BF16)[:, 0:128]),
                 reads=[ps7], writes=[ufc], tag="rev")
            p.do("tensor", lambda t_, c=c, ufc=ufc: t_.matmul(
                psb[5][:, 0:8], ufc[:], wgc[c][:], start=(c == 0), stop=(c == NC - 1)),
                reads=[ufc, wgc[c]], writes=[psb[5]], tag="rmm")
        p.do("vector", lambda v, stat=stat: v.tensor_reduce(out=stat[:, 0:1], in_=psb[5][:, 0:8], axis=AX.X,
                                                 op=ALU.max), reads=[psb[5]], writes=[stat], tag="lmax")
        p.do("vector", lambda v, stat=stat: v.tensor_scalar(out=stat[:, 1:2], in0=stat[:, 0:1], scalar1=-1.0,
                                                 scalar2=None, op0=ALU.mult), reads=[stat], writes=[stat])
        p.do("scalar", lambda s, stat=stat, et=et: s.activation(out=et[:], in_=psb[5][:, 0:8], func=AF.Exp,
                                              bias=stat[:, 1:2]), reads=[psb[5], stat], writes=[et], tag="et")
        p.do("vector", lambda v, stat=stat, et=et: v.tensor_reduce(out=stat[:, 2:3], in_=et[:], axis=AX.X, op=ALU.add),
             reads=[et], writes=[stat])
        p.do("vector", lambda v, tt=tt, stat=stat: v.reciprocal(out=cw_all[:, tt:tt + 1], in_=stat[:, 2:3]),
             reads=[stat], writes=[cw_all], tag="cw")
        p.do("vector", lambda v, mask=mask, stat=stat: v.tensor_scalar(out=mask[:], in0=psb[5][:, 0:8], scalar1=stat[:, 0:1],
                                                 scalar2=None, op0=ALU.is_ge),
             reads=[psb[5], stat], writes=[mask], tag="mge")
        p.do("vector", lambda v, mask=mask, tmp8=tmp8: v.scalar_tensor_tensor(out=tmp8[:], in0=mask[:], scalar=1.0,
                                                        in1=io8[:], op0=ALU.mult, op1=ALU.mult),
             reads=[mask, io8], writes=[tmp8])
        p.do("vector", lambda v, tmp8=tmp8: v.tensor_scalar(out=tmp8[:], in0=tmp8[:], scalar1=8.0, scalar2=None,
                                                 op0=ALU.add), reads=[tmp8], writes=[tmp8])
        p.do("vector", lambda v, tt=tt, tmp8=tmp8: v.tensor_reduce(out=t1_all[:, tt:tt + 1], in_=tmp8[:],
                                                        axis=AX.X, op=ALU.min),
             reads=[tmp8], writes=[t1_all], tag="top1")
        p.do("vector", lambda v, mask=mask: v.tensor_scalar(out=mask[:], in0=io8[:], scalar1=8.0, scalar2=None,
                                                 op0=ALU.add), reads=[io8], writes=[mask])
        p.do("vector", lambda v, tt=tt, mask=mask: v.tensor_scalar(
            out=mask[:], in0=mask[:], scalar1=t1_all[:, tt:tt + 1], scalar2=None, op0=ALU.is_equal),
            reads=[mask, t1_all], writes=[mask], tag="onehot")
        ps7 = psb[4]
        p.do("tensor", lambda t_, tt=tt, mask=mask, ps7=ps7: t_.transpose(ps7[0:8, 0:128], mask[:], iden_f[:]),
             reads=[mask, iden_f], writes=[ps7], tag="mtr")
        p.do("vector", lambda v, tt=tt, ps7=ps7: v.tensor_copy(out=maskT[:, tt * 128:(tt + 1) * 128],
                                                      in_=ps7[0:8, 0:128]),
             reads=[ps7], writes=[maskT], tag="mtev")
        p.do("sync", lambda s, tt=tt, ub=ub: s.dma_start(
            out=ustage[tt * 128:(tt + 1) * 128, :], in_=ub[:]),
            reads=[ub], writes=[ustage], dma=True, tag="ust")
    p.do("vector", lambda v: v.tensor_tensor_scan(out=exclT[:], data0=zer8[:], data1=maskT[:],
                                                  initial=0.0, op0=ALU.mult, op1=ALU.add),
         reads=[zer8, maskT], writes=[exclT], tag="prefix")
    p.do("vector", lambda v: v.tensor_tensor(out=exclT[:], in0=exclT[:], in1=maskT[:],
                                             op=ALU.subtract), reads=[exclT, maskT], writes=[exclT])
    cntrow = sb(rph, "cntrow", [1, 8], F32)
    p.do("tensor", lambda t_: t_.transpose(psb[7][0:1, 0:8],
                                           exclT[:, NT_ALL * 128 - 1:NT_ALL * 128], iden_f[0:8, 0:8]),
         reads=[exclT, iden_f], writes=[psb[7]], tag="ctr")
    p.do("tensor", lambda t_: t_.transpose(psb[6][0:1, 0:8],
                                           maskT[:, NT_ALL * 128 - 1:NT_ALL * 128], iden_f[0:8, 0:8]),
         reads=[maskT, iden_f], writes=[psb[6]], tag="ctr2")
    p.do("vector", lambda v: v.tensor_copy(out=cntrow[:], in_=psb[7][0:1, 0:8]),
         reads=[psb[7]], writes=[cntrow], tag="cntc")
    p.do("vector", lambda v: v.tensor_tensor(out=cntrow[:], in0=cntrow[:],
                                             in1=psb[6][0:1, 0:8], op=ALU.add),
         reads=[cntrow, psb[6]], writes=[cntrow], tag="cnt")
    p.do("sync", lambda s: s.dma_start(out=cnt_in, in_=cntrow[:]),
         reads=[cntrow], writes=[cnt_in], dma=True, tag="cntst")
    cc = p.do("gpsimd", lambda g: g.collective_compute(
        "AllGather", ALU.bypass, replica_groups=[list(range(8))],
        ins=[cnt_in.opt()], outs=[cnt_out.opt()]),
        reads=[cnt_in], writes=[cnt_out], tag="agcnt")
    cc.amount = 1
    cc.sem_key = "cc_cnt"

    def counts_math(st_, sfx):
        cnts = sb(st_, "cnts" + sfx, [8, 8], F32)
        p.do("sync", lambda s: s.dma_start(out=cnts[:], in_=cnt_out),
             reads=[cnt_out], writes=[cnts], dma=True, tag="cld" + sfx)
        offs = sb(st_, "offs" + sfx, [8, 8], F32)
        p.do("tensor", lambda t_: t_.matmul(psb[6][0:8, 0:8], ST8[:], cnts[:], start=True, stop=True),
             reads=[ST8, cnts], writes=[psb[6]], tag="offs" + sfx)
        p.do("vector", lambda v: v.tensor_copy(out=offs[:], in_=psb[6][0:8, 0:8]),
             reads=[psb[6]], writes=[offs])
        lbh = sb(st_, "lbh" + sfx, [8, 8], F32)
        zsm = sb(st_, "zsm" + sfx, [8, 8], F32)
        p.do("vector", lambda v: v.memset(zsm[:], 1.0), reads=[], writes=[zsm], unsafe=True)
        p.do("vector", lambda v: v.tensor_tensor_scan(out=lbh[:], data0=zsm[:], data1=cnts[:],
                                                      initial=0.0, op0=ALU.mult, op1=ALU.add),
             reads=[zsm, cnts], writes=[lbh], tag="lbh" + sfx)
        p.do("vector", lambda v: v.tensor_tensor(out=lbh[:], in0=lbh[:], in1=cnts[:], op=ALU.subtract),
             reads=[lbh, cnts], writes=[lbh])
        return cnts, offs, lbh

    def myrow(st_, mat, name):
        out = sb(st_, name, [1, 8], F32)
        p.do("tensor", lambda t_, mat=mat: t_.matmul(psb[5][0:1, 0:8], my1[:], mat[:],
                                                     start=True, stop=True),
             reads=[my1, mat], writes=[psb[5]], tag=name)
        p.do("vector", lambda v: v.tensor_copy(out=out[:], in_=psb[5][0:1, 0:8]),
             reads=[psb[5]], writes=[out])
        return out

    def bcast_row(st_, row, name):
        out = sb(st_, name, [128, 8], F32)
        p.do("tensor", lambda t_, row=row: t_.matmul(psb[4][:, 0:8], ones_row_f[:], row[:],
                                                     start=True, stop=True),
             reads=[ones_row_f, row], writes=[psb[4]], tag=name)
        p.do("vector", lambda v: v.tensor_copy(out=out[:], in_=psb[4][:, 0:8]),
             reads=[psb[4]], writes=[out])
        return out

    cnts, offs, lbh = counts_math(rph, "r")
    myoffB = bcast_row(rph, myrow(rph, offs, "myoff"), "myoffB")
    mylbB = bcast_row(rph, myrow(rph, lbh, "mylb"), "mylbB")
    mask_all = sb(rph, "mask_all", [128, 8 * NT_ALL], F32)
    exm_all = sb(rph, "exm_all", [128, 8 * NT_ALL], F32)
    tmp_all = sb(rph, "tmp_all", [128, 8 * NT_ALL], F32)
    m8 = sb(rph, "m8", [128, 8], F32)
    posf_all = sb(rph, "posf_all", [128, NT_ALL], F32)
    posu_all = sb(rph, "posu_all", [128, NT_ALL], U32)
    ubs_ = [sb(rph, f"ubs{i}", [128, D], BF16) for i in range(2)]
    M3 = mask_all[:].rearrange("p (t e) -> p t e", e=8)
    E3 = exm_all[:].rearrange("p (t e) -> p t e", e=8)
    T3 = tmp_all[:].rearrange("p (t e) -> p t e", e=8)
    p.do("vector", lambda v: v.tensor_scalar(out=m8[:], in0=io8[:], scalar1=8.0, scalar2=None,
                                             op0=ALU.add), reads=[io8], writes=[m8])
    p.do("vector", lambda v: v.tensor_copy(
        out=mask_all[:].rearrange("p (e t) -> p e t", t=NT_ALL),
        in_=m8[:].unsqueeze(2).broadcast_to([128, 8, NT_ALL])), reads=[m8], writes=[mask_all], tag="m8rep")
    p.do("vector", lambda v: v.tensor_copy(
        out=tmp_all[:].rearrange("p (e t) -> p e t", t=NT_ALL),
        in_=mask_all[:].rearrange("p (t e) -> p t e", e=8).transpose_last2()
        if False else mask_all[:].rearrange("p (e t) -> p e t", t=NT_ALL)),
        reads=[mask_all], writes=[tmp_all])
    p.do("vector", lambda v: v.tensor_tensor(
        out=M3, in0=tmp_all[:].rearrange("p (e t) -> p t e", t=NT_ALL),
        in1=t1_all[:].unsqueeze(2).broadcast_to([128, NT_ALL, 8]), op=ALU.is_equal),
        reads=[tmp_all, t1_all], writes=[mask_all], tag="ohw")
    for tt in range(NT_ALL):
        ps7 = psb[6 + tt % 2]
        p.do("tensor", lambda t_, tt=tt, ps7=ps7: t_.transpose(
            ps7[0:128, 0:8], exclT[:, tt * 128:(tt + 1) * 128], iden_f[0:8, 0:8]),
            reads=[exclT, iden_f], writes=[ps7], tag="extr")
        p.do("vector", lambda v, tt=tt, ps7=ps7: v.tensor_copy(
            out=exm_all[:, tt * 8:(tt + 1) * 8], in_=ps7[0:128, 0:8]),
            reads=[ps7], writes=[exm_all], tag="exev")
    p.do("vector", lambda v: v.tensor_tensor(out=exm_all[:], in0=exm_all[:], in1=mask_all[:],
         op=ALU.mult), reads=[exm_all, mask_all], writes=[exm_all])
    p.do("vector", lambda v: v.tensor_reduce(out=rloc_all[:], in_=E3, axis=AX.X, op=ALU.add),
         reads=[exm_all], writes=[rloc_all], tag="rloc")
    p.do("vector", lambda v: v.tensor_tensor(
        out=T3, in0=M3, in1=myoffB[:].unsqueeze(1).broadcast_to([128, NT_ALL, 8]), op=ALU.mult),
        reads=[mask_all, myoffB], writes=[tmp_all])
    p.do("vector", lambda v: v.tensor_reduce(out=rk_all[:], in_=T3, axis=AX.X, op=ALU.add),
         reads=[tmp_all], writes=[rk_all])
    p.do("vector", lambda v: v.tensor_tensor(out=rk_all[:], in0=rk_all[:], in1=rloc_all[:],
         op=ALU.add), reads=[rk_all, rloc_all], writes=[rk_all], tag="rank")
    p.do("vector", lambda v: v.tensor_tensor(
        out=T3, in0=M3, in1=mylbB[:].unsqueeze(1).broadcast_to([128, NT_ALL, 8]), op=ALU.mult),
        reads=[mask_all, mylbB], writes=[tmp_all])
    p.do("vector", lambda v: v.tensor_reduce(out=posf_all[:], in_=T3, axis=AX.X, op=ALU.add),
         reads=[tmp_all], writes=[posf_all])
    p.do("vector", lambda v: v.tensor_tensor(out=posf_all[:], in0=posf_all[:], in1=rloc_all[:],
         op=ALU.add), reads=[posf_all, rloc_all], writes=[posf_all], tag="pos")
    p.do("vector", lambda v: v.tensor_copy(out=posu_all[:], in_=posf_all[:]),
         reads=[posf_all], writes=[posu_all])
    for tt in range(NT_ALL):
        ubs = ubs_[tt % 2]
        p.do("sync", lambda s, tt=tt, ubs=ubs: s.dma_start(
            out=ubs[:], in_=ustage[tt * 128:(tt + 1) * 128, :]),
            reads=[ustage], writes=[ubs], dma=True, tag="uld")
        p.do("gpsimd", lambda g, tt=tt, ubs=ubs: g.indirect_dma_start(
            out=lsort, out_offset=bass.IndirectOffsetOnAxis(ap=posu_all[:, tt:tt + 1], axis=0),
            in_=ubs[:], in_offset=None), reads=[ubs, posu_all], writes=[lsort], dma=True, tag="scat")
    cc2 = p.do("gpsimd", lambda g: g.collective_compute(
        "AllGather", ALU.bypass, replica_groups=[list(range(8))],
        ins=[lsort.opt()], outs=[pool.opt()]),
        reads=[lsort], writes=[pool], tag="agpool")
    cc2.amount = 1
    cc2.sem_key = "cc_pool"
    p.fence()
    rph.close()

    eph = ExitStack()
    W1c = [sb(eph, f"W1c{c}", [128, FEXP], BF16) for c in range(NC)]
    W2c = [sb(eph, f"W2c{c}", [128, D], BF16) for c in range(32)]
    for c in range(NC):
        p.do("gpsimd", lambda g, c=c: g.dma_start(out=W1c[c][:], in_=w["W1"][c * 128:(c + 1) * 128, :]),
             reads=[w["W1"]], writes=[W1c[c]], dma=True, tag="w1ld", skey="wexp")
    for c in range(32):
        p.do("gpsimd", lambda g, c=c: g.dma_start(out=W2c[c][:], in_=w["W2"][c * 128:(c + 1) * 128, :]),
             reads=[w["W2"]], writes=[W2c[c]], dma=True, tag="w2ld", skey="wexp")
    p.do("gpsimd", lambda g: g.engine_nop(), reads=[], writes=W1c + W2c, tag="wexpbar")
    b1c = sb(eph, "b1c", [128, 32], F32)
    b2r = sb(eph, "b2r", [1, D], BF16)
    ones_row_bb = sb(eph, "ones_row_bb", [1, 128], BF16)
    for dst, src_ in ((b1c, "b1_cols"), (b2r, "b2_row"), (ones_row_bb, "ones_row")):
        p.do("gpsimd", lambda g, dst=dst, src_=src_: g.dma_start(out=dst[:], in_=w[src_]),
             reads=[w[src_]], writes=[dst], dma=True)
    iota_r = sb(eph, "iota_r", [128, RTILES], I32)
    iota_f = sb(eph, "iota_f", [128, RTILES], F32)
    p.do("gpsimd", lambda g: g.iota(iota_r[:], pattern=[[128, RTILES]], base=0, channel_multiplier=1),
         reads=[], writes=[iota_r], unsafe=True, tag="iotar")
    p.do("vector", lambda v: v.tensor_copy(out=iota_f[:], in_=iota_r[:]),
         reads=[iota_r], writes=[iota_f])
    cnts2, offs2, lbh2 = counts_math(eph, "e")
    cols3 = sb(eph, "cols3", [8, 3], F32)
    matT = sb(eph, "matT", [8, 8], F32)
    for j, mat in enumerate((offs2, lbh2, cnts2)):
        p.do("tensor", lambda t_, mat=mat: t_.transpose(psb[5][0:8, 0:8], mat[:], iden_f[0:8, 0:8]),
             reads=[mat, iden_f], writes=[psb[5]], tag="colT")
        p.do("vector", lambda v: v.tensor_copy(out=matT[:], in_=psb[5][0:8, 0:8]),
             reads=[psb[5]], writes=[matT])
        p.do("tensor", lambda t_, j=j: t_.matmul(psb[4][0:8, j:j + 1], matT[:], my1[:],
                                                 start=True, stop=True),
             reads=[matT, my1], writes=[psb[4]], tag="colmm")
        p.do("vector", lambda v, j=j: v.tensor_copy(out=cols3[:, j:j + 1], in_=psb[4][0:8, j:j + 1]),
             reads=[psb[4]], writes=[cols3])
    idxf = sb(eph, "idxf", [128, RTILES], F32)
    idxu = sb(eph, "idxu", [128, RTILES], U32)
    bci = sb(eph, "bci", [128, 3], F32)
    segs = sb(eph, "segs", [128, RTILES], F32)
    segv = sb(eph, "segv", [128, RTILES], F32)
    p.do("vector", lambda v: v.memset(idxf[:], 0.0), reads=[], writes=[idxf], unsafe=True)
    for i in range(8):
        p.do("tensor", lambda t_, i=i: t_.matmul(psb[3][:, 0:3], rep8[:, i * 128:(i + 1) * 128],
                                                 cols3[:], start=True, stop=True),
             reads=[rep8, cols3], writes=[psb[3]], tag="bci")
        p.do("vector", lambda v: v.tensor_copy(out=bci[:], in_=psb[3][:, 0:3]),
             reads=[psb[3]], writes=[bci])
        p.do("vector", lambda v: v.tensor_tensor(out=bci[:, 2:3], in0=bci[:, 0:1], in1=bci[:, 2:3],
                                                 op=ALU.add), reads=[bci], writes=[bci], tag="end")
        p.do("vector", lambda v, i=i: v.scalar_tensor_tensor(
            out=bci[:, 1:2], in0=bci[:, 1:2], scalar=float(i * 4096), in1=bci[:, 0:1],
            op0=ALU.add, op1=ALU.subtract), reads=[bci], writes=[bci], tag="gval")
        p.do("vector", lambda v: v.tensor_scalar(out=segs[:], in0=iota_f[:], scalar1=bci[:, 0:1],
                                                 scalar2=None, op0=ALU.is_ge),
             reads=[iota_f, bci], writes=[segs])
        p.do("vector", lambda v: v.tensor_scalar(out=segv[:], in0=iota_f[:], scalar1=bci[:, 2:3],
                                                 scalar2=None, op0=ALU.is_lt),
             reads=[iota_f, bci], writes=[segv])
        p.do("vector", lambda v: v.tensor_tensor(out=segs[:], in0=segs[:], in1=segv[:], op=ALU.mult),
             reads=[segs, segv], writes=[segs])
        p.do("vector", lambda v: v.tensor_scalar(out=segv[:], in0=iota_f[:], scalar1=bci[:, 1:2],
                                                 scalar2=None, op0=ALU.add),
             reads=[iota_f, bci], writes=[segv])
        p.do("vector", lambda v: v.scalar_tensor_tensor(out=segv[:], in0=segs[:], scalar=1.0,
                                                        in1=segv[:], op0=ALU.mult, op1=ALU.mult),
             reads=[segs, segv], writes=[segv])
        p.do("vector", lambda v: v.tensor_tensor(out=idxf[:], in0=idxf[:], in1=segv[:], op=ALU.add),
             reads=[idxf, segv], writes=[idxf], tag="idxacc")
    p.do("vector", lambda v: v.tensor_copy(out=idxu[:], in_=idxf[:]),
         reads=[idxf], writes=[idxu], tag="idxu")
    ypool_half = [io["ypool"], io["ypool2"]]
    zrow = sb(eph, "zrow", [1, D], BF16)
    p.do("vector", lambda v: v.memset(zrow[:], 0.0), reads=[], writes=[zrow], unsafe=True)
    p.do("sync", lambda s: s.dma_start(
        out=ypool_half[1][8 * (RTILES * 128 - 2624):8 * (RTILES * 128 - 2624) + 1, :], in_=zrow[:]),
        reads=[zrow], writes=[ypool_half[1]], dma=True, tag="zsent")
    NST = 11
    xrow = [sb(eph, f"xrow{i}", [128, D], BF16) for i in range(2)]
    xfm = [[sb(eph, f"xfm{i}_{c}", [128, 512], BF16) for c in range(NC)] for i in range(2)]
    hfm = [sb(eph, f"hfm{c}", [128, 512], BF16) for c in range(32)]
    yrow = [sb(eph, f"yrow{i}", [128, D], BF16) for i in range(2)]
    ypool_half = [io["ypool"], io["ypool2"]]
    for stt_ in range(NST):
        r0 = stt_ * 4
        nrt = min(4, RTILES - r0)
        XF = xfm[stt_ % 2]
        for j in range(nrt):
            rt = r0 + j
            XR = xrow[rt % 2]
            p.do("gpsimd", lambda g, rt=rt, XR=XR: g.indirect_dma_start(
                out=XR[:], out_offset=None, in_=pool,
                in_offset=bass.IndirectOffsetOnAxis(ap=idxu[:, rt:rt + 1], axis=0)),
                reads=[pool, idxu], writes=[XR], dma=True, tag="xgat")
            for c in range(NC):
                ps = psb[6 + (j * NC + c) % 2]
                p.do("tensor", lambda t_, c=c, XR=XR, ps=ps: t_.transpose(
                    ps.bitcast(BF16)[:, 0:128], XR[:, c * 128:(c + 1) * 128], iden_b[:]),
                    reads=[XR, iden_b], writes=[ps], tag="xtr")
                p.do("vector", lambda v, c=c, j=j, ps=ps, XF=XF: v.tensor_copy(
                    out=XF[c][:, j * 128:(j + 1) * 128], in_=ps.bitcast(BF16)[:, 0:128]),
                    reads=[ps], writes=[XF[c]], tag="xev")
        nr = nrt * 128
        for fc in range(32):
            ps = psb[fc % 2]
            for c in range(NC):
                p.do("tensor", lambda t_, c=c, fc=fc, ps=ps, XF=XF, nr=nr: t_.matmul(
                    ps[:, 0:nr], W1c[c][:, fc * 128:(fc + 1) * 128], XF[c][:, 0:nr],
                    start=(c == 0), stop=(c == NC - 1)), reads=[W1c[c], XF[c]], writes=[ps], tag="h1")
            p.do("scalar", lambda s, fc=fc, ps=ps, nr=nr: s.activation(
                out=hfm[fc][:, 0:nr], in_=ps[:, 0:nr], func=AF.Gelu, bias=b1c[:, fc:fc + 1]),
                reads=[ps, b1c], writes=[hfm[fc]], tag="gelu")
        for j in range(nrt):
            rt = r0 + j
            YR = yrow[rt % 2]
            for oc in range(2):
                ps = psb[4 + oc]
                for fc in range(32):
                    p.do("tensor", lambda t_, fc=fc, oc=oc, ps=ps, j=j: t_.matmul(
                        ps[:], hfm[fc][:, j * 128:(j + 1) * 128], W2c[fc][:, oc * 512:(oc + 1) * 512],
                        start=(fc == 0), stop=False), reads=[hfm[fc], W2c[fc]], writes=[ps], tag="h2")
                p.do("tensor", lambda t_, oc=oc, ps=ps: t_.matmul(
                    ps[:], ones_row_bb[:], b2r[:, oc * 512:(oc + 1) * 512],
                    start=False, stop=True), reads=[ones_row_bb, b2r], writes=[ps], tag="b2")
            p.do("vector", lambda v, YR=YR: v.tensor_copy(out=YR[:, 0:512], in_=psb[4][:]),
                 reads=[psb[4]], writes=[YR], tag="yev0")
            p.do("scalar", lambda s, YR=YR: s.copy(out=YR[:, 512:1024], in_=psb[5][:]),
                 reads=[psb[5]], writes=[YR], tag="yev1")
            p.do("sync", lambda s, rt=rt, YR=YR: s.dma_start(
                out=ybuf[rt * 128:(rt + 1) * 128, :], in_=YR[:]),
                reads=[YR], writes=[ybuf], dma=True, tag="yst")
        if stt_ == 4:
            pass
        if stt_ == 5:
            cc3a = p.do("gpsimd", lambda g: g.collective_compute(
                "AllGather", ALU.bypass, replica_groups=[list(range(8))],
                ins=[ybuf[0:2624, :].opt()], outs=[ypool_half[0].opt()]),
                reads=[ybuf], writes=[ypool_half[0]], tag="agy1")
            cc3a.amount = 1
            cc3a.sem_key = "cc_y1"
    cc3b = p.do("gpsimd", lambda g: g.collective_compute(
        "AllGather", ALU.bypass, replica_groups=[list(range(8))],
        ins=[ybuf[2624:RTILES * 128, :].opt()], outs=[ypool_half[1][0:8 * (RTILES * 128 - 2624), :].opt()]),
        reads=[ybuf], writes=[ypool_half[1]], tag="agy2")
    cc3b.amount = 1
    cc3b.sem_key = "cc_y2"
    p.fence()
    eph.close()

    HROWS = 2624
    fph = ExitStack()
    xt2 = [sb(fph, f"x2t{i}", [128, D], F32) for i in range(2)]
    ygat = [sb(fph, f"ygat{i}", [128, D], BF16) for i in range(2)]
    rf1 = sb(fph, "rf1", [128, NT_ALL], F32)
    rf2 = sb(fph, "rf2", [128, NT_ALL], F32)
    ovA = sb(fph, "ovA", [128, NT_ALL], F32)
    ruA = sb(fph, "ruA", [128, 2 * NT_ALL], U32)
    ot_ = [sb(fph, f"fot{i}", [128, D], F32) for i in range(2)]
    p.do("vector", lambda v: v.scalar_tensor_tensor(
        out=rf1[:], in0=t1_all[:], scalar=float(HROWS), in1=rk_all[:],
        op0=ALU.mult, op1=ALU.add), reads=[t1_all, rk_all], writes=[rf1], tag="rid1")
    p.do("vector", lambda v: v.tensor_scalar(
        out=ovA[:], in0=rk_all[:], scalar1=float(HROWS) - 0.5, scalar2=1e9,
        op0=ALU.is_gt, op1=ALU.mult), reads=[rk_all], writes=[ovA], tag="ovr1")
    p.do("vector", lambda v: v.tensor_tensor(out=rf1[:], in0=rf1[:], in1=ovA[:],
         op=ALU.add), reads=[rf1, ovA], writes=[rf1])
    p.do("vector", lambda v: v.tensor_copy(out=ruA[:, 0:NT_ALL], in_=rf1[:]),
         reads=[rf1], writes=[ruA])
    p.do("vector", lambda v: v.scalar_tensor_tensor(
        out=rf2[:], in0=t1_all[:], scalar=float(RTILES * 128 - HROWS), in1=rk_all[:],
        op0=ALU.mult, op1=ALU.add), reads=[t1_all, rk_all], writes=[rf2], tag="rid2")
    p.do("vector", lambda v: v.tensor_scalar(
        out=rf2[:], in0=rf2[:], scalar1=float(HROWS), scalar2=None, op0=ALU.subtract),
        reads=[rf2], writes=[rf2])
    p.do("vector", lambda v: v.tensor_scalar(
        out=ovA[:], in0=rk_all[:], scalar1=float(HROWS) - 0.5, scalar2=1e9,
        op0=ALU.is_le, op1=ALU.mult), reads=[rk_all], writes=[ovA], tag="ovr2a")
    p.do("vector", lambda v: v.tensor_tensor(out=rf2[:], in0=rf2[:], in1=ovA[:],
         op=ALU.add), reads=[rf2, ovA], writes=[rf2])
    p.do("vector", lambda v: v.tensor_scalar(
        out=ovA[:], in0=rk_all[:], scalar1=float(CAP) - 0.5, scalar2=1e9,
        op0=ALU.is_gt, op1=ALU.mult), reads=[rk_all], writes=[ovA], tag="ovr2b")
    p.do("vector", lambda v: v.tensor_tensor(out=rf2[:], in0=rf2[:], in1=ovA[:],
         op=ALU.add), reads=[rf2, ovA], writes=[rf2])
    p.do("vector", lambda v: v.tensor_scalar(
        out=rf2[:], in0=rf2[:], scalar1=float(8 * (RTILES * 128 - HROWS)), scalar2=None,
        op0=ALU.min), reads=[rf2], writes=[rf2], tag="clmp")
    p.do("vector", lambda v: v.tensor_copy(out=ruA[:, NT_ALL:2 * NT_ALL], in_=rf2[:]),
         reads=[rf2], writes=[ruA])
    final_stores = []
    for tt in range(NT_ALL):
        i2 = tt % 2
        X, YG, OT = xt2[i2], ygat[i2], ot_[i2]
        p.do("sync", lambda s, tt=tt, X=X: s.dma_start(
            out=X[:], in_=x_ap[tt * 128:(tt + 1) * 128, :]), reads=[x_ap], writes=[X], dma=True, tag="ldx2")
        p.do("gpsimd", lambda g, tt=tt, YG=YG: g.indirect_dma_start(
            out=YG[:], out_offset=None, in_=ypool_half[1],
            in_offset=bass.IndirectOffsetOnAxis(ap=ruA[:, NT_ALL + tt:NT_ALL + tt + 1], axis=0)),
            reads=[ypool_half[1], ruA], writes=[YG], dma=True, tag="ygat2")
        p.do("gpsimd", lambda g, tt=tt, YG=YG: g.indirect_dma_start(
            out=YG[:], out_offset=None, in_=ypool_half[0],
            in_offset=bass.IndirectOffsetOnAxis(ap=ruA[:, tt:tt + 1], axis=0),
            bounds_check=8 * HROWS - 1, oob_is_err=False),
            reads=[ypool_half[0], ruA], writes=[YG], dma=True, tag="ygat1")
        p.do("vector", lambda v, tt=tt, X=X, YG=YG, OT=OT: v.scalar_tensor_tensor(
            out=OT[:], in0=YG[:], scalar=cw_all[:, tt:tt + 1], in1=X[:],
            op0=ALU.mult, op1=ALU.add), reads=[YG, cw_all, X], writes=[OT], tag="comb")
        stx = p.do("sync", lambda s, tt=tt, OT=OT: s.dma_start(
            out=out_ap[tt * 128:(tt + 1) * 128, :], in_=OT[:]),
            reads=[OT], writes=[out_ap], dma=True, tag="stx")
        final_stores.append(stx)
    p.fence()
    fph.close()
    return final_stores


def make_launch_b():
    nc = bass.Bass("TRN2", target_bir_lowering=False)
    st = ExitStack()
    io = {}
    io["x"] = nc.dram_tensor("x", [B_PER_CORE * T, D], F32, kind="ExternalInput").ap()
    io["out"] = nc.dram_tensor("out", [B_PER_CORE * T, D], F32, kind="ExternalOutput").ap()
    for name, shape in [("wg", [D, 8]), ("W1", [D, FEXP]), ("W2", [FEXP, D]),
                        ("b1_cols", [128, 32]), ("b2_row", [1, D]),
                        ("my1_col", [8, 1]), ("iota8m8", [128, 8]), ("st8", [8, 8]),
                        ("rep8", [8, 8 * 128])]:
        io[name] = nc.dram_tensor(name, shape, F32, kind="ExternalInput").ap()
    for name in ("iden_f", "cvals", "ones_row"):
        io[name] = nc.dram_tensor(name, list(HOST_CONST_SHAPES[name]), F32, kind="ExternalInput").ap()
    io["lsort"] = nc.dram_tensor("lsort", [B_PER_CORE * T, D], BF16).ap()
    io["pool"] = nc.dram_tensor("pool", [8 * B_PER_CORE * T, D], BF16, addr_space="Shared").ap()
    io["cnt_in"] = nc.dram_tensor("cnt_in", [1, 8], F32).ap()
    io["cnt_out"] = nc.dram_tensor("cnt_out", [8, 8], F32).ap()
    io["ybuf"] = nc.dram_tensor("ybuf", [RTILES * 128, D], BF16).ap()
    io["ypool"] = nc.dram_tensor("ypool", [8 * 2624, D], BF16, addr_space="Shared").ap()
    io["ypool2"] = nc.dram_tensor("ypool2", [8 * (RTILES * 128 - 2624) + 1, D], BF16, addr_space="Shared").ap()
    io["ustage"] = nc.dram_tensor("ustage", [B_PER_CORE * T, D], BF16).ap()
    io["dbg"] = nc.dram_tensor("dbg", [128, 104 + RTILES], F32, kind="ExternalOutput").ap()
    p = Prog(nc)
    with st:
        finals = build_l0_ffn(p, st, io)
        p.do("gpsimd", lambda g: g.engine_nop(), deps=finals, tag="fence")
        with nc.Block() as block:
            p.emit(block, st)
    return nc


def l0_ffn_maps(inputs, x_full):
    consts = host_constants()
    xs = x_full.reshape(16, T, D)
    in_maps = []
    io8 = np.broadcast_to(np.arange(8, dtype=np.float32) - 8.0, (128, 8)).copy()
    st8 = np.tril(np.ones((8, 8), np.float32), -1).T.copy()
    rep8 = np.zeros((8, 8 * 128), np.float32)
    for i in range(8):
        rep8[i, i * 128:(i + 1) * 128] = 1.0
    for c in range(NCORE):
        m = {"x": np.ascontiguousarray(xs[2 * c:2 * c + 2].reshape(B_PER_CORE * T, D))}
        m["wg"] = inputs["L0_ffn_wg"]
        m["W1"] = np.ascontiguousarray(inputs["L0_ffn_W1"][c])
        m["W2"] = np.ascontiguousarray(inputs["L0_ffn_W2"][c])
        m["b1_cols"] = np.ascontiguousarray(inputs["L0_ffn_b1"][c].reshape(32, 128).T)
        m["b2_row"] = np.ascontiguousarray(inputs["L0_ffn_b2"][c].reshape(1, D))
        m["my1_col"] = np.eye(8, dtype=np.float32)[:, c:c + 1].copy()
        m["iota8m8"] = io8
        m["st8"] = st8
        m["rep8"] = rep8
        for k in ("iden_f", "cvals", "ones_row"):
            m[k] = consts[k]
        in_maps.append(m)
    return in_maps



A_KEYS = ["qkv_w", "o_w", "in_w", "out_w", "Bp_w", "Cp_w", "dt_w", "gate_w",
          "n1_cols", "n2_cols", "A_cols", "dtb_col"]
B_KEYS = ["wg", "W1", "W2", "b1_cols", "b2_row", "my1_col", "iota8m8", "st8", "rep8"]
C_KEYS = ["Wq", "Wk", "Wv", "Wo", "bq_col", "bk_col", "bv_row", "bo_cols"]
D_KEYS = ["wg_rows", "wu_rows", "wd_rows"]
SHARED_KEYS = list(HOST_CONST_SHAPES)

A_SHAPES = {"qkv_w": [3072, D], "o_w": [D, D], "in_w": [64, D], "out_w": [D, 64],
            "Bp_w": [16, 64], "Cp_w": [16, 64], "dt_w": [64, 64], "gate_w": [64, D],
            "n1_cols": [128, 8], "n2_cols": [128, 8], "A_cols": [128, 8], "dtb_col": [64, 1]}
B_SHAPES = {"wg": [D, 8], "W1": [D, FEXP], "W2": [FEXP, D], "b1_cols": [128, 32],
            "b2_row": [1, D], "my1_col": [8, 1], "iota8m8": [128, 8], "st8": [8, 8],
            "rep8": [8, 8 * 128]}
C_SHAPES = {"Wq": [128, D], "Wk": [128, D], "Wv": [128, D], "Wo": [D, 128],
            "bq_col": [128, 1], "bk_col": [128, 1], "bv_row": [1, 128], "bo_cols": [128, 8]}
D_SHAPES = {"wg_rows": [WG_TPC * 128, D], "wu_rows": [WG_TPC * 128, D], "wd_rows": [128, I_FF]}


def make_fused():
    nc = bass.Bass("TRN2", target_bir_lowering=False)
    st = ExitStack()
    shared = {}
    for name in SHARED_KEYS:
        shared[name] = nc.dram_tensor(name, list(HOST_CONST_SHAPES[name]), F32,
                                      kind="ExternalInput").ap()
    x0 = nc.dram_tensor("x", [B_PER_CORE * T, D], F32, kind="ExternalInput").ap()
    out = nc.dram_tensor("out", [B_PER_CORE * T, D], F32, kind="ExternalOutput").ap()
    x1 = nc.dram_tensor("x1", [B_PER_CORE * T, D], F32).ap()
    x2 = nc.dram_tensor("x2", [B_PER_CORE * T, D], F32).ap()
    x3 = nc.dram_tensor("x3", [B_PER_CORE * T, D], F32).ap()

    def decl(prefix, shapes):
        d = {}
        for k, shp in shapes.items():
            d[k] = nc.dram_tensor(prefix + k, shp, F32, kind="ExternalInput").ap()
        return d

    io_a = decl("a_", A_SHAPES)
    io_b = decl("b_", B_SHAPES)
    io_c = decl("c_", C_SHAPES)
    io_d = decl("d_", D_SHAPES)
    io_b["lsort"] = nc.dram_tensor("b_lsort", [B_PER_CORE * T, D], BF16).ap()
    io_b["pool"] = nc.dram_tensor("b_pool", [8 * B_PER_CORE * T, D], BF16, addr_space="Shared").ap()
    io_b["cnt_in"] = nc.dram_tensor("b_cnt_in", [1, 8], F32).ap()
    io_b["cnt_out"] = nc.dram_tensor("b_cnt_out", [8, 8], F32).ap()
    io_b["ybuf"] = nc.dram_tensor("b_ybuf", [RTILES * 128, D], BF16).ap()
    io_b["ypool"] = nc.dram_tensor("b_ypool", [8 * 2624, D], BF16, addr_space="Shared").ap()
    io_b["ypool2"] = nc.dram_tensor("b_ypool2", [8 * (RTILES * 128 - 2624) + 1, D], BF16, addr_space="Shared").ap()
    io_b["ustage"] = nc.dram_tensor("b_ustage", [B_PER_CORE * T, D], BF16).ap()

    p = Prog(nc)
    with st:
        psb8 = [st.enter_context(nc.psum_tensor(f"psb{i}", [128, 512], F32)) for i in range(8)]
        psO = psb8[7][0:65, :]
        cvals_s = st.enter_context(nc.sbuf_tensor("cvals_s", [128, len(CONST_VALS)], F32))
        p.do("gpsimd", lambda g: g.dma_start(out=cvals_s[:], in_=shared["cvals"]),
             reads=[shared["cvals"]], writes=[cvals_s], dma=True, tag="cvals")
        for j, v in enumerate(CONST_VALS):
            nc.const_aps.register(F32, float(v), cvals_s[:, j:j + 1])
        p.fence()

        def run(builder, io_x, pfx, xin, xout):
            io_full = dict(shared)
            io_full.update(io_x)
            io_full["x"] = xin
            io_full["out"] = xout
            io_full["pfx"] = pfx
            io_full["psb"] = psb8
            io_full["psO"] = psO
            io_full["cvals_ready"] = True
            ph = ExitStack()
            finals = builder(p, ph, io_full)
            p.fence()
            ph.close()
            return finals

        io_dw = dict(shared)
        io_dw.update(io_d)
        io_dw["pfx"] = "d_"
        io_dw["psb"] = psb8
        io_dw["cvals_ready"] = True
        wq_st = ExitStack()
        io_d["wq_outs"] = build_l1_ffn_wquant(p, wq_st, io_dw)
        p.fence()
        wq_st.close()
        run(build_l0_attn, io_a, "a_", x0, x1)
        run(build_l0_ffn, io_b, "b_", x1, x2)
        run(build_l1_attn, io_c, "c_", x2, x3)
        finals = run(build_l1_ffn, io_d, "d_", x3, out)
        p.do("gpsimd", lambda g: g.engine_nop(), deps=finals, tag="final")
        with nc.Block() as block:
            p.emit(block, st)
    return nc


def fused_maps(inputs):
    ma = l0_attn_maps(inputs)
    mb = l0_ffn_maps(inputs, np.zeros((16, T, D), np.float32))
    mc = l1_attn_maps(inputs, np.zeros((16, T, D), np.float32))
    md = l1_ffn_maps(inputs, np.zeros((16, T, D), np.float32))
    consts = host_constants()
    out = []
    for c in range(NCORE):
        m = {}
        for k in SHARED_KEYS:
            m[k] = consts[k]
        m["x"] = ma[c]["x"]
        for k in A_KEYS:
            m["a_" + k] = ma[c][k]
        for k in B_KEYS:
            m["b_" + k] = mb[c][k]
        for k in C_KEYS:
            m["c_" + k] = mc[c][k]
        for k in D_KEYS:
            m["d_" + k] = md[c][k]
        out.append(m)
    return out


def kernel_fused(**inputs):
    nc = make_fused()
    res = run_bass_kernel_spmd(nc, fused_maps(inputs), core_ids=list(range(NCORE)))
    return _gather_out(res)


# version 30
# speedup vs baseline: 1.1403x; 1.1403x over previous
import numpy as np
import ml_dtypes
from contextlib import ExitStack
import concourse.bass as bass
import concourse.mybir as mybir
from concourse.bass_utils import run_bass_kernel_spmd

F32 = mybir.dt.float32
BF16 = mybir.dt.bfloat16
I32 = mybir.dt.int32
U32 = mybir.dt.uint32
AF = mybir.ActivationFunctionType
ALU = mybir.AluOpType
AX = mybir.AxisListType

PHASES = (1, 2, 3, 4)
NBATCH = 2
A2_LIMIT = 99
B_PER_CORE = 2
T = 2048
D = 1024
NCORE = 8
EPS_RMS = 1.1920929e-07
CONST_VALS = [0.0, 1.0, -1.0, 0.5, 1e-5, EPS_RMS, 240.0, 0.125, 1.0 / 1024.0, 339.41125496954285]


ENGINES = ("sync", "scalar", "vector", "tensor", "gpsimd")


def _is_psum(x):
    try:
        sp = x.space
        sp = sp() if callable(sp) else sp
        return "PSUM" in str(sp).upper()
    except Exception:
        return False


def _tname(x):
    if hasattr(x, "tensor"):
        return x.tensor.name
    if hasattr(x, "name"):
        return x.name
    return str(x)


class Op:
    __slots__ = ("engine", "fn", "deps", "amount", "signaled", "val", "unsafe", "tag", "sem_key")

    def __init__(self, engine, fn, amount, unsafe, tag):
        self.engine = engine
        self.fn = fn
        self.deps = []
        self.amount = amount
        self.signaled = False
        self.val = None
        self.unsafe = unsafe
        self.tag = tag
        self.sem_key = engine


class Prog:
    def __init__(self, nc):
        self.nc = nc
        self.streams = {e: [] for e in ENGINES}
        self.last_writer = {}
        self.readers = {}
        self.n_ops = 0
        self.fence_toks = []

    def fence(self):
        toks = {}
        for e in ENGINES:
            for o in self.streams[e]:
                if o.sem_key.startswith("cc_"):
                    continue
                toks[(e, o.sem_key)] = o
        self.fence_toks = list(toks.values())

    def do(self, engine, fn, reads=(), writes=(), deps=(), dma=False, unsafe=False, tag="",
           skey=None):
        o = Op(engine, fn, 16 if dma else 1, unsafe, tag)
        if dma:
            assert len(writes) >= 1
            if skey is None:
                skey = _tname(writes[0])
                for pre in ("a_", "b_", "c_", "d_"):
                    if skey.startswith(pre):
                        skey = skey[2:]
                        break
            o.sem_key = "dma_" + skey
            o.signaled = True
        deps = set(d for d in deps if d is not None)
        deps.update(self.fence_toks)
        for r in reads:
            if r is None:
                continue
            t = _tname(r)
            lw = self.last_writer.get(t)
            if lw is not None:
                deps.add(lw)
            if _is_psum(r):
                for rd in self.readers.get(t, ()):
                    if rd.engine != engine:
                        deps.add(rd)
        for w in writes:
            if w is None:
                continue
            t = _tname(w)
            lw = self.last_writer.get(t)
            if lw is not None:
                deps.add(lw)
            for rd in self.readers.get(t, ()):
                deps.add(rd)
        deps.discard(o)
        o.deps = [d for d in deps]
        for d in o.deps:
            d.signaled = True
        self.streams[engine].append(o)
        self.n_ops += 1
        for r in reads:
            if r is None:
                continue
            self.readers.setdefault(_tname(r), []).append(o)
        for w in writes:
            if w is None:
                continue
            t = _tname(w)
            self.last_writer[t] = o
            self.readers[t] = []
        return o

    def emit(self, block, stack):
        keys = set()
        counters = {}
        for e in ENGINES:
            for o in self.streams[e]:
                if o.signaled:
                    keys.add(o.sem_key)
                    c = counters.get(o.sem_key, 0) + o.amount
                    counters[o.sem_key] = c
                    o.val = c
        assert len(keys) <= 100, f"too many semaphores: {len(keys)}"
        sems = {k: stack.enter_context(self.nc.semaphore(f"s_{k}"[:40])) for k in sorted(keys)}

        def make_body(ename):
            ops = self.streams[ename]

            def body(eng):
                watermark = {}
                for o in ops:
                    need = {}
                    for d in o.deps:
                        if (d.engine == ename == "tensor" and not d.unsafe
                                and d.sem_key == ename):
                            continue
                        assert d.val is not None, f"dep of {o.tag} not signaled"
                        need[d.sem_key] = max(need.get(d.sem_key, 0), d.val)
                    for s, v in sorted(need.items()):
                        if watermark.get(s, 0) < v:
                            eng.wait_ge(sems[s], v)
                            watermark[s] = v
                    inst = o.fn(eng)
                    if o.signaled:
                        if inst is None:
                            eng.sem_inc(sems[o.sem_key], o.amount)
                        else:
                            inst.then_inc(sems[o.sem_key], o.amount)
            return body

        for ename in ENGINES:
            if self.streams[ename]:
                getattr(block, ename)(make_body(ename))



def host_constants():
    c = {}
    c["iden_f"] = np.eye(128, dtype=np.float32)
    mw = np.zeros((128, 896), np.float32)
    mw[:, 0:384] = -1e30
    mw[:, 384:512] = np.where(np.arange(128)[None, :] >= np.arange(128)[:, None], 0.0, -1e30)
    c["diagmask4"] = mw
    R = np.zeros((8, 64, 128), np.float32)
    for i in range(8):
        for p_ in range(128):
            R[i, 8 * i + p_ // 16, p_] = 1.0
    c["Rrep"] = np.concatenate([R[i] for i in range(8)], axis=1)
    RB = np.zeros((16, 128), np.float32)
    for p_ in range(128):
        RB[p_ % 16, p_] = 1.0
    c["RB"] = RB
    G = np.transpose(R, (0, 2, 1))
    c["Grep"] = np.concatenate([G[i] for i in range(8)], axis=1)
    c["ones_col"] = np.ones((128, 1), np.float32)
    c["ones_row"] = np.ones((1, 128), np.float32)
    c["cvals"] = np.broadcast_to(np.array(CONST_VALS, np.float32), (128, len(CONST_VALS))).copy()
    return c


HOST_CONST_SHAPES = {k: v.shape for k, v in host_constants().items()}



def prep_wT(p, st, W_ap, F, Din, iden_b, name, psA, psB):
    nc = p.nc
    nchunk = -(-Din // 128)
    wT = [st.enter_context(nc.sbuf_tensor(f"{name}_T{c}", [min(128, Din - 128 * c), F], BF16))
          for c in range(nchunk)]
    nrow = -(-F // 128)
    tmp = [st.enter_context(nc.sbuf_tensor(f"{name}_tmpp{i}", [min(128, F), Din], BF16)) for i in range(2)]
    for r in range(nrow):
        buf = tmp[r % 2]
        rows = min(128, F - 128 * r)
        p.do("gpsimd", lambda g, r=r, buf=buf, rows=rows: g.dma_start(
            out=buf[0:rows, :], in_=W_ap[r * 128:r * 128 + rows, :]),
            reads=[W_ap], writes=[buf], dma=True, tag=f"{name}ld")
        for c in range(nchunk):
            cols = min(128, Din - 128 * c)
            ps = psA if (r * nchunk + c) % 2 == 0 else psB
            p.do("tensor", lambda t, buf=buf, c=c, ps=ps, rows=rows, cols=cols: t.transpose(
                ps.bitcast(BF16)[0:cols, 0:rows], buf[0:rows, c * 128:c * 128 + cols], iden_b[:]),
                reads=[buf, iden_b], writes=[ps], tag=f"{name}tr")
            p.do("vector", lambda v, ps=ps, c=c, r=r, rows=rows, cols=cols: v.tensor_copy(
                out=wT[c][0:cols, r * 128:r * 128 + rows], in_=ps.bitcast(BF16)[0:cols, 0:rows]),
                reads=[ps], writes=[wT[c]], tag=f"{name}ev")
    return wT



def KK_A2L():
    return A2_LIMIT


def build_l0_attn(p, st, io):
    nc = p.nc
    pfx = io.get("pfx", "")
    NT, NC, NQC = T // 128, D // 128, T // 512
    x_ap, out_ap, w = io["x"], io["out"], io

    def sb(st_, name, shape, dt):
        return st_.enter_context(nc.sbuf_tensor(pfx + name, shape, dt))

    if not io.get("cvals_ready"):
        cvals_s = sb(st, "cvals_s", [128, len(CONST_VALS)], F32)
        p.do("gpsimd", lambda g: g.dma_start(out=cvals_s[:], in_=w["cvals"]),
             reads=[w["cvals"]], writes=[cvals_s], dma=True, tag="cvals")
        for j, v in enumerate(CONST_VALS):
            nc.const_aps.register(F32, float(v), cvals_s[:, j:j + 1])
        p.fence()
    iden_f = sb(st, "idenf_s", [128, 128], F32)
    iden_b = sb(st, "idenb_s", [128, 128], BF16)
    diagm = sb(st, "diagm_s", [128, 896], BF16)
    p.do("gpsimd", lambda g: g.dma_start(out=iden_f[:], in_=w["iden_f"]), reads=[w["iden_f"]], writes=[iden_f], dma=True)
    p.do("gpsimd", lambda g: g.dma_start(out=iden_b[:], in_=w["iden_f"]), reads=[w["iden_f"]], writes=[iden_b], dma=True)
    p.do("gpsimd", lambda g: g.dma_start(out=diagm[:], in_=w["diagmask4"]), reads=[w["diagmask4"]], writes=[diagm], dma=True)

    psb = io.get("psb") or [st.enter_context(nc.psum_tensor(f"psb{i}", [128, 512], F32)) for i in range(7)]
    psO = io.get("psO") or st.enter_context(nc.psum_tensor("psO", [65, 512], F32))
    psb = psb[:7]

    wtmp = [sb(st, f"wtmp{i}", [128, D], BF16) for i in range(2)]

    def prep(W_ap, F, Din, name, st_):
        nchunk = -(-Din // 128)
        wT = [sb(st_, f"{name}_T{c}", [min(128, Din - 128 * c), F], BF16) for c in range(nchunk)]
        nrow = -(-F // 128)
        for r in range(nrow):
            buf = wtmp[r % 2]
            rows = min(128, F - 128 * r)
            p.do("gpsimd", lambda g, r=r, buf=buf, rows=rows, W_ap=W_ap, Din=Din: g.dma_start(
                out=buf[0:rows, 0:Din], in_=W_ap[r * 128:r * 128 + rows, :]),
                reads=[W_ap], writes=[buf], dma=True, tag=f"{name}ld")
            for c in range(nchunk):
                cols = min(128, Din - 128 * c)
                ps = psb[(r * nchunk + c) % 2]
                p.do("tensor", lambda t, buf=buf, c=c, ps=ps, rows=rows, cols=cols: t.transpose(
                    ps.bitcast(BF16)[0:cols, 0:rows], buf[0:rows, c * 128:c * 128 + cols],
                    iden_b[0:rows, 0:rows]),
                    reads=[buf, iden_b], writes=[ps], tag=f"{name}tr")
                p.do("vector", lambda v, ps=ps, c=c, r=r, rows=rows, cols=cols, wt=wT[c]: v.tensor_copy(
                    out=wt[0:cols, r * 128:r * 128 + rows], in_=ps.bitcast(BF16)[0:cols, 0:rows]),
                    reads=[ps], writes=[wT[c]], tag=f"{name}ev")
        return wT

    inT = prep(w["in_w"], 64, D, "inw", st)
    gaT = prep(w["gate_w"], 64, D, "gaw", st)
    ouT = prep(w["out_w"], D, 64, "ouw", st)
    dtT = prep(w["dt_w"], 64, 64, "dtw", st)
    bpT = prep(w["Bp_w"], 16, 64, "bpw", st)
    cpT = prep(w["Cp_w"], 16, 64, "cpw", st)

    n1c = sb(st, "n1c", [128, NC], F32)
    n2c = sb(st, "n2c", [128, NC], F32)
    dtb = sb(st, "dtb", [64, 1], F32)
    negA = sb(st, "negA", [128, 8], F32)
    for dst, src_ in ((n1c, "n1_cols"), (n2c, "n2_cols"), (dtb, "dtb_col"), (negA, "A_cols")):
        p.do("gpsimd", lambda g, dst=dst, src_=src_: g.dma_start(out=dst[:], in_=w[src_]),
             reads=[w[src_]], writes=[dst], dma=True)
    p.do("scalar", lambda s: s.activation(out=negA[:], in_=negA[:], func=AF.Exp),
         reads=[negA], writes=[negA])
    p.do("vector", lambda v: v.tensor_scalar(out=negA[:], in0=negA[:], scalar1=-1.0, scalar2=None,
                                             op0=ALU.mult), reads=[negA], writes=[negA])
    for c in range(NC):
        p.do("scalar", lambda s, c=c: s.activation(out=inT[c][:], in_=inT[c][:], func=AF.Identity,
                                                   scale=n2c[:, c:c + 1]),
             reads=[inT[c], n2c], writes=[inT[c]], tag="fold2")
        p.do("scalar", lambda s, c=c: s.activation(out=gaT[c][:], in_=gaT[c][:], func=AF.Identity,
                                                   scale=n2c[:, c:c + 1]),
             reads=[gaT[c], n2c], writes=[gaT[c]], tag="fold3")
    Rrep = sb(st, "Rrep_s", [64, 8 * 128], BF16)
    RBt = sb(st, "RBt", [16, 128], BF16)
    Grep = sb(st, "Grep_s", [128, 8 * 64], BF16)
    onecol_b = sb(st, "onecol_b", [128, 1], BF16)
    onerow_b = sb(st, "onerow_b", [1, 128], BF16)
    for dst, src_ in ((Rrep, "Rrep"), (RBt, "RB"), (Grep, "Grep"), (onecol_b, "ones_col"), (onerow_b, "ones_row")):
        p.do("gpsimd", lambda g, dst=dst, src_=src_: g.dma_start(out=dst[:], in_=w[src_]),
             reads=[w[src_]], writes=[dst], dma=True)

    u_fm = [sb(st, f"u_fm{c}", [128, T], F32) for c in range(NC)]
    h_fm = [sb(st, f"h_fm{c}", [128, T], BF16) for c in range(NC)]
    final_stores = []

    for b in range(NBATCH):
        xb = x_ap[b * T:(b + 1) * T, :]
        ob = out_ap[b * T:(b + 1) * T, :]
        bst = ExitStack()
        xt = [sb(bst, f"xt{b}_{i}", [128, D], F32) for i in range(2)]
        ot = [sb(bst, f"ot{b}_{i}", [128, D], F32) for i in range(2)]

        a1 = ExitStack()
        ut = [sb(a1, f"ut{b}_{i}", [128, D], F32) for i in range(2)]
        ht2 = [sb(a1, f"ht{b}_{i}", [128, D], BF16) for i in range(2)]
        sqs_a = [sb(a1, f"sqs{b}_{i}", [128, 12], F32) for i in range(2)]
        stat = [sb(a1, f"stat{b}_{i}", [128, 8], F32) for i in range(2)]

        def a1_head(tt):
            i = tt % 2
            X, U, ST, SQ = xt[i], ut[i], stat[i], sqs_a[i]
            p.do("sync", lambda s, tt=tt, X=X, xb=xb: s.dma_start(
                out=X[:], in_=xb[tt * 128:(tt + 1) * 128, :]), reads=[xb], writes=[X], dma=True, tag="ldx")
            emit_ln_tile(p, X, U, ST, SQ)
            p.do("vector", lambda v, U=U, SQ=SQ: v.bn_stats(
                out=SQ[:, 0:6], in_=U[:, 0:512]), reads=[U], writes=[SQ], tag="bns2")
            p.do("vector", lambda v, U=U, SQ=SQ: v.bn_stats(
                out=SQ[:, 6:12], in_=U[:, 512:1024]), reads=[U], writes=[SQ], tag="bns2")
            p.do("vector", lambda v, ST=ST, SQ=SQ: v.bn_aggr(
                out=ST[:, 5:7], in_=SQ[:, 0:12].rearrange("p (a b) -> p a b", b=6)),
                reads=[SQ], writes=[ST], tag="bna2")
            p.do("vector", lambda v, ST=ST: v.tensor_tensor(
                out=ST[:, 7:8], in0=ST[:, 5:6], in1=ST[:, 5:6], op=ALU.mult),
                reads=[ST], writes=[ST], tag="musq")
            p.do("vector", lambda v, ST=ST: v.tensor_tensor(
                out=ST[:, 7:8], in0=ST[:, 7:8], in1=ST[:, 6:7], op=ALU.add),
                reads=[ST], writes=[ST])
            p.do("scalar", lambda s, ST=ST: s.activation(
                out=ST[:, 6:7], in_=ST[:, 7:8], func=AF.Sqrt, bias=EPS_RMS),
                reads=[ST], writes=[ST])
            p.do("vector", lambda v, ST=ST: v.reciprocal(
                out=ST[:, 5:6], in_=ST[:, 6:7]), reads=[ST], writes=[ST])
            p.do("scalar", lambda s, U=U, ST=ST, tt=tt: s.activation(
                out=ht2[tt % 2][:], in_=U[:], func=AF.Identity, scale=ST[:, 5:6]),
                reads=[U, ST], writes=[ht2[tt % 2]], tag="ht")

        a1_head(0)
        for tt in range(NT):
            i = tt % 2
            U, ht = ut[i], ht2[i]
            if tt + 1 < NT:
                a1_head(tt + 1)
            for c in range(NC):
                pu, ph = psb[c % 2], psb[2 + c % 2]
                p.do("tensor", lambda t, U=U, c=c, pu=pu: t.transpose(
                    pu[:, 0:128], U[:, c * 128:(c + 1) * 128], iden_f[:]),
                    reads=[U, iden_f], writes=[pu], tag="trU")
                p.do("vector", lambda v, tt=tt, c=c, pu=pu: v.tensor_copy(
                    out=u_fm[c][:, tt * 128:(tt + 1) * 128], in_=pu[:, 0:128]),
                    reads=[pu], writes=[u_fm[c]], tag="evU")
                p.do("tensor", lambda t, c=c, ph=ph, ht=ht: t.transpose(
                    ph.bitcast(BF16)[:, 0:128], ht[:, c * 128:(c + 1) * 128], iden_b[:]),
                    reads=[ht, iden_b], writes=[ph], tag="trH")
                p.do("scalar", lambda s, tt=tt, c=c, ph=ph: s.activation(
                    out=h_fm[c][:, tt * 128:(tt + 1) * 128], in_=ph.bitcast(BF16)[:, 0:128],
                    func=AF.Identity, scale=n1c[:, c:c + 1]),
                    reads=[ph, n1c], writes=[h_fm[c]], tag="evH")
        if io.get("dbg") is not None and b == 0:
            dbg = io["dbg"]
            for tt_ in (2,):
                pass
            p.do("sync", lambda s: s.dma_start(out=dbg[0:128, 0:8], in_=stat[0][:]),
                 reads=[stat[0]], writes=[dbg], dma=True, tag="dbgstat")
            p.do("sync", lambda s: s.dma_start(out=dbg[0:128, 8:8 + T], in_=u_fm[0][:]),
                 reads=[u_fm[0]], writes=[dbg], dma=True, tag="dbgu")
            p.do("sync", lambda s: s.dma_start(out=dbg[0:128, 8 + T:8 + T + D], in_=ut[0][:]),
                 reads=[ut[0]], writes=[dbg], dma=True, tag="dbgut")
        p.fence()
        a1.close()

        a2 = ExitStack()
        DO2 = 2 in PHASES
        attn_fm = [sb(a2, f"at{b}_fm{c}", [128, T], BF16) for c in range(NC)]
        Qh = [[sb(a2, f"Qh{b}_{pp}_{h}", [128, T], BF16) for h in range(2)] for pp in range(2)]
        Kh = [[sb(a2, f"Kh{b}_{pp}_{h}", [128, T], BF16) for h in range(2)] for pp in range(2)]
        Vh = [[sb(a2, f"Vh{b}_0_{h}", [128, 16 * 65], BF16) for h in range(2)]] * 2
        pairT = [[sb(a2, f"pT{b}_0_{c}", [128, 384], BF16) for c in range(NC)]] * 2
        qa = sb(a2, f"qa{b}", [64, 512], BF16)
        kmb = sb(a2, f"kmb{b}", [64, 1], F32)
        kmbb = sb(a2, f"kmbb{b}", [64, 1], BF16)
        p_sb = [sb(a2, f"p_sb{b}_{i}", [128, 512], BF16) for i in range(4)]
        o_sb = sb(a2, f"o_sb{b}", [64, 512], BF16)
        rdenf = psb[6][0:1, :]
        rden = sb(a2, f"rden{b}", [1, 512], BF16)
        for hp in (range(8) if DO2 else []):
            pp = hp % 2
            QhP, KhP, VhP, pT = Qh[pp], Kh[pp], Vh[pp], pairT[pp]
            for j, rowbase in enumerate((hp * 128, 1024 + hp * 128, 2048 + hp * 128)):
                buf = wtmp[j % 2]
                p.do("gpsimd", lambda g, buf=buf, rowbase=rowbase: g.dma_start(
                    out=buf[:], in_=w["qkv_w"][rowbase:rowbase + 128, :]),
                    reads=[w["qkv_w"]], writes=[buf], dma=True, tag="qkvld")
                for c in range(NC):
                    ps = psb[2 + (j * NC + c) % 2]
                    p.do("tensor", lambda t, buf=buf, c=c, ps=ps: t.transpose(
                        ps.bitcast(BF16)[:, 0:128], buf[:, c * 128:(c + 1) * 128], iden_b[:]),
                        reads=[buf, iden_b], writes=[ps], tag="qkvtr")
                    p.do("vector", lambda v, c=c, j=j, ps=ps, pT=pT: v.tensor_copy(
                        out=pT[c][:, j * 128:(j + 1) * 128], in_=ps.bitcast(BF16)[:, 0:128]),
                        reads=[ps], writes=[pT[c]], tag="qkvev")
            for which, cb in ((("q", 0), ("k", 128)) if KK_A2L() >= 0.5 else ()):
                dst0, dst1 = (QhP[0], QhP[1]) if which == "q" else (KhP[0], KhP[1])
                for qc in range(NQC):
                    ps = psb[2 + qc % 2]
                    for c in range(NC):
                        p.do("tensor", lambda t, c=c, qc=qc, cb=cb, ps=ps, pT=pT: t.matmul(
                            ps[:], pT[c][:, cb:cb + 128], h_fm[c][:, qc * 512:(qc + 1) * 512],
                            start=(c == 0), stop=(c == NC - 1)),
                            reads=[pT[c], h_fm[c]], writes=[ps], tag="proj")
                    p.do("vector", lambda v, qc=qc, ps=ps, dst0=dst0: v.tensor_copy(
                        out=dst0[0:64, qc * 512:(qc + 1) * 512], in_=ps[0:64, :]),
                        reads=[ps], writes=[dst0], tag="pe0")
                    p.do("scalar", lambda s, qc=qc, ps=ps, dst1=dst1: s.copy(
                        out=dst1[0:64, qc * 512:(qc + 1) * 512], in_=ps[64:128, :]),
                        reads=[ps], writes=[dst1], tag="pe1")
            for h in (range(2) if KK_A2L() >= 2 else []):
                Q, K = QhP[h], KhP[h]
                p.do("vector", lambda v, K=K: v.tensor_reduce(
                    out=kmb[:, 0:1], in_=K[0:64, :], axis=AX.X, op=ALU.max,
                    apply_absolute_value=True), reads=[K], writes=[kmb], tag="km")
                p.do("vector", lambda v: v.tensor_copy(out=kmbb[:], in_=kmb[:, 0:1]),
                     reads=[kmb], writes=[kmbb])
                for qc in range(NQC):
                    p.do("scalar", lambda s, Q=Q, qc=qc: s.activation(
                        out=qa[:], in_=Q[0:64, qc * 512:(qc + 1) * 512], func=AF.Abs),
                        reads=[Q], writes=[qa], tag="qa")
                    p.do("tensor", lambda t, qc=qc: t.matmul(
                        psb[6][0:1, :], kmbb[:, 0:1], qa[:],
                        start=True, stop=True), reads=[kmbb, qa], writes=[psb[6]], tag="mhat")
                    p.do("scalar", lambda s, qc=qc, Q=Q: s.activation(
                        out=Q[64:65, qc * 512:(qc + 1) * 512], in_=psb[6][0:1, :],
                        func=AF.Identity, scale=-1.0, bias=240.0),
                        reads=[psb[6]], writes=[Q], tag="mrow")
            for tt in (range(NT) if KK_A2L() >= 0.8 else []):
                ps = psb[2 + tt % 2]
                for c in range(NC):
                    p.do("tensor", lambda t, c=c, tt=tt, ps=ps, pT=pT: t.matmul(
                        ps[:, 0:128], h_fm[c][:, tt * 128:(tt + 1) * 128],
                        pT[c][:, 256:384],
                        start=(c == 0), stop=(c == NC - 1)),
                        reads=[h_fm[c], pT[c]], writes=[ps], tag="projv")
                p.do("vector", lambda v, tt=tt, ps=ps, VhP=VhP: v.tensor_copy(
                    out=VhP[0][:, tt * 65:tt * 65 + 64], in_=ps[:, 0:64]),
                    reads=[ps], writes=[VhP[0]], tag="ve0")
                p.do("scalar", lambda s, tt=tt, ps=ps, VhP=VhP: s.copy(
                    out=VhP[1][:, tt * 65:tt * 65 + 64], in_=ps[:, 64:128]),
                    reads=[ps], writes=[VhP[1]], tag="ve1")
            for h in (range(2) if KK_A2L() >= 1.5 else []):
                p.do("vector", lambda v, h=h, VhP=VhP: v.memset(
                    VhP[h][:].rearrange("p (t j) -> p t j", j=65)[:, :, 64:65], 1.0),
                    reads=[], writes=[VhP[h]], unsafe=True, tag="vone")
                p.do("vector", lambda v, h=h, KhP=KhP: v.memset(KhP[h][64:65, :], 1.0),
                     reads=[], writes=[KhP[h]], unsafe=True, tag="kone")
            psAcc = [psO, psb[4][0:65, :]]
            for qc in (range(NQC) if KK_A2L() >= 3 else []):
                nkt = 4 * qc + 4

                def emit_S2(kt, h, qc=qc, QhP=QhP, KhP=KhP):
                    psS = psb[h]
                    P = p_sb[(2 * kt + h) % 4]
                    Q, K = QhP[h], KhP[h]
                    diag = kt >= 4 * qc
                    if diag:
                        v_i = kt - 4 * qc
                        p.do("tensor", lambda t, v_i=v_i, psS=psS: t.matmul(
                            psS[:], iden_b[:],
                            diagm[:, (3 - v_i) * 128:(3 - v_i) * 128 + 512],
                            start=True, stop=False), reads=[iden_b, diagm], writes=[psS], tag="Smask")
                    p.do("tensor", lambda t, kt=kt, qc=qc, Q=Q, K=K, psS=psS, diag=diag: t.matmul(
                        psS[:], K[0:65, kt * 128:(kt + 1) * 128], Q[0:65, qc * 512:(qc + 1) * 512],
                        start=not diag, stop=True), reads=[K, Q], writes=[psS], tag="S")
                    p.do("scalar", lambda s, psS=psS, P=P: s.activation(
                        out=P[:], in_=psS[:], func=AF.Exp, scale=0.125),
                        reads=[psS], writes=[P], tag="exp")

                def emit_PV2(kt, h, nkt=nkt, VhP=VhP):
                    P = p_sb[(2 * kt + h) % 4]
                    acc = psAcc[h]
                    p.do("tensor", lambda t, kt=kt, h=h, P=P, nkt=nkt, acc=acc, VhP=VhP: t.matmul(
                        acc[:], VhP[h][:, kt * 65:(kt + 1) * 65], P[:],
                        start=(kt == 0), stop=(kt == nkt - 1)),
                        reads=[VhP[h], P], writes=[acc], tag="PV")

                for kt in range(nkt):
                    emit_S2(kt, 0)
                    emit_S2(kt, 1)
                    if KK_A2L() >= 4 and kt >= 1:
                        emit_PV2(kt - 1, 0)
                        emit_PV2(kt - 1, 1)
                if KK_A2L() >= 4:
                    emit_PV2(nkt - 1, 0)
                    emit_PV2(nkt - 1, 1)
                if KK_A2L() < 5:
                    continue
                for h in range(2):
                    acc = psAcc[h]
                    p.do("scalar", lambda s, acc=acc: s.activation(out=rdenf[:], in_=acc[64:65, :],
                                                                  func=AF.Ln),
                         reads=[acc], writes=[rdenf], tag="rdl")
                    p.do("scalar", lambda s: s.activation(out=rden[:], in_=rdenf[:],
                                                          func=AF.Exp, scale=-1.0),
                         reads=[rdenf], writes=[rden], tag="rde")
                    p.do("tensor", lambda t: t.matmul(
                        psb[5][0:64, :], onerow_b[0:1, 0:64], rden[:], start=True, stop=True),
                        reads=[onerow_b, rden], writes=[psb[5]], tag="bcast")
                    p.do("vector", lambda v, acc=acc: v.tensor_copy(out=o_sb[:], in_=acc[0:64, :]),
                         reads=[acc], writes=[o_sb], tag="oev")
                    hrow = h * 64
                    p.do("vector", lambda v, hp=hp, hrow=hrow, qc=qc: v.tensor_tensor(
                        out=attn_fm[hp][hrow:hrow + 64, qc * 512:(qc + 1) * 512],
                        in0=o_sb[:], in1=psb[5][0:64, :], op=ALU.mult),
                        reads=[o_sb, psb[5]], writes=[attn_fm[hp]], tag="an")
        oslice = [pairT[1][c] for c in range(NC)]
        for dc in (range(NC) if (DO2 and KK_A2L() >= 6) else []):
            buf = wtmp[dc % 2]
            p.do("gpsimd", lambda g, buf=buf, dc=dc: g.dma_start(
                out=buf[:], in_=w["o_w"][dc * 128:(dc + 1) * 128, :]),
                reads=[w["o_w"]], writes=[buf], dma=True, tag="owld")
            for c in range(NC):
                ps = psb[c % 2]
                p.do("tensor", lambda t, buf=buf, c=c, ps=ps: t.transpose(
                    ps.bitcast(BF16)[:, 0:128], buf[:, c * 128:(c + 1) * 128], iden_b[:]),
                    reads=[buf, iden_b], writes=[ps], tag="owtr")
                p.do("vector", lambda v, c=c, ps=ps: v.tensor_copy(
                    out=oslice[c][:, 0:128], in_=ps.bitcast(BF16)[:, 0:128]),
                    reads=[ps], writes=[oslice[c]], tag="owev")
            for qc in range(NQC):
                ps = psb[4 + qc % 2]
                for c in range(NC):
                    p.do("tensor", lambda t, qc=qc, c=c, ps=ps: t.matmul(
                        ps[:], oslice[c][:, 0:128], attn_fm[c][:, qc * 512:(qc + 1) * 512],
                        start=(c == 0), stop=(c == NC - 1)),
                        reads=[oslice[c], attn_fm[c]], writes=[ps], tag="oproj")
                p.do("vector", lambda v, dc=dc, qc=qc, ps=ps: v.tensor_tensor(
                    out=u_fm[dc][:, qc * 512:(qc + 1) * 512],
                    in0=u_fm[dc][:, qc * 512:(qc + 1) * 512], in1=ps[:], op=ALU.add),
                    reads=[u_fm[dc], ps], writes=[u_fm[dc]], tag="oev2")
        p.fence()
        a2.close()

        a3 = ExitStack()
        DO3 = 3 in PHASES
        z_sb = sb(a3, f"z{b}", [64, T], BF16)
        gate_sb = sb(a3, f"ga{b}", [64, T], BF16)
        dtbf = sb(a3, f"dtbf{b}", [64, T], BF16)
        dtzb = sb(a3, f"dtzb{b}", [64, T], BF16)
        Bi_sb = sb(a3, f"Bi{b}", [16, T], BF16)
        Ci_sb = sb(a3, f"Ci{b}", [16, T], BF16)
        sp1 = sb(a3, f"sp1{b}", [64, 512], F32)
        sp2 = sb(a3, f"sp2{b}", [64, 512], F32)
        a_sc_ = [sb(a3, f"asc{b}_0", [128, T], F32)] * 2
        u_sc_ = [sb(a3, f"usc{b}_0", [128, T], F32)] * 2
        h_sc_ = [sb(a3, f"hsc{b}_{i}", [128, T], F32) for i in range(2)]
        w_sc_ = [sb(a3, f"wsc{b}_0", [128, T], BF16)] * 2
        yg_sb = sb(a3, f"yg{b}", [64, T], BF16)
        usq = sb(a3, f"usq{b}", [128, 512], BF16)
        r2row = sb(a3, f"r2r{b}", [1, T], F32)
        r2rowb = sb(a3, f"r2rb{b}", [1, T], BF16)
        for qc in (range(NQC) if DO3 else []):
            for c in range(NC):
                p.do("scalar", lambda s, c=c, qc=qc: s.activation(
                    out=usq[:], in_=u_fm[c][:, qc * 512:(qc + 1) * 512], func=AF.Square),
                    reads=[u_fm[c]], writes=[usq], tag="usq")
                p.do("tensor", lambda t, c=c: t.matmul(
                    psb[6][0:1, :], onecol_b[:, 0:1], usq[:],
                    start=(c == 0), stop=(c == NC - 1)), reads=[onecol_b, usq], writes=[psb[6]], tag="s2")
            p.do("scalar", lambda s, qc=qc: s.activation(
                out=r2row[:, qc * 512:(qc + 1) * 512], in_=psb[6][0:1, :], func=AF.Sqrt,
                scale=1.0 / D, bias=EPS_RMS), reads=[psb[6]], writes=[r2row], tag="s2ev")
        if not DO3:
            p.do("vector", lambda v: v.memset(r2row[:], 1.0), reads=[], writes=[r2row], unsafe=True)
        p.do("scalar", lambda s: s.activation(out=r2row[:], in_=r2row[:], func=AF.Ln),
             reads=[r2row], writes=[r2row], tag="r2l")
        p.do("scalar", lambda s: s.activation(out=r2rowb[:], in_=r2row[:], func=AF.Exp, scale=-1.0),
             reads=[r2row], writes=[r2rowb], tag="r2e")
        for qc in (range(NQC) if DO3 else []):
            p.do("tensor", lambda t, qc=qc: t.matmul(
                psb[6][:], onerow_b[0:1, :], r2rowb[:, qc * 512:(qc + 1) * 512],
                start=True, stop=True), reads=[onerow_b, r2rowb], writes=[psb[6]], tag="r2bc")
            for c in range(NC):
                p.do("vector", lambda v, c=c, qc=qc: v.tensor_tensor(
                    out=h_fm[c][:, qc * 512:(qc + 1) * 512],
                    in0=u_fm[c][:, qc * 512:(qc + 1) * 512], in1=psb[6][:], op=ALU.mult),
                    reads=[u_fm[c], psb[6]], writes=[h_fm[c]], tag="h2")
        for qc in (range(NQC) if DO3 else []):
            for c in range(NC):
                p.do("tensor", lambda t, c=c, qc=qc: t.matmul(
                    psb[0][0:64, :], inT[c][:, 0:64], h_fm[c][:, qc * 512:(qc + 1) * 512],
                    start=(c == 0), stop=(c == NC - 1)), reads=[inT[c], h_fm[c]], writes=[psb[0]], tag="zmm")
            p.do("vector", lambda v, qc=qc: v.tensor_copy(
                out=z_sb[:, qc * 512:(qc + 1) * 512], in_=psb[0][0:64, :]),
                reads=[psb[0]], writes=[z_sb], tag="zev")
            for c in range(NC):
                p.do("tensor", lambda t, c=c, qc=qc: t.matmul(
                    psb[1][0:64, :], gaT[c][:, 0:64], h_fm[c][:, qc * 512:(qc + 1) * 512],
                    start=(c == 0), stop=(c == NC - 1)), reads=[gaT[c], h_fm[c]], writes=[psb[1]], tag="gmm")
            p.do("scalar", lambda s, qc=qc: s.activation(
                out=gate_sb[:, qc * 512:(qc + 1) * 512], in_=psb[1][0:64, :], func=AF.Silu),
                reads=[psb[1]], writes=[gate_sb], tag="gev")
            p.do("tensor", lambda t, qc=qc: t.matmul(
                psb[2][0:64, :], dtT[0][0:64, 0:64], z_sb[:, qc * 512:(qc + 1) * 512],
                start=True, stop=True), reads=[dtT[0], z_sb], writes=[psb[2]], tag="dtmm")
            p.do("scalar", lambda s: s.activation(
                out=sp1[:], in_=psb[2][0:64, :], func=AF.Abs, bias=dtb[0:64, :]),
                reads=[psb[2], dtb], writes=[sp1])
            p.do("scalar", lambda s: s.activation(
                out=sp1[:], in_=sp1[:], func=AF.Exp, scale=-1.0), reads=[sp1], writes=[sp1])
            p.do("scalar", lambda s: s.activation(
                out=sp1[:], in_=sp1[:], func=AF.Ln, bias=1.0), reads=[sp1], writes=[sp1])
            p.do("scalar", lambda s: s.activation(
                out=sp2[:], in_=psb[2][0:64, :], func=AF.Relu, bias=dtb[0:64, :]),
                reads=[psb[2], dtb], writes=[sp2])
            p.do("vector", lambda v, qc=qc: v.tensor_tensor(
                out=dtbf[:, qc * 512:(qc + 1) * 512], in0=sp1[:], in1=sp2[:], op=ALU.add),
                reads=[sp1, sp2], writes=[dtbf], tag="dt")
            p.do("vector", lambda v, qc=qc: v.tensor_tensor(
                out=dtzb[:, qc * 512:(qc + 1) * 512], in0=dtbf[:, qc * 512:(qc + 1) * 512],
                in1=z_sb[:, qc * 512:(qc + 1) * 512], op=ALU.mult),
                reads=[dtbf, z_sb], writes=[dtzb], tag="dtz")
            p.do("tensor", lambda t, qc=qc: t.matmul(
                psb[3][0:16, :], bpT[0][0:64, 0:16], z_sb[:, qc * 512:(qc + 1) * 512],
                start=True, stop=True), reads=[bpT[0], z_sb], writes=[psb[3]], tag="bimm")
            p.do("vector", lambda v, qc=qc: v.tensor_copy(
                out=Bi_sb[:, qc * 512:(qc + 1) * 512], in_=psb[3][0:16, :]),
                reads=[psb[3]], writes=[Bi_sb], tag="biev")
            p.do("tensor", lambda t, qc=qc: t.matmul(
                psb[4][0:16, :], cpT[0][0:64, 0:16], z_sb[:, qc * 512:(qc + 1) * 512],
                start=True, stop=True), reads=[cpT[0], z_sb], writes=[psb[4]], tag="cimm")
            p.do("scalar", lambda s, qc=qc: s.copy(
                out=Ci_sb[:, qc * 512:(qc + 1) * 512], in_=psb[4][0:16, :]),
                reads=[psb[4]], writes=[Ci_sb], tag="ciev")
        for i in (range(8) if DO3 else []):
            a_sc, u_sc, h_sc, w_sc = a_sc_[i % 2], u_sc_[i % 2], h_sc_[i % 2], w_sc_[i % 2]
            for qc in range(NQC):
                p.do("tensor", lambda t, i=i, qc=qc: t.matmul(
                    psb[0][:], Rrep[:, i * 128:(i + 1) * 128], dtbf[:, qc * 512:(qc + 1) * 512],
                    start=True, stop=True), reads=[Rrep, dtbf], writes=[psb[0]], tag="dtrep")
                p.do("scalar", lambda s, i=i, qc=qc: s.activation(
                    out=a_sc[:, qc * 512:(qc + 1) * 512], in_=psb[0][:], func=AF.Identity,
                    scale=negA[:, i:i + 1], bias=1.0), reads=[psb[0], negA], writes=[a_sc], tag="a")
                p.do("tensor", lambda t, i=i, qc=qc: t.matmul(
                    psb[1][:], Rrep[:, i * 128:(i + 1) * 128], dtzb[:, qc * 512:(qc + 1) * 512],
                    start=True, stop=True), reads=[Rrep, dtzb], writes=[psb[1]], tag="dtzrep")
                p.do("scalar", lambda s, qc=qc: s.copy(
                    out=u_sc[:, qc * 512:(qc + 1) * 512], in_=psb[1][:]),
                    reads=[psb[1]], writes=[u_sc], tag="dtrev")
                p.do("tensor", lambda t, qc=qc: t.matmul(
                    psb[2][:], RBt[:, :], Bi_sb[:, qc * 512:(qc + 1) * 512],
                    start=True, stop=True), reads=[RBt, Bi_sb], writes=[psb[2]], tag="birep")
                p.do("vector", lambda v, qc=qc: v.tensor_tensor(
                    out=u_sc[:, qc * 512:(qc + 1) * 512], in0=u_sc[:, qc * 512:(qc + 1) * 512],
                    in1=psb[2][:], op=ALU.mult), reads=[u_sc, psb[2]], writes=[u_sc], tag="usc")
            p.do("vector", lambda v: v.tensor_tensor_scan(
                out=h_sc[:], data0=a_sc[:], data1=u_sc[:], initial=0.0,
                op0=ALU.mult, op1=ALU.add), reads=[a_sc, u_sc], writes=[h_sc], tag="scan")
            for qc in range(NQC):
                p.do("tensor", lambda t, qc=qc: t.matmul(
                    psb[3][:], RBt[:, :], Ci_sb[:, qc * 512:(qc + 1) * 512],
                    start=True, stop=True), reads=[RBt, Ci_sb], writes=[psb[3]], tag="cirep")
                p.do("vector", lambda v, qc=qc: v.tensor_tensor(
                    out=w_sc[:, qc * 512:(qc + 1) * 512], in0=h_sc[:, qc * 512:(qc + 1) * 512],
                    in1=psb[3][:], op=ALU.mult), reads=[h_sc, psb[3]], writes=[w_sc], tag="wsc")
                yps = [psb[4], psb[5], psb[6], psO][qc]
                p.do("tensor", lambda t, i=i, qc=qc, yps=yps: t.matmul(
                    yps[0:64, 0:512], Grep[:, i * 64:(i + 1) * 64],
                    w_sc[:, qc * 512:(qc + 1) * 512],
                    start=(i == 0), stop=(i == 7)), reads=[Grep, w_sc],
                    writes=[yps], tag="yacc")
        for qc in (range(NQC) if DO3 else []):
            yps = [psb[4], psb[5], psb[6], psO][qc]
            p.do("vector", lambda v, qc=qc, yps=yps: v.tensor_tensor(
                out=yg_sb[:, qc * 512:(qc + 1) * 512], in0=yps[0:64, 0:512],
                in1=gate_sb[:, qc * 512:(qc + 1) * 512], op=ALU.mult),
                reads=[yps, gate_sb], writes=[yg_sb], tag="yg")
        for dc in (range(NC) if DO3 else []):
            for qc in range(NQC):
                ps = psb[(dc * NQC + qc) % 4]
                p.do("tensor", lambda t, dc=dc, qc=qc, ps=ps: t.matmul(
                    ps[:], ouT[0][0:64, dc * 128:(dc + 1) * 128], yg_sb[:, qc * 512:(qc + 1) * 512],
                    start=True, stop=True), reads=[ouT[0], yg_sb], writes=[ps], tag="sproj")
                p.do("vector", lambda v, dc=dc, qc=qc, ps=ps: v.tensor_tensor(
                    out=u_fm[dc][:, qc * 512:(qc + 1) * 512],
                    in0=u_fm[dc][:, qc * 512:(qc + 1) * 512], in1=ps[:], op=ALU.add),
                    reads=[u_fm[dc], ps], writes=[u_fm[dc]], tag="sev")
        p.fence()
        a3.close()

        for tt in range(NT):
            i = tt % 2
            X, O = xt[i], ot[i]
            p.do("sync", lambda s, tt=tt, X=X, xb=xb: s.dma_start(
                out=X[:], in_=xb[tt * 128:(tt + 1) * 128, :]), reads=[xb], writes=[X], dma=True, tag="ldx2")
            for c in range(NC):
                pu = psb[4 + c % 2]
                p.do("tensor", lambda t, tt=tt, c=c, pu=pu: t.transpose(
                    pu[:, 0:128], u_fm[c][:, tt * 128:(tt + 1) * 128], iden_f[:]),
                    reads=[u_fm[c], iden_f], writes=[pu], tag="trO")
                p.do("vector", lambda v, c=c, X=X, O=O, pu=pu: v.tensor_tensor(
                    out=O[:, c * 128:(c + 1) * 128], in0=X[:, c * 128:(c + 1) * 128],
                    in1=pu[:, 0:128], op=ALU.add), reads=[X, pu], writes=[O], tag="oadd")
            stx = p.do("sync", lambda s, tt=tt, O=O, ob=ob: s.dma_start(
                out=ob[tt * 128:(tt + 1) * 128, :], in_=O[:]), reads=[O], writes=[ob], dma=True, tag="stx")
            final_stores.append(stx)
        p.fence()
        bst.close()
    return final_stores



def make_launch_a():
    nc = bass.Bass("TRN2", target_bir_lowering=False)
    st = ExitStack()
    io = {}
    io["x"] = nc.dram_tensor("x", [B_PER_CORE * T, D], F32, kind="ExternalInput").ap()
    io["out"] = nc.dram_tensor("out", [B_PER_CORE * T, D], F32, kind="ExternalOutput").ap()
    io["dbg"] = nc.dram_tensor("dbg", [128, 8 + T + D], F32, kind="ExternalOutput").ap()
    for name, shape in [("qkv_w", [3072, D]), ("o_w", [D, D]),
                        ("in_w", [64, D]), ("out_w", [D, 64]),
                        ("Bp_w", [16, 64]), ("Cp_w", [16, 64]),
                        ("dt_w", [64, 64]), ("gate_w", [64, D]),
                        ("n1_cols", [128, 8]), ("n2_cols", [128, 8]),
                        ("A_cols", [128, 8]), ("dtb_col", [64, 1])]:
        io[name] = nc.dram_tensor(name, shape, F32, kind="ExternalInput").ap()
    for name, shape in HOST_CONST_SHAPES.items():
        io[name] = nc.dram_tensor(name, list(shape), F32, kind="ExternalInput").ap()
    p = Prog(nc)
    with st:
        finals = build_l0_attn(p, st, io)
        p.do("gpsimd", lambda g: g.engine_nop(), deps=finals, tag="fence")
        with nc.Block() as block:
            p.emit(block, st)
    return nc


def l0_attn_maps(inputs):
    consts = host_constants()
    xs = inputs["L0_attn_x"].reshape(16, T, D)
    in_maps = []
    for c in range(NCORE):
        m = {"x": np.ascontiguousarray(xs[2 * c:2 * c + 2].reshape(B_PER_CORE * T, D))}
        for k in ["qkv_w", "o_w", "in_w", "out_w", "Bp_w", "Cp_w", "dt_w", "gate_w"]:
            m[k] = inputs["L0_attn_" + k]
        m["n1_cols"] = np.ascontiguousarray(inputs["L0_attn_norm1_w"].reshape(8, 128).T)
        m["n2_cols"] = np.ascontiguousarray(inputs["L0_attn_norm2_w"].reshape(8, 128).T)
        m["A_cols"] = np.ascontiguousarray(inputs["L0_attn_A_log"].reshape(8, 128).T)
        m["dtb_col"] = np.ascontiguousarray(inputs["L0_attn_dt_b"].reshape(64, 1))
        m.update(consts)
        in_maps.append(m)
    return in_maps


def _gather_out(res):
    return np.concatenate([np.asarray(r["out"]).reshape(B_PER_CORE, T, D)
                           for r in res.results], 0)


def kernel_unfused(**inputs):
    cores = list(range(NCORE))
    x1 = _gather_out(run_bass_kernel_spmd(make_launch_a(), l0_attn_maps(inputs), core_ids=cores))
    x2 = _gather_out(run_bass_kernel_spmd(make_launch_b(), l0_ffn_maps(inputs, x1), core_ids=cores))
    x3 = _gather_out(run_bass_kernel_spmd(make_launch_c(), l1_attn_maps(inputs, x2), core_ids=cores))
    x4 = _gather_out(run_bass_kernel_spmd(make_launch_d(), l1_ffn_maps(inputs, x3), core_ids=cores))
    return x4


def kernel(**inputs):
    nc = make_fused()
    res = run_bass_kernel_spmd(nc, fused_maps(inputs), core_ids=list(range(NCORE)))
    return _gather_out(res)



def emit_ln_tile(p, X, U, ST, sqs):
    p.do("vector", lambda v: v.bn_stats(out=sqs[:, 0:6], in_=X[:, 0:512]),
         reads=[X], writes=[sqs], tag="bns")
    p.do("vector", lambda v: v.bn_stats(out=sqs[:, 6:12], in_=X[:, 512:1024]),
         reads=[X], writes=[sqs], tag="bns")
    p.do("vector", lambda v: v.bn_aggr(out=ST[:, 0:2],
                                       in_=sqs[:, 0:12].rearrange("p (a b) -> p a b", b=6)),
         reads=[sqs], writes=[ST], tag="bna")
    p.do("scalar", lambda s: s.activation(out=ST[:, 2:3], in_=ST[:, 1:2], func=AF.Sqrt, bias=1e-5),
         reads=[ST], writes=[ST])
    p.do("vector", lambda v: v.reciprocal(out=ST[:, 3:4], in_=ST[:, 2:3]), reads=[ST], writes=[ST])
    p.do("vector", lambda v: v.scalar_tensor_tensor(out=ST[:, 4:5], in0=ST[:, 0:1], scalar=-1.0,
                                                    in1=ST[:, 3:4], op0=ALU.mult, op1=ALU.mult),
         reads=[ST], writes=[ST])
    p.do("scalar", lambda s: s.activation(out=U[:], in_=X[:], func=AF.Identity, scale=ST[:, 3:4],
                                          bias=ST[:, 4:5]), reads=[X, ST], writes=[U], tag="u")


def emit_fp4_quant(p, U, DEQ, scr, cols, pool_adds=False, defer_deq=False):
    nb = cols // 16
    bm, xn, m = scr["bm"], scr["xn"], scr["m"]
    U3 = U[:].rearrange("p (b i) -> p b i", i=16)
    p.do("vector", lambda v: v.tensor_reduce(out=bm[:, 0:nb], in_=U3, axis=AX.X, op=ALU.max,
                                             apply_absolute_value=True),
         reads=[U], writes=[bm], tag="bmax")
    p.do("vector", lambda v: v.tensor_scalar(out=bm[:, 0:nb], in0=bm[:, 0:nb], scalar1=6e-12,
                                             scalar2=1.0 / 6.0, op0=ALU.max, op1=ALU.mult),
         reads=[bm], writes=[bm], tag="bscale")
    p.do("vector", lambda v: v.reciprocal(out=bm[:, nb:2 * nb], in_=bm[:, 0:nb]),
         reads=[bm], writes=[bm], tag="binv")
    inv3 = bm[:, nb:2 * nb].unsqueeze(2).broadcast_to([128, nb, 16])
    scl3 = bm[:, 0:nb].unsqueeze(2).broadcast_to([128, nb, 16])
    p.do("vector", lambda v: v.tensor_tensor(out=xn[:].rearrange("p (b i) -> p b i", i=16),
                                             in0=U3, in1=inv3, op=ALU.mult),
         reads=[U, bm], writes=[xn], tag="xn")
    p.do("vector", lambda v: v.tensor_scalar(out=m[:].bitcast(I32), in0=xn[:].bitcast(I32),
                                             scalar1=0x7F800000, scalar2=None, op0=ALU.bitwise_and),
         reads=[xn], writes=[m], tag="emask")
    p.do("vector", lambda v: v.tensor_scalar(out=m[:], in0=m[:], scalar1=6291456.0,
                                             scalar2=6291456.0, op0=ALU.mult, op1=ALU.max),
         reads=[m], writes=[m], tag="magic")
    eng_add = "gpsimd" if pool_adds else "vector"
    p.do(eng_add, lambda v: v.tensor_tensor(out=xn[:], in0=xn[:], in1=m[:], op=ALU.add),
         reads=[xn, m], writes=[xn], tag="madd")
    p.do(eng_add, lambda v: v.tensor_tensor(out=xn[:], in0=xn[:], in1=m[:], op=ALU.subtract),
         reads=[xn, m], writes=[xn], tag="msub")
    def deq():
        p.do("vector", lambda v: v.tensor_tensor(out=DEQ[:].rearrange("p (b i) -> p b i", i=16),
                                                 in0=xn[:].rearrange("p (b i) -> p b i", i=16),
                                                 in1=scl3, op=ALU.mult),
             reads=[xn, bm], writes=[DEQ], tag="deq")
    if defer_deq:
        return deq
    deq()



I_FF = 2816
NIT = 22
WG_TPC = 3


def build_l1_ffn_wquant(p, st, io):
    nc = p.nc
    pfx = io.get("pfx", "")

    def sb(st_, name, shape, dt):
        return st_.enter_context(nc.sbuf_tensor(pfx + name, shape, dt))

    w = io
    wg_in = nc.dram_tensor(pfx + "wg_in", [WG_TPC * 128, D], BF16).ap()
    wu_in = nc.dram_tensor(pfx + "wu_in", [WG_TPC * 128, D], BF16).ap()
    wd_in = nc.dram_tensor(pfx + "wd_in", [128, I_FF], BF16).ap()
    wg_out = nc.dram_tensor(pfx + "wg_out", [8 * WG_TPC * 128, D], BF16, addr_space="Shared").ap()
    wu_out = nc.dram_tensor(pfx + "wu_out", [8 * WG_TPC * 128, D], BF16, addr_space="Shared").ap()
    wd_out = nc.dram_tensor(pfx + "wd_out", [8 * 128, I_FF], BF16, addr_space="Shared").ap()
    wq = ExitStack()
    wtile = sb(wq, "wtile", [128, I_FF], F32)
    wdeq = sb(wq, "wdeq", [128, I_FF], BF16)
    scr = {"bm": sb(wq, "q_bm", [128, 2 * (I_FF // 16)], F32),
           "xn": sb(wq, "q_xn", [128, I_FF], F32),
           "m": sb(wq, "q_m", [128, I_FF], F32)}
    for name, src_rows, cin in (("wg", "wg_rows", wg_in), ("wu", "wu_rows", wu_in)):
        for j in range(WG_TPC):
            p.do("sync", lambda s, j=j, src_rows=src_rows: s.dma_start(
                out=wtile[:, 0:D], in_=w[src_rows][j * 128:(j + 1) * 128, :]),
                reads=[w[src_rows]], writes=[wtile], dma=True, tag="wld")
            scr_d = {"bm": _View(scr["bm"], 0, 2 * (D // 16)), "xn": _View(scr["xn"], 0, D),
                     "m": _View(scr["m"], 0, D)}
            emit_fp4_quant(p, _View(wtile, 0, D), _View(wdeq, 0, D), scr_d, D)
            p.do("sync", lambda s, j=j, cin=cin: s.dma_start(
                out=cin[j * 128:(j + 1) * 128, :], in_=wdeq[:, 0:D]),
                reads=[wdeq], writes=[cin], dma=True, tag="wst")
    p.do("sync", lambda s: s.dma_start(out=wtile[:], in_=w["wd_rows"][:]),
         reads=[w["wd_rows"]], writes=[wtile], dma=True, tag="wldd")
    emit_fp4_quant(p, _View(wtile, 0, I_FF), _View(wdeq, 0, I_FF), scr, I_FF)
    p.do("sync", lambda s: s.dma_start(out=wd_in[:], in_=wdeq[:]),
         reads=[wdeq], writes=[wd_in], dma=True, tag="wstd")
    for cin, cout in ((wg_in, wg_out), (wu_in, wu_out), (wd_in, wd_out)):
        cc = p.do("gpsimd", lambda g, cin=cin, cout=cout: g.collective_compute(
            "AllGather", ALU.bypass, replica_groups=[list(range(8))],
            ins=[cin.opt()], outs=[cout.opt()]),
            reads=[cin], writes=[cout], tag="ag")
        cc.amount = 1
        cc.sem_key = "cc_" + _tname(cout)
    wq.close()
    return {"wg_out": wg_out, "wu_out": wu_out, "wd_out": wd_out}


def build_l1_ffn(p, st, io):
    nc = p.nc
    pfx = io.get("pfx", "")
    NT_ALL = (B_PER_CORE * T) // 128
    NC = D // 128
    x_ap, out_ap, w = io["x"], io["out"], io

    def sb(st_, name, shape, dt):
        return st_.enter_context(nc.sbuf_tensor(pfx + name, shape, dt))

    if not io.get("cvals_ready"):
        cvals_s = sb(st, "cvals_s", [128, len(CONST_VALS)], F32)
        p.do("gpsimd", lambda g: g.dma_start(out=cvals_s[:], in_=w["cvals"]),
             reads=[w["cvals"]], writes=[cvals_s], dma=True)
        for j, v in enumerate(CONST_VALS):
            nc.const_aps.register(F32, float(v), cvals_s[:, j:j + 1])
        p.fence()
    iden_b = sb(st, "idenb_s", [128, 128], BF16)
    p.do("gpsimd", lambda g: g.dma_start(out=iden_b[:], in_=w["iden_f"]),
         reads=[w["iden_f"]], writes=[iden_b], dma=True)
    psb = io.get("psb") or [st.enter_context(nc.psum_tensor(f"psb{i}", [128, 512], F32)) for i in range(8)]

    outs = io.get("wq_outs")
    if outs is None:
        outs = build_l1_ffn_wquant(p, st, io)
    wg_out, wu_out, wd_out = outs["wg_out"], outs["wu_out"], outs["wd_out"]

    wgT = [sb(st, f"wgT{c}", [128, I_FF], BF16) for c in range(NC)]
    wuT = [sb(st, f"wuT{c}", [128, I_FF], BF16) for c in range(NC)]
    wdT = [sb(st, f"wdT{c}", [128, D], BF16) for c in range(NIT)]
    wl = ExitStack()
    wrow = [sb(wl, f"wrow{i}", [128, I_FF], BF16) for i in range(2)]
    for t in range(NIT):
        r, j = t % 8, t // 8
        buf = wrow[t % 2]
        for which, pool, dstT in (("g", wg_out, wgT), ("u", wu_out, wuT)):
            p.do("sync", lambda s, buf=buf, pool=pool, r=r, j=j: s.dma_start(
                out=buf[:, 0:D], in_=pool[r * WG_TPC * 128 + j * 128: r * WG_TPC * 128 + (j + 1) * 128, :]),
                reads=[pool], writes=[buf], dma=True, tag="wgld")
            for c in range(NC):
                ps = psb[c % 2]
                p.do("tensor", lambda tt_, buf=buf, c=c, ps=ps: tt_.transpose(
                    ps.bitcast(BF16)[:, 0:128], buf[:, c * 128:(c + 1) * 128], iden_b[:]),
                    reads=[buf, iden_b], writes=[ps], tag="wgtr")
                p.do("vector", lambda v, ps=ps, c=c, t=t, dstT=dstT: v.tensor_copy(
                    out=dstT[c][:, t * 128:(t + 1) * 128], in_=ps.bitcast(BF16)[:, 0:128]),
                    reads=[ps], writes=[dstT[c]], tag="wgev")
    for r in range(8):
        buf = wrow[r % 2]
        p.do("sync", lambda s, buf=buf, r=r: s.dma_start(
            out=buf[:], in_=wd_out[r * 128:(r + 1) * 128, :]),
            reads=[wd_out], writes=[buf], dma=True, tag="wdld")
        for c in range(NIT):
            ps = psb[c % 2]
            p.do("tensor", lambda tt_, buf=buf, c=c, ps=ps: tt_.transpose(
                ps.bitcast(BF16)[:, 0:128], buf[:, c * 128:(c + 1) * 128], iden_b[:]),
                reads=[buf, iden_b], writes=[ps], tag="wdtr")
            p.do("vector", lambda v, ps=ps, c=c, r=r: v.tensor_copy(
                out=wdT[c][:, r * 128:(r + 1) * 128], in_=ps.bitcast(BF16)[:, 0:128]),
                reads=[ps], writes=[wdT[c]], tag="wdev")
    p.fence()
    wl.close()

    act = ExitStack()
    xt = [sb(act, f"xt{i}", [128, D], F32) for i in range(2)]
    ut = [sb(act, f"ut{i}", [128, D], F32) for i in range(2)]
    sqs = [sb(act, f"sqs{i}", [128, 12], F32) for i in range(2)]
    stat = [sb(act, f"stat{i}", [128, 8], F32) for i in range(2)]
    uq = [sb(act, f"uq{i}", [128, D], BF16) for i in range(2)]
    xq_fm = [[sb(act, f"xq{i}_{c}", [128, 128], BF16) for c in range(NC)] for i in range(2)]
    g_sb = [sb(act, f"g_sb{i}", [128, 512], BF16) for i in range(3)]
    h_seg = [sb(act, f"h_seg{i}", [128, 512], F32) for i in range(3)]
    hq = [sb(act, f"hq{i}", [128, I_FF], BF16) for i in range(2)]
    hq_fm = [[sb(act, f"hq{i}_{c}", [128, 128], BF16) for c in range(NIT)] for i in range(2)]
    scr2 = [{"bm": sb(act, f"q2bm{i}", [128, 128], F32),
             "xn": sb(act, f"q2xn{i}", [128, 512], F32),
             "m": sb(act, f"q2m{i}", [128, 512], F32)} for i in range(2)]
    scru = {"bm": sb(act, "qubm", [128, 128], F32),
            "xn": sb(act, "quxn", [128, D], F32),
            "m": sb(act, "qum", [128, D], F32)}
    ot = ut
    IC = [(0, 512), (512, 512), (1024, 512), (1536, 512), (2048, 512), (2560, 256)]
    final_stores = []

    def head(tt):
        i2 = tt % 2
        X, UT, UQ = xt[i2], ut[i2], uq[i2]
        p.do("sync", lambda s, tt=tt, X=X: s.dma_start(
            out=X[:], in_=x_ap[tt * 128:(tt + 1) * 128, :]), reads=[x_ap], writes=[X], dma=True, tag="ldx")
        emit_ln_tile(p, X, UT, stat[i2], sqs[i2])
        return emit_fp4_quant(p, _View(UT, 0, D), _View(UQ, 0, D),
                              {"bm": _View(scru["bm"], 0, 2 * (D // 16)), "xn": _View(scru["xn"], 0, D),
                               "m": _View(scru["m"], 0, D)}, D, pool_adds=True, defer_deq=True)

    u_deq = head(0)
    u_deq()
    u_deq = None
    for tt in range(NT_ALL):
        i2 = tt % 2
        X, UQ, HQ, OT = xt[i2], uq[i2], hq[i2], ot[i2]
        XQ, HQF = xq_fm[i2], hq_fm[i2]
        if tt + 1 < NT_ALL:
            u_deq = head(tt + 1)
        for c in range(NC):
            ps = psb[6 + c % 2]
            p.do("tensor", lambda t_, c=c, ps=ps, UQ=UQ: t_.transpose(
                ps.bitcast(BF16)[:, 0:128], UQ[:, c * 128:(c + 1) * 128], iden_b[:]),
                reads=[UQ, iden_b], writes=[ps], tag="xqtr")
            p.do("scalar", lambda s, c=c, ps=ps, XQ=XQ: s.copy(
                out=XQ[c][:], in_=ps.bitcast(BF16)[:, 0:128]),
                reads=[ps], writes=[XQ[c]], tag="xqev")
        def emit_hqtr(chunks):
            hbanks = (6, 7, 0, 1)
            for c in chunks:
                ps = psb[hbanks[c % 4]]
                p.do("tensor", lambda t_, c=c, ps=ps, HQ=HQ: t_.transpose(
                    ps.bitcast(BF16)[:, 0:128], HQ[:, c * 128:(c + 1) * 128], iden_b[:]),
                    reads=[HQ, iden_b], writes=[ps], tag="hqtr")
                p.do("scalar", lambda s, c=c, ps=ps, HQF=HQF: s.copy(
                    out=HQF[c][:], in_=ps.bitcast(BF16)[:, 0:128]),
                    reads=[ps], writes=[HQF[c]], tag="hqev")

        pend_deq = None
        pend_chunks = None
        for io_, (c0, cw) in enumerate(IC):
            pg = psb[io_ % 3]
            pu_ = psb[3 + io_ % 3]
            GS = g_sb[io_ % 3]
            SC2s = scr2[io_ % 2]
            for c in range(NC):
                p.do("tensor", lambda t_, c=c, c0=c0, cw=cw, pg=pg, XQ=XQ: t_.matmul(
                    pg[:, 0:cw], XQ[c][:], wgT[c][:, c0:c0 + cw],
                    start=(c == 0), stop=(c == NC - 1)), reads=[XQ[c], wgT[c]], writes=[pg], tag="gmm")
            p.do("scalar", lambda s, cw=cw, pg=pg, GS=GS: s.activation(
                out=GS[:, 0:cw], in_=pg[0:128, 0:cw], func=AF.Silu),
                reads=[pg], writes=[GS], tag="silu")
            for c in range(NC):
                p.do("tensor", lambda t_, c=c, c0=c0, cw=cw, pu_=pu_, XQ=XQ: t_.matmul(
                    pu_[:, 0:cw], XQ[c][:], wuT[c][:, c0:c0 + cw],
                    start=(c == 0), stop=(c == NC - 1)), reads=[XQ[c], wuT[c]], writes=[pu_], tag="umm")
            HS = h_seg[io_ % 3]
            p.do("vector", lambda v, cw=cw, pu_=pu_, HS=HS, GS=GS: v.tensor_tensor(
                out=HS[:, 0:cw], in0=pu_[0:128, 0:cw], in1=GS[:, 0:cw], op=ALU.mult),
                reads=[pu_, GS], writes=[HS], tag="hmul")
            d_ = emit_fp4_quant(p, _View(HS, 0, cw), _View(HQ, c0, cw),
                                {"bm": _View(SC2s["bm"], 0, 2 * (cw // 16)),
                                 "xn": _View(SC2s["xn"], 0, cw), "m": _View(SC2s["m"], 0, cw)},
                                cw, pool_adds=True, defer_deq=True)
            if io_ == 0 and u_deq is not None:
                u_deq()
                u_deq = None
            if pend_deq is not None:
                pend_deq()
                if io_ == 5:
                    emit_hqtr(range(0, 16))
            pend_deq = d_
        pend_deq()
        emit_hqtr(range(16, NIT))
        for oc in range(2):
            ps = psb[6 + oc]
            for c in range(NIT):
                p.do("tensor", lambda t_, c=c, oc=oc, ps=ps, HQF=HQF: t_.matmul(
                    ps[:], HQF[c][:], wdT[c][:, oc * 512:(oc + 1) * 512],
                    start=(c == 0), stop=(c == NIT - 1)), reads=[HQF[c], wdT[c]], writes=[ps], tag="dmm")
            p.do("vector", lambda v, oc=oc, X=X, ps=ps, OT=OT: v.tensor_tensor(
                out=OT[:, oc * 512:(oc + 1) * 512], in0=X[:, oc * 512:(oc + 1) * 512],
                in1=ps[:], op=ALU.add), reads=[X, ps], writes=[OT], tag="oadd")
        stx = p.do("sync", lambda s, tt=tt, OT=OT: s.dma_start(
            out=out_ap[tt * 128:(tt + 1) * 128, :], in_=OT[:]), reads=[OT], writes=[out_ap], dma=True, tag="stx")
        final_stores.append(stx)
    p.fence()
    act.close()
    return final_stores


class _View:
    def __init__(self, t, c0, n):
        self.t, self.c0, self.n = t, c0, n
        self.tensor = t[:].tensor

    def __getitem__(self, sl):
        return self.t[:, self.c0:self.c0 + self.n][sl]


def make_launch_d():
    nc = bass.Bass("TRN2", target_bir_lowering=False)
    st = ExitStack()
    io = {}
    io["x"] = nc.dram_tensor("x", [B_PER_CORE * T, D], F32, kind="ExternalInput").ap()
    io["out"] = nc.dram_tensor("out", [B_PER_CORE * T, D], F32, kind="ExternalOutput").ap()
    for name, shape in [("wg_rows", [WG_TPC * 128, D]), ("wu_rows", [WG_TPC * 128, D]),
                        ("wd_rows", [128, I_FF])]:
        io[name] = nc.dram_tensor(name, shape, F32, kind="ExternalInput").ap()
    for name in ("iden_f", "cvals"):
        io[name] = nc.dram_tensor(name, list(HOST_CONST_SHAPES[name]), F32, kind="ExternalInput").ap()
    p = Prog(nc)
    with st:
        finals = build_l1_ffn(p, st, io)
        p.do("gpsimd", lambda g: g.engine_nop(), deps=finals, tag="fence")
        with nc.Block() as block:
            p.emit(block, st)
    return nc


def l1_ffn_maps(inputs, x_full):
    consts = host_constants()
    xs = x_full.reshape(16, T, D)
    wg = inputs["L1_ffn_w_gate"]
    wu = inputs["L1_ffn_w_up"]
    wd = inputs["L1_ffn_w_down"]
    in_maps = []
    for c in range(NCORE):
        m = {"x": np.ascontiguousarray(xs[2 * c:2 * c + 2].reshape(B_PER_CORE * T, D))}
        rows = []
        for j in range(WG_TPC):
            t = c + 8 * j
            if t < NIT:
                rows.append(wg[t * 128:(t + 1) * 128])
            else:
                rows.append(np.zeros((128, D), np.float32))
        m["wg_rows"] = np.ascontiguousarray(np.concatenate(rows, 0))
        rows = []
        for j in range(WG_TPC):
            t = c + 8 * j
            if t < NIT:
                rows.append(wu[t * 128:(t + 1) * 128])
            else:
                rows.append(np.zeros((128, D), np.float32))
        m["wu_rows"] = np.ascontiguousarray(np.concatenate(rows, 0))
        m["wd_rows"] = np.ascontiguousarray(wd[c * 128:(c + 1) * 128])
        m["iden_f"] = consts["iden_f"]
        m["cvals"] = consts["cvals"]
        in_maps.append(m)
    return in_maps



def build_l1_attn(p, st, io):
    nc = p.nc
    pfx = io.get("pfx", "")
    NT, NC, NQC = T // 128, D // 128, T // 512
    x_ap, out_ap, w = io["x"], io["out"], io

    def sb(st_, name, shape, dt):
        return st_.enter_context(nc.sbuf_tensor(pfx + name, shape, dt))

    if not io.get("cvals_ready"):
        cvals_s = sb(st, "cvals_s", [128, len(CONST_VALS)], F32)
        p.do("gpsimd", lambda g: g.dma_start(out=cvals_s[:], in_=w["cvals"]),
             reads=[w["cvals"]], writes=[cvals_s], dma=True)
        for j, v in enumerate(CONST_VALS):
            nc.const_aps.register(F32, float(v), cvals_s[:, j:j + 1])
        p.fence()
    iden_f = sb(st, "idenf_s", [128, 128], F32)
    iden_b = sb(st, "idenb_s", [128, 128], BF16)
    diagm = sb(st, "diagm_s", [128, 896], BF16)
    for dst, src_ in ((iden_f, "iden_f"), (iden_b, "iden_f"), (diagm, "diagmask4")):
        p.do("gpsimd", lambda g, dst=dst, src_=src_: g.dma_start(out=dst[:], in_=w[src_]),
             reads=[w[src_]], writes=[dst], dma=True)
    psb = io.get("psb") or [st.enter_context(nc.psum_tensor(f"psb{i}", [128, 512], F32)) for i in range(8)]
    wtmp = [sb(st, f"wtmp{i}", [128, D], BF16) for i in range(2)]

    WT = {}
    for name in ("Wq", "Wk", "Wv"):
        buf = wtmp[0]
        p.do("gpsimd", lambda g, name=name, buf=buf: g.dma_start(out=buf[:], in_=w[name]),
             reads=[w[name]], writes=[buf], dma=True, tag="wld")
        wt = [sb(st, f"{name}T{c}", [128, 128], BF16) for c in range(NC)]
        for c in range(NC):
            ps = psb[c % 2]
            p.do("tensor", lambda t, buf=buf, c=c, ps=ps: t.transpose(
                ps.bitcast(BF16)[:, 0:128], buf[:, c * 128:(c + 1) * 128], iden_b[:]),
                reads=[buf, iden_b], writes=[ps], tag="wtr")
            p.do("vector", lambda v, ps=ps, c=c, wt=wt: v.tensor_copy(
                out=wt[c][:], in_=ps.bitcast(BF16)[:, 0:128]),
                reads=[ps], writes=[wt[c]], tag="wev")
        WT[name] = wt
    WoT = sb(st, "WoT", [128, D], BF16)
    for r in range(NC):
        buf = wtmp[1]
        p.do("gpsimd", lambda g, r=r, buf=buf: g.dma_start(
            out=buf[:, 0:128], in_=w["Wo"][r * 128:(r + 1) * 128, :]),
            reads=[w["Wo"]], writes=[buf], dma=True, tag="wold")
        ps = psb[2]
        p.do("tensor", lambda t, buf=buf, ps=ps: t.transpose(
            ps.bitcast(BF16)[:, 0:128], buf[:, 0:128], iden_b[:]),
            reads=[buf, iden_b], writes=[ps], tag="wotr")
        p.do("vector", lambda v, ps=ps, r=r: v.tensor_copy(
            out=WoT[:, r * 128:(r + 1) * 128], in_=ps.bitcast(BF16)[:, 0:128]),
            reads=[ps], writes=[WoT], tag="woev")
    onecol_cb = sb(st, "onecol_cb", [128, 1], BF16)
    p.do("gpsimd", lambda g: g.dma_start(out=onecol_cb[:], in_=w["ones_col"]),
         reads=[w["ones_col"]], writes=[onecol_cb], dma=True)
    bq_c = sb(st, "bq_c", [128, 1], F32)
    bk_c = sb(st, "bk_c", [128, 1], F32)
    bv_r = sb(st, "bv_r", [1, 128], BF16)
    bo_c = sb(st, "bo_c", [128, NC], F32)
    onerow_b = sb(st, "onerow_c", [1, 128], BF16)
    for dst, src_ in ((bq_c, "bq_col"), (bk_c, "bk_col"), (bv_r, "bv_row"), (bo_c, "bo_cols"),
                      (onerow_b, "ones_row")):
        p.do("gpsimd", lambda g, dst=dst, src_=src_: g.dma_start(out=dst[:], in_=w[src_]),
             reads=[w[src_]], writes=[dst], dma=True)

    SCL = 0.08838834764831845
    MR = 339.41125496954285
    final_stores = []
    for b in range(B_PER_CORE):
        xb = x_ap[b * T:(b + 1) * T, :]
        ob = out_ap[b * T:(b + 1) * T, :]
        bst = ExitStack()
        xt = [sb(bst, f"xt{b}_{i}", [128, D], F32) for i in range(2)]
        ut = [sb(bst, f"ut{b}_{i}", [128, D], F32) for i in range(2)]
        sqs = [sb(bst, f"sqs{b}_{i}", [128, 12], F32) for i in range(2)]
        stat = [sb(bst, f"stat{b}_{i}", [128, 8], F32) for i in range(2)]
        u_fm = [sb(bst, f"cu{b}_fm{c}", [128, T], BF16) for c in range(NC)]
        Q = sb(bst, f"Q{b}", [128, T], BF16)
        K = sb(bst, f"K{b}", [128, T], BF16)
        Vt = sb(bst, f"V{b}", [128, 16 * 128], BF16)
        qa = sb(bst, f"qa{b}", [128, T], BF16)
        mrow = sb(bst, f"mrow{b}", [1, T], BF16)
        kmb = sb(bst, f"km{b}", [128, 1], F32)
        kmbb = sb(bst, f"kmb{b}", [128, 1], BF16)
        p_sb = [sb(bst, f"p{b}_{i}", [128, 512], BF16) for i in range(3)]
        o_sb = sb(bst, f"o_sb{b}", [128, 512], BF16)
        rdenf = sb(bst, f"rdf{b}", [1, 512], F32)
        rden = sb(bst, f"rd{b}", [1, 512], BF16)
        attn_n = sb(bst, f"an{b}", [128, T], BF16)
        upd_fm = [sb(bst, f"up{b}_{c}", [128, T], BF16) for c in range(NC)]
        ot = [sb(bst, f"ot{b}_{i}", [128, D], F32) for i in range(2)]
        def c_head(tt):
            X, UT = xt[tt % 2], ut[tt % 2]
            p.do("sync", lambda s, tt=tt, X=X, xb=xb: s.dma_start(
                out=X[:], in_=xb[tt * 128:(tt + 1) * 128, :]), reads=[xb], writes=[X], dma=True, tag="ldx")
            emit_ln_tile(p, X, UT, stat[tt % 2], sqs[tt % 2])

        c_head(0)
        for tt in range(NT):
            UT = ut[tt % 2]
            if tt + 1 < NT:
                c_head(tt + 1)
            for c in range(NC):
                ps = psb[c % 2]
                p.do("tensor", lambda t, c=c, ps=ps, UT=UT: t.transpose(
                    ps[:, 0:128], UT[:, c * 128:(c + 1) * 128], iden_f[:]),
                    reads=[UT, iden_f], writes=[ps], tag="utr")
                p.do("vector", lambda v, tt=tt, c=c, ps=ps: v.tensor_copy(
                    out=u_fm[c][:, tt * 128:(tt + 1) * 128], in_=ps[:, 0:128]),
                    reads=[ps], writes=[u_fm[c]], tag="uev")
        for which, dst, bias in (("Wq", Q, bq_c), ("Wk", K, bk_c)):
            for qc in range(NQC):
                ps = psb[qc % 4]
                for c in range(NC):
                    p.do("tensor", lambda t, c=c, qc=qc, which=which, ps=ps: t.matmul(
                        ps[:], WT[which][c][:], u_fm[c][:, qc * 512:(qc + 1) * 512],
                        start=(c == 0), stop=(c == NC - 1)),
                        reads=[WT[which][c], u_fm[c]], writes=[ps], tag="qkproj")
                p.do("scalar", lambda s, qc=qc, ps=ps, dst=dst, bias=bias: s.activation(
                    out=dst[:, qc * 512:(qc + 1) * 512], in_=ps[:], func=AF.Identity, bias=bias[:]),
                    reads=[ps, bias], writes=[dst], tag="qkev")
        p.do("scalar", lambda s: s.activation(out=qa[:], in_=Q[:], func=AF.Abs),
             reads=[Q], writes=[qa], tag="qa")
        p.do("vector", lambda v: v.tensor_reduce(out=kmb[:], in_=K[:], axis=AX.X, op=ALU.max,
                                                 apply_absolute_value=True), reads=[K], writes=[kmb])
        p.do("vector", lambda v: v.tensor_copy(out=kmbb[:], in_=kmb[:]), reads=[kmb], writes=[kmbb])
        for qc in range(NQC):
            p.do("tensor", lambda t, qc=qc: t.matmul(
                psb[6][0:1, :], kmbb[:], qa[:, qc * 512:(qc + 1) * 512], start=True, stop=True),
                reads=[kmbb, qa], writes=[psb[6]], tag="mhat")
            p.do("scalar", lambda s, qc=qc: s.activation(
                out=mrow[:, qc * 512:(qc + 1) * 512], in_=psb[6][0:1, :], func=AF.Identity,
                scale=-1.0, bias=MR), reads=[psb[6]], writes=[mrow], tag="mrow")
        for tt in range(NT):
            ps = psb[4 + tt % 2]
            for c in range(NC):
                p.do("tensor", lambda t, c=c, tt=tt, ps=ps: t.matmul(
                    ps[:, 0:128], u_fm[c][:, tt * 128:(tt + 1) * 128], WT["Wv"][c][:],
                    start=(c == 0), stop=False), reads=[u_fm[c], WT["Wv"][c]], writes=[ps], tag="vproj")
            p.do("tensor", lambda t, tt=tt, ps=ps: t.matmul(
                ps[:, 0:128], onerow_b[:], bv_r[:], start=False, stop=True),
                reads=[onerow_b, bv_r], writes=[ps], tag="vbias")
            p.do("vector", lambda v, tt=tt, ps=ps: v.tensor_copy(
                out=Vt[:, tt * 128:(tt + 1) * 128], in_=ps[:, 0:128]),
                reads=[ps], writes=[Vt], tag="vev")
        def c_emit_S(kt, qc):
            psS = psb[kt % 3]
            P = p_sb[kt % 3]
            diag = kt >= 4 * qc
            if diag:
                v_i = kt - 4 * qc
                p.do("tensor", lambda t, v_i=v_i, psS=psS: t.matmul(
                    psS[:], iden_b[:],
                    diagm[:, (3 - v_i) * 128:(3 - v_i) * 128 + 512],
                    start=True, stop=False), reads=[iden_b, diagm], writes=[psS], tag="Smask")
            p.do("tensor", lambda t, kt=kt, qc=qc, psS=psS, diag=diag: t.matmul(
                psS[:], K[:, kt * 128:(kt + 1) * 128], Q[:, qc * 512:(qc + 1) * 512],
                start=not diag, stop=False), reads=[K, Q], writes=[psS], tag="S")
            p.do("tensor", lambda t, qc=qc, psS=psS: t.matmul(
                psS[:], onerow_b[:], mrow[:, qc * 512:(qc + 1) * 512],
                start=False, stop=True), reads=[onerow_b, mrow], writes=[psS], tag="Sm")
            p.do("scalar", lambda s, psS=psS, P=P: s.activation(
                out=P[:], in_=psS[:], func=AF.Exp, scale=SCL),
                reads=[psS], writes=[P], tag="exp")

        def c_emit_PV(kt, qc, nkt):
            P = p_sb[kt % 3]
            pvb = psb[4 + 2 * (qc % 2)]
            dnb = psb[5 + 2 * (qc % 2)]
            p.do("tensor", lambda t, kt=kt, P=P, nkt=nkt, pvb=pvb: t.matmul(
                pvb[:], Vt[:, kt * 128:(kt + 1) * 128].opt(), P[:],
                start=(kt == 0), stop=(kt == nkt - 1)), reads=[Vt, P], writes=[pvb], tag="PV")
            p.do("tensor", lambda t, kt=kt, P=P, nkt=nkt, dnb=dnb: t.matmul(
                dnb[0:1, :], onecol_cb[:], P[:],
                start=(kt == 0), stop=(kt == nkt - 1)), reads=[onecol_cb, P], writes=[dnb], tag="den")

        def c_make_tail(qc, nkt):
            pvb = psb[4 + 2 * (qc % 2)]
            dnb = psb[5 + 2 * (qc % 2)]
            bcb = psb[(nkt - 1) % 3]

            def tail():
                p.do("scalar", lambda s, dnb=dnb: s.activation(out=rdenf[:], in_=dnb[0:1, :], func=AF.Ln),
                     reads=[dnb], writes=[rdenf], tag="rdl")
                p.do("scalar", lambda s: s.activation(out=rden[:], in_=rdenf[:], func=AF.Exp, scale=-1.0),
                     reads=[rdenf], writes=[rden], tag="rde")
                p.do("tensor", lambda t, bcb=bcb: t.matmul(
                    bcb[:], onerow_b[:], rden[:], start=True, stop=True),
                    reads=[onerow_b, rden], writes=[bcb], tag="bcast")
                p.do("scalar", lambda s, pvb=pvb: s.copy(out=o_sb[:], in_=pvb[:]),
                     reads=[pvb], writes=[o_sb], tag="oev")
                p.do("vector", lambda v, qc=qc, bcb=bcb: v.tensor_tensor(
                    out=attn_n[:, qc * 512:(qc + 1) * 512], in0=o_sb[:], in1=bcb[:], op=ALU.mult),
                    reads=[o_sb, bcb], writes=[attn_n], tag="an")
            return tail

        pend_tail = None
        for qc in range(NQC):
            nkt = 4 * qc + 4
            for kt in range(nkt):
                c_emit_S(kt, qc)
                if kt == 2 and pend_tail is not None:
                    pend_tail()
                    pend_tail = None
                if kt >= 2:
                    c_emit_PV(kt - 2, qc, nkt)
            c_emit_PV(nkt - 2, qc, nkt)
            c_emit_PV(nkt - 1, qc, nkt)
            pend_tail = c_make_tail(qc, nkt)
        pend_tail()
        for dc in range(NC):
            for qc in range(NQC):
                ps = psb[4 + (dc * NQC + qc) % 2]
                p.do("tensor", lambda t, dc=dc, qc=qc, ps=ps: t.matmul(
                    ps[:], WoT[:, dc * 128:(dc + 1) * 128], attn_n[:, qc * 512:(qc + 1) * 512],
                    start=True, stop=True), reads=[WoT, attn_n], writes=[ps], tag="oproj")
                p.do("scalar", lambda s, dc=dc, qc=qc, ps=ps: s.activation(
                    out=upd_fm[dc][:, qc * 512:(qc + 1) * 512], in_=ps[:], func=AF.Identity,
                    bias=bo_c[:, dc:dc + 1]), reads=[ps, bo_c], writes=[upd_fm[dc]], tag="opev")
        for tt in range(NT):
            i = tt % 2
            X, O = xt[i], ot[i]
            p.do("sync", lambda s, tt=tt, X=X, xb=xb: s.dma_start(
                out=X[:], in_=xb[tt * 128:(tt + 1) * 128, :]), reads=[xb], writes=[X], dma=True, tag="ldx2")
            for c in range(NC):
                pu = psb[6 + c % 2]
                p.do("tensor", lambda t, tt=tt, c=c, pu=pu: t.transpose(
                    pu.bitcast(BF16)[:, 0:128], upd_fm[c][:, tt * 128:(tt + 1) * 128], iden_b[:]),
                    reads=[upd_fm[c], iden_b], writes=[pu], tag="trO")
                p.do("vector", lambda v, c=c, X=X, O=O, pu=pu: v.tensor_tensor(
                    out=O[:, c * 128:(c + 1) * 128], in0=X[:, c * 128:(c + 1) * 128],
                    in1=pu.bitcast(BF16)[:, 0:128], op=ALU.add), reads=[X, pu], writes=[O], tag="oadd")
            stx = p.do("sync", lambda s, tt=tt, O=O, ob=ob: s.dma_start(
                out=ob[tt * 128:(tt + 1) * 128, :], in_=O[:]), reads=[O], writes=[ob], dma=True, tag="stx")
            final_stores.append(stx)
        p.fence()
        bst.close()
    return final_stores





def make_launch_c():
    nc = bass.Bass("TRN2", target_bir_lowering=False)
    st = ExitStack()
    io = {}
    io["x"] = nc.dram_tensor("x", [B_PER_CORE * T, D], F32, kind="ExternalInput").ap()
    io["out"] = nc.dram_tensor("out", [B_PER_CORE * T, D], F32, kind="ExternalOutput").ap()
    for name, shape in [("Wq", [128, D]), ("Wk", [128, D]), ("Wv", [128, D]), ("Wo", [D, 128]),
                        ("bq_col", [128, 1]), ("bk_col", [128, 1]), ("bv_row", [1, 128]),
                        ("bo_cols", [128, 8])]:
        io[name] = nc.dram_tensor(name, shape, F32, kind="ExternalInput").ap()
    for name in ("iden_f", "cvals", "diagmask4", "ones_row", "ones_col"):
        io[name] = nc.dram_tensor(name, list(HOST_CONST_SHAPES[name]), F32, kind="ExternalInput").ap()
    p = Prog(nc)
    with st:
        finals = build_l1_attn(p, st, io)
        p.do("gpsimd", lambda g: g.engine_nop(), deps=finals, tag="fence")
        with nc.Block() as block:
            p.emit(block, st)
    return nc


def l1_attn_maps(inputs, x_full):
    consts = host_constants()
    xs = x_full.reshape(16, T, D)
    in_maps = []
    for c in range(NCORE):
        m = {"x": np.ascontiguousarray(xs[2 * c:2 * c + 2].reshape(B_PER_CORE * T, D))}
        for k in ("Wq", "Wk", "Wv", "Wo"):
            m[k] = inputs["L1_attn_" + k]
        m["bq_col"] = np.ascontiguousarray(inputs["L1_attn_bq"].reshape(128, 1))
        m["bk_col"] = np.ascontiguousarray(inputs["L1_attn_bk"].reshape(128, 1))
        m["bv_row"] = np.ascontiguousarray(inputs["L1_attn_bv"].reshape(1, 128))
        m["bo_cols"] = np.ascontiguousarray(inputs["L1_attn_bo"].reshape(8, 128).T)
        for k in ("iden_f", "cvals", "diagmask4", "ones_row", "ones_col"):
            m[k] = consts[k]
        in_maps.append(m)
    return in_maps



CAP = 5121
RTILES = 41
FEXP = 4096


def build_l0_ffn(p, st, io):
    nc = p.nc
    pfx = io.get("pfx", "")
    NT_ALL = (B_PER_CORE * T) // 128
    NC = D // 128
    x_ap, out_ap, w = io["x"], io["out"], io

    def sb(st_, name, shape, dt):
        return st_.enter_context(nc.sbuf_tensor(pfx + name, shape, dt))

    if not io.get("cvals_ready"):
        cvals_s = sb(st, "cvals_s", [128, len(CONST_VALS)], F32)
        p.do("gpsimd", lambda g: g.dma_start(out=cvals_s[:], in_=w["cvals"]),
             reads=[w["cvals"]], writes=[cvals_s], dma=True)
        for j, v in enumerate(CONST_VALS):
            nc.const_aps.register(F32, float(v), cvals_s[:, j:j + 1])
        p.fence()
    iden_f = sb(st, "idenf_s", [128, 128], F32)
    iden_b = sb(st, "idenb_s", [128, 128], BF16)
    my1 = sb(st, "my1_s", [8, 1], F32)
    io8 = sb(st, "io8_s", [128, 8], F32)
    ST8 = sb(st, "ST8_s", [8, 8], F32)
    rep8 = sb(st, "rep8_s", [8, 8 * 128], F32)
    ones_row_f = sb(st, "ones_row_f", [1, 128], F32)
    for dst, src_ in ((iden_f, "iden_f"), (iden_b, "iden_f"), (my1, "my1_col"), (io8, "iota8m8"),
                      (ST8, "st8"), (rep8, "rep8"), (ones_row_f, "ones_row")):
        p.do("gpsimd", lambda g, dst=dst, src_=src_: g.dma_start(out=dst[:], in_=w[src_]),
             reads=[w[src_]], writes=[dst], dma=True)
    psb = io.get("psb") or [st.enter_context(nc.psum_tensor(f"psb{i}", [128, 512], F32)) for i in range(8)]

    lsort = io["lsort"]
    pool = io["pool"]
    cnt_in = io["cnt_in"]
    cnt_out = io["cnt_out"]
    ybuf = io["ybuf"]
    ypool = io["ypool"]
    ustage = io["ustage"]

    wgc = [sb(st, f"wgc{c}", [128, 8], BF16) for c in range(NC)]
    for c in range(NC):
        p.do("gpsimd", lambda g, c=c: g.dma_start(out=wgc[c][:], in_=w["wg"][c * 128:(c + 1) * 128, :]),
             reads=[w["wg"]], writes=[wgc[c]], dma=True, tag="wgld", skey="wgc")
    p.do("gpsimd", lambda g: g.engine_nop(), reads=[], writes=wgc, tag="wgcbar")

    cw_all = sb(st, "cw_all", [128, NT_ALL], F32)
    t1_all = sb(st, "t1_all", [128, NT_ALL], F32)
    rk_all = sb(st, "rk_all", [128, NT_ALL], F32)

    rph = ExitStack()
    xt = [sb(rph, f"xt{i}", [128, D], F32) for i in range(2)]
    ut_ = [sb(rph, f"ut{i}", [128, D], F32) for i in range(2)]
    ub_ = [sb(rph, f"ub{i}", [128, D], BF16) for i in range(2)]
    sqs_ = [sb(rph, f"sqs{i}", [128, 12], F32) for i in range(2)]
    stat_ = [sb(rph, f"stat{i}", [128, 8], F32) for i in range(2)]
    ufc_ = [sb(rph, f"ufc{i}", [128, 128], BF16) for i in range(2)]
    mask_ = [sb(rph, f"mask{i}", [128, 8], F32) for i in range(2)]
    et_ = [sb(rph, f"et{i}", [128, 8], F32) for i in range(2)]
    tmp8_ = [sb(rph, f"tmp8{i}", [128, 8], F32) for i in range(2)]
    maskT = sb(rph, "maskT", [8, NT_ALL * 128], F32)
    exclT = sb(rph, "exclT", [8, NT_ALL * 128], F32)
    zer8 = sb(rph, "zer8", [8, NT_ALL * 128], F32)
    rloc_all = sb(rph, "rloc_all", [128, NT_ALL], F32)
    p.do("vector", lambda v: v.memset(zer8[:], 1.0), reads=[], writes=[zer8], unsafe=True)

    def r_head(tt):
        X = xt[tt % 2]
        ut, ub, sqs, stat = ut_[tt % 2], ub_[tt % 2], sqs_[tt % 2], stat_[tt % 2]
        p.do("sync", lambda s, tt=tt, X=X: s.dma_start(
            out=X[:], in_=x_ap[tt * 128:(tt + 1) * 128, :]), reads=[x_ap], writes=[X], dma=True, tag="ldx")
        emit_ln_tile(p, X, ut, stat, sqs)
        p.do("scalar", lambda s, ut=ut, ub=ub: s.activation(out=ub[:], in_=ut[:], func=AF.Identity),
             reads=[ut], writes=[ub], tag="ub")

    r_head(0)
    for tt in range(NT_ALL):
        ut, ub, sqs, stat = ut_[tt % 2], ub_[tt % 2], sqs_[tt % 2], stat_[tt % 2]
        mask, et, tmp8 = mask_[tt % 2], et_[tt % 2], tmp8_[tt % 2]
        if tt + 1 < NT_ALL:
            r_head(tt + 1)
        for c in range(NC):
            ps7 = psb[6 + c % 2]
            ufc = ufc_[c % 2]
            p.do("tensor", lambda t_, c=c, ub=ub, ps7=ps7: t_.transpose(
                ps7.bitcast(BF16)[:, 0:128], ub[:, c * 128:(c + 1) * 128], iden_b[:]),
                reads=[ub, iden_b], writes=[ps7], tag="rtr")
            p.do("vector", lambda v, c=c, ps7=ps7, ufc=ufc: v.tensor_copy(out=ufc[:], in_=ps7.bitcast(BF16)[:, 0:128]),
                 reads=[ps7], writes=[ufc], tag="rev")
            p.do("tensor", lambda t_, c=c, ufc=ufc: t_.matmul(
                psb[5][:, 0:8], ufc[:], wgc[c][:], start=(c == 0), stop=(c == NC - 1)),
                reads=[ufc, wgc[c]], writes=[psb[5]], tag="rmm")
        p.do("vector", lambda v, stat=stat: v.tensor_reduce(out=stat[:, 0:1], in_=psb[5][:, 0:8], axis=AX.X,
                                                 op=ALU.max), reads=[psb[5]], writes=[stat], tag="lmax")
        p.do("vector", lambda v, stat=stat: v.tensor_scalar(out=stat[:, 1:2], in0=stat[:, 0:1], scalar1=-1.0,
                                                 scalar2=None, op0=ALU.mult), reads=[stat], writes=[stat])
        p.do("scalar", lambda s, stat=stat, et=et: s.activation(out=et[:], in_=psb[5][:, 0:8], func=AF.Exp,
                                              bias=stat[:, 1:2]), reads=[psb[5], stat], writes=[et], tag="et")
        p.do("vector", lambda v, stat=stat, et=et: v.tensor_reduce(out=stat[:, 2:3], in_=et[:], axis=AX.X, op=ALU.add),
             reads=[et], writes=[stat])
        p.do("vector", lambda v, tt=tt, stat=stat: v.reciprocal(out=cw_all[:, tt:tt + 1], in_=stat[:, 2:3]),
             reads=[stat], writes=[cw_all], tag="cw")
        p.do("vector", lambda v, mask=mask, stat=stat: v.tensor_scalar(out=mask[:], in0=psb[5][:, 0:8], scalar1=stat[:, 0:1],
                                                 scalar2=None, op0=ALU.is_ge),
             reads=[psb[5], stat], writes=[mask], tag="mge")
        p.do("vector", lambda v, mask=mask, tmp8=tmp8: v.scalar_tensor_tensor(out=tmp8[:], in0=mask[:], scalar=1.0,
                                                        in1=io8[:], op0=ALU.mult, op1=ALU.mult),
             reads=[mask, io8], writes=[tmp8])
        p.do("vector", lambda v, tmp8=tmp8: v.tensor_scalar(out=tmp8[:], in0=tmp8[:], scalar1=8.0, scalar2=None,
                                                 op0=ALU.add), reads=[tmp8], writes=[tmp8])
        p.do("vector", lambda v, tt=tt, tmp8=tmp8: v.tensor_reduce(out=t1_all[:, tt:tt + 1], in_=tmp8[:],
                                                        axis=AX.X, op=ALU.min),
             reads=[tmp8], writes=[t1_all], tag="top1")
        p.do("vector", lambda v, mask=mask: v.tensor_scalar(out=mask[:], in0=io8[:], scalar1=8.0, scalar2=None,
                                                 op0=ALU.add), reads=[io8], writes=[mask])
        p.do("vector", lambda v, tt=tt, mask=mask: v.tensor_scalar(
            out=mask[:], in0=mask[:], scalar1=t1_all[:, tt:tt + 1], scalar2=None, op0=ALU.is_equal),
            reads=[mask, t1_all], writes=[mask], tag="onehot")
        ps7 = psb[4]
        p.do("tensor", lambda t_, tt=tt, mask=mask, ps7=ps7: t_.transpose(ps7[0:8, 0:128], mask[:], iden_f[:]),
             reads=[mask, iden_f], writes=[ps7], tag="mtr")
        p.do("vector", lambda v, tt=tt, ps7=ps7: v.tensor_copy(out=maskT[:, tt * 128:(tt + 1) * 128],
                                                      in_=ps7[0:8, 0:128]),
             reads=[ps7], writes=[maskT], tag="mtev")
        p.do("sync", lambda s, tt=tt, ub=ub: s.dma_start(
            out=ustage[tt * 128:(tt + 1) * 128, :], in_=ub[:]),
            reads=[ub], writes=[ustage], dma=True, tag="ust")
    p.do("vector", lambda v: v.tensor_tensor_scan(out=exclT[:], data0=zer8[:], data1=maskT[:],
                                                  initial=0.0, op0=ALU.mult, op1=ALU.add),
         reads=[zer8, maskT], writes=[exclT], tag="prefix")
    p.do("vector", lambda v: v.tensor_tensor(out=exclT[:], in0=exclT[:], in1=maskT[:],
                                             op=ALU.subtract), reads=[exclT, maskT], writes=[exclT])
    cntrow = sb(rph, "cntrow", [1, 8], F32)
    p.do("tensor", lambda t_: t_.transpose(psb[7][0:1, 0:8],
                                           exclT[:, NT_ALL * 128 - 1:NT_ALL * 128], iden_f[0:8, 0:8]),
         reads=[exclT, iden_f], writes=[psb[7]], tag="ctr")
    p.do("tensor", lambda t_: t_.transpose(psb[6][0:1, 0:8],
                                           maskT[:, NT_ALL * 128 - 1:NT_ALL * 128], iden_f[0:8, 0:8]),
         reads=[maskT, iden_f], writes=[psb[6]], tag="ctr2")
    p.do("vector", lambda v: v.tensor_copy(out=cntrow[:], in_=psb[7][0:1, 0:8]),
         reads=[psb[7]], writes=[cntrow], tag="cntc")
    p.do("vector", lambda v: v.tensor_tensor(out=cntrow[:], in0=cntrow[:],
                                             in1=psb[6][0:1, 0:8], op=ALU.add),
         reads=[cntrow, psb[6]], writes=[cntrow], tag="cnt")
    p.do("sync", lambda s: s.dma_start(out=cnt_in, in_=cntrow[:]),
         reads=[cntrow], writes=[cnt_in], dma=True, tag="cntst")
    cc = p.do("gpsimd", lambda g: g.collective_compute(
        "AllGather", ALU.bypass, replica_groups=[list(range(8))],
        ins=[cnt_in.opt()], outs=[cnt_out.opt()]),
        reads=[cnt_in], writes=[cnt_out], tag="agcnt")
    cc.amount = 1
    cc.sem_key = "cc_cnt"

    def counts_math(st_, sfx):
        cnts = sb(st_, "cnts" + sfx, [8, 8], F32)
        p.do("sync", lambda s: s.dma_start(out=cnts[:], in_=cnt_out),
             reads=[cnt_out], writes=[cnts], dma=True, tag="cld" + sfx)
        offs = sb(st_, "offs" + sfx, [8, 8], F32)
        p.do("tensor", lambda t_: t_.matmul(psb[6][0:8, 0:8], ST8[:], cnts[:], start=True, stop=True),
             reads=[ST8, cnts], writes=[psb[6]], tag="offs" + sfx)
        p.do("vector", lambda v: v.tensor_copy(out=offs[:], in_=psb[6][0:8, 0:8]),
             reads=[psb[6]], writes=[offs])
        lbh = sb(st_, "lbh" + sfx, [8, 8], F32)
        zsm = sb(st_, "zsm" + sfx, [8, 8], F32)
        p.do("vector", lambda v: v.memset(zsm[:], 1.0), reads=[], writes=[zsm], unsafe=True)
        p.do("vector", lambda v: v.tensor_tensor_scan(out=lbh[:], data0=zsm[:], data1=cnts[:],
                                                      initial=0.0, op0=ALU.mult, op1=ALU.add),
             reads=[zsm, cnts], writes=[lbh], tag="lbh" + sfx)
        p.do("vector", lambda v: v.tensor_tensor(out=lbh[:], in0=lbh[:], in1=cnts[:], op=ALU.subtract),
             reads=[lbh, cnts], writes=[lbh])
        return cnts, offs, lbh

    def myrow(st_, mat, name):
        out = sb(st_, name, [1, 8], F32)
        p.do("tensor", lambda t_, mat=mat: t_.matmul(psb[5][0:1, 0:8], my1[:], mat[:],
                                                     start=True, stop=True),
             reads=[my1, mat], writes=[psb[5]], tag=name)
        p.do("vector", lambda v: v.tensor_copy(out=out[:], in_=psb[5][0:1, 0:8]),
             reads=[psb[5]], writes=[out])
        return out

    def bcast_row(st_, row, name):
        out = sb(st_, name, [128, 8], F32)
        p.do("tensor", lambda t_, row=row: t_.matmul(psb[4][:, 0:8], ones_row_f[:], row[:],
                                                     start=True, stop=True),
             reads=[ones_row_f, row], writes=[psb[4]], tag=name)
        p.do("vector", lambda v: v.tensor_copy(out=out[:], in_=psb[4][:, 0:8]),
             reads=[psb[4]], writes=[out])
        return out

    cnts, offs, lbh = counts_math(rph, "r")
    myoffB = bcast_row(rph, myrow(rph, offs, "myoff"), "myoffB")
    mylbB = bcast_row(rph, myrow(rph, lbh, "mylb"), "mylbB")
    mask_all = sb(rph, "mask_all", [128, 8 * NT_ALL], F32)
    exm_all = sb(rph, "exm_all", [128, 8 * NT_ALL], F32)
    tmp_all = sb(rph, "tmp_all", [128, 8 * NT_ALL], F32)
    m8 = sb(rph, "m8", [128, 8], F32)
    posf_all = sb(rph, "posf_all", [128, NT_ALL], F32)
    posu_all = sb(rph, "posu_all", [128, NT_ALL], U32)
    ubs_ = [sb(rph, f"ubs{i}", [128, D], BF16) for i in range(2)]
    M3 = mask_all[:].rearrange("p (t e) -> p t e", e=8)
    E3 = exm_all[:].rearrange("p (t e) -> p t e", e=8)
    T3 = tmp_all[:].rearrange("p (t e) -> p t e", e=8)
    p.do("vector", lambda v: v.tensor_scalar(out=m8[:], in0=io8[:], scalar1=8.0, scalar2=None,
                                             op0=ALU.add), reads=[io8], writes=[m8])
    p.do("vector", lambda v: v.tensor_copy(
        out=mask_all[:].rearrange("p (e t) -> p e t", t=NT_ALL),
        in_=m8[:].unsqueeze(2).broadcast_to([128, 8, NT_ALL])), reads=[m8], writes=[mask_all], tag="m8rep")
    p.do("vector", lambda v: v.tensor_copy(
        out=tmp_all[:].rearrange("p (e t) -> p e t", t=NT_ALL),
        in_=mask_all[:].rearrange("p (t e) -> p t e", e=8).transpose_last2()
        if False else mask_all[:].rearrange("p (e t) -> p e t", t=NT_ALL)),
        reads=[mask_all], writes=[tmp_all])
    p.do("vector", lambda v: v.tensor_tensor(
        out=M3, in0=tmp_all[:].rearrange("p (e t) -> p t e", t=NT_ALL),
        in1=t1_all[:].unsqueeze(2).broadcast_to([128, NT_ALL, 8]), op=ALU.is_equal),
        reads=[tmp_all, t1_all], writes=[mask_all], tag="ohw")
    for tt in range(NT_ALL):
        ps7 = psb[6 + tt % 2]
        p.do("tensor", lambda t_, tt=tt, ps7=ps7: t_.transpose(
            ps7[0:128, 0:8], exclT[:, tt * 128:(tt + 1) * 128], iden_f[0:8, 0:8]),
            reads=[exclT, iden_f], writes=[ps7], tag="extr")
        p.do("vector", lambda v, tt=tt, ps7=ps7: v.tensor_copy(
            out=exm_all[:, tt * 8:(tt + 1) * 8], in_=ps7[0:128, 0:8]),
            reads=[ps7], writes=[exm_all], tag="exev")
    p.do("vector", lambda v: v.tensor_tensor(out=exm_all[:], in0=exm_all[:], in1=mask_all[:],
         op=ALU.mult), reads=[exm_all, mask_all], writes=[exm_all])
    p.do("vector", lambda v: v.tensor_reduce(out=rloc_all[:], in_=E3, axis=AX.X, op=ALU.add),
         reads=[exm_all], writes=[rloc_all], tag="rloc")
    p.do("vector", lambda v: v.tensor_tensor(
        out=T3, in0=M3, in1=myoffB[:].unsqueeze(1).broadcast_to([128, NT_ALL, 8]), op=ALU.mult),
        reads=[mask_all, myoffB], writes=[tmp_all])
    p.do("vector", lambda v: v.tensor_reduce(out=rk_all[:], in_=T3, axis=AX.X, op=ALU.add),
         reads=[tmp_all], writes=[rk_all])
    p.do("vector", lambda v: v.tensor_tensor(out=rk_all[:], in0=rk_all[:], in1=rloc_all[:],
         op=ALU.add), reads=[rk_all, rloc_all], writes=[rk_all], tag="rank")
    p.do("vector", lambda v: v.tensor_tensor(
        out=T3, in0=M3, in1=mylbB[:].unsqueeze(1).broadcast_to([128, NT_ALL, 8]), op=ALU.mult),
        reads=[mask_all, mylbB], writes=[tmp_all])
    p.do("vector", lambda v: v.tensor_reduce(out=posf_all[:], in_=T3, axis=AX.X, op=ALU.add),
         reads=[tmp_all], writes=[posf_all])
    p.do("vector", lambda v: v.tensor_tensor(out=posf_all[:], in0=posf_all[:], in1=rloc_all[:],
         op=ALU.add), reads=[posf_all, rloc_all], writes=[posf_all], tag="pos")
    p.do("vector", lambda v: v.tensor_copy(out=posu_all[:], in_=posf_all[:]),
         reads=[posf_all], writes=[posu_all])
    for tt in range(NT_ALL):
        ubs = ubs_[tt % 2]
        p.do("sync", lambda s, tt=tt, ubs=ubs: s.dma_start(
            out=ubs[:], in_=ustage[tt * 128:(tt + 1) * 128, :]),
            reads=[ustage], writes=[ubs], dma=True, tag="uld")
        p.do("gpsimd", lambda g, tt=tt, ubs=ubs: g.indirect_dma_start(
            out=lsort, out_offset=bass.IndirectOffsetOnAxis(ap=posu_all[:, tt:tt + 1], axis=0),
            in_=ubs[:], in_offset=None), reads=[ubs, posu_all], writes=[lsort], dma=True, tag="scat")
    cc2 = p.do("gpsimd", lambda g: g.collective_compute(
        "AllGather", ALU.bypass, replica_groups=[list(range(8))],
        ins=[lsort.opt()], outs=[pool.opt()]),
        reads=[lsort], writes=[pool], tag="agpool")
    cc2.amount = 1
    cc2.sem_key = "cc_pool"
    p.fence()
    rph.close()

    eph = ExitStack()
    W1c = [sb(eph, f"W1c{c}", [128, FEXP], BF16) for c in range(NC)]
    W2c = [sb(eph, f"W2c{c}", [128, D], BF16) for c in range(32)]
    for c in range(NC):
        p.do("gpsimd", lambda g, c=c: g.dma_start(out=W1c[c][:], in_=w["W1"][c * 128:(c + 1) * 128, :]),
             reads=[w["W1"]], writes=[W1c[c]], dma=True, tag="w1ld", skey="wexp")
    for c in range(32):
        p.do("gpsimd", lambda g, c=c: g.dma_start(out=W2c[c][:], in_=w["W2"][c * 128:(c + 1) * 128, :]),
             reads=[w["W2"]], writes=[W2c[c]], dma=True, tag="w2ld", skey="wexp")
    p.do("gpsimd", lambda g: g.engine_nop(), reads=[], writes=W1c + W2c, tag="wexpbar")
    b1c = sb(eph, "b1c", [128, 32], F32)
    b2r = sb(eph, "b2r", [1, D], BF16)
    ones_row_bb = sb(eph, "ones_row_bb", [1, 128], BF16)
    for dst, src_ in ((b1c, "b1_cols"), (b2r, "b2_row"), (ones_row_bb, "ones_row")):
        p.do("gpsimd", lambda g, dst=dst, src_=src_: g.dma_start(out=dst[:], in_=w[src_]),
             reads=[w[src_]], writes=[dst], dma=True)
    iota_r = sb(eph, "iota_r", [128, RTILES], I32)
    iota_f = sb(eph, "iota_f", [128, RTILES], F32)
    p.do("gpsimd", lambda g: g.iota(iota_r[:], pattern=[[128, RTILES]], base=0, channel_multiplier=1),
         reads=[], writes=[iota_r], unsafe=True, tag="iotar")
    p.do("vector", lambda v: v.tensor_copy(out=iota_f[:], in_=iota_r[:]),
         reads=[iota_r], writes=[iota_f])
    cnts2, offs2, lbh2 = counts_math(eph, "e")
    cols3 = sb(eph, "cols3", [8, 3], F32)
    matT = sb(eph, "matT", [8, 8], F32)
    for j, mat in enumerate((offs2, lbh2, cnts2)):
        p.do("tensor", lambda t_, mat=mat: t_.transpose(psb[5][0:8, 0:8], mat[:], iden_f[0:8, 0:8]),
             reads=[mat, iden_f], writes=[psb[5]], tag="colT")
        p.do("vector", lambda v: v.tensor_copy(out=matT[:], in_=psb[5][0:8, 0:8]),
             reads=[psb[5]], writes=[matT])
        p.do("tensor", lambda t_, j=j: t_.matmul(psb[4][0:8, j:j + 1], matT[:], my1[:],
                                                 start=True, stop=True),
             reads=[matT, my1], writes=[psb[4]], tag="colmm")
        p.do("vector", lambda v, j=j: v.tensor_copy(out=cols3[:, j:j + 1], in_=psb[4][0:8, j:j + 1]),
             reads=[psb[4]], writes=[cols3])
    idxf = sb(eph, "idxf", [128, RTILES], F32)
    idxu = sb(eph, "idxu", [128, RTILES], U32)
    bci = sb(eph, "bci", [128, 3], F32)
    segs = sb(eph, "segs", [128, RTILES], F32)
    segv = sb(eph, "segv", [128, RTILES], F32)
    p.do("vector", lambda v: v.memset(idxf[:], 0.0), reads=[], writes=[idxf], unsafe=True)
    for i in range(8):
        p.do("tensor", lambda t_, i=i: t_.matmul(psb[3][:, 0:3], rep8[:, i * 128:(i + 1) * 128],
                                                 cols3[:], start=True, stop=True),
             reads=[rep8, cols3], writes=[psb[3]], tag="bci")
        p.do("vector", lambda v: v.tensor_copy(out=bci[:], in_=psb[3][:, 0:3]),
             reads=[psb[3]], writes=[bci])
        p.do("vector", lambda v: v.tensor_tensor(out=bci[:, 2:3], in0=bci[:, 0:1], in1=bci[:, 2:3],
                                                 op=ALU.add), reads=[bci], writes=[bci], tag="end")
        p.do("vector", lambda v, i=i: v.scalar_tensor_tensor(
            out=bci[:, 1:2], in0=bci[:, 1:2], scalar=float(i * 4096), in1=bci[:, 0:1],
            op0=ALU.add, op1=ALU.subtract), reads=[bci], writes=[bci], tag="gval")
        p.do("vector", lambda v: v.tensor_scalar(out=segs[:], in0=iota_f[:], scalar1=bci[:, 0:1],
                                                 scalar2=None, op0=ALU.is_ge),
             reads=[iota_f, bci], writes=[segs])
        p.do("vector", lambda v: v.tensor_scalar(out=segv[:], in0=iota_f[:], scalar1=bci[:, 2:3],
                                                 scalar2=None, op0=ALU.is_lt),
             reads=[iota_f, bci], writes=[segv])
        p.do("vector", lambda v: v.tensor_tensor(out=segs[:], in0=segs[:], in1=segv[:], op=ALU.mult),
             reads=[segs, segv], writes=[segs])
        p.do("vector", lambda v: v.tensor_scalar(out=segv[:], in0=iota_f[:], scalar1=bci[:, 1:2],
                                                 scalar2=None, op0=ALU.add),
             reads=[iota_f, bci], writes=[segv])
        p.do("vector", lambda v: v.scalar_tensor_tensor(out=segv[:], in0=segs[:], scalar=1.0,
                                                        in1=segv[:], op0=ALU.mult, op1=ALU.mult),
             reads=[segs, segv], writes=[segv])
        p.do("vector", lambda v: v.tensor_tensor(out=idxf[:], in0=idxf[:], in1=segv[:], op=ALU.add),
             reads=[idxf, segv], writes=[idxf], tag="idxacc")
    p.do("vector", lambda v: v.tensor_copy(out=idxu[:], in_=idxf[:]),
         reads=[idxf], writes=[idxu], tag="idxu")
    ypool_half = [io["ypool"], io["ypool2"]]
    zrow = sb(eph, "zrow", [1, D], BF16)
    p.do("vector", lambda v: v.memset(zrow[:], 0.0), reads=[], writes=[zrow], unsafe=True)
    p.do("sync", lambda s: s.dma_start(
        out=ypool_half[1][8 * (RTILES * 128 - 2624):8 * (RTILES * 128 - 2624) + 1, :], in_=zrow[:]),
        reads=[zrow], writes=[ypool_half[1]], dma=True, tag="zsent")
    NST = 11
    xrow = [sb(eph, f"xrow{i}", [128, D], BF16) for i in range(2)]
    xfm = [[sb(eph, f"xfm{i}_{c}", [128, 512], BF16) for c in range(NC)] for i in range(2)]
    hfm = [sb(eph, f"hfm{c}", [128, 512], BF16) for c in range(32)]
    yrow = [sb(eph, f"yrow{i}", [128, D], BF16) for i in range(2)]
    ypool_half = [io["ypool"], io["ypool2"]]
    for stt_ in range(NST):
        r0 = stt_ * 4
        nrt = min(4, RTILES - r0)
        XF = xfm[stt_ % 2]
        for j in range(nrt):
            rt = r0 + j
            XR = xrow[rt % 2]
            p.do("gpsimd", lambda g, rt=rt, XR=XR: g.indirect_dma_start(
                out=XR[:], out_offset=None, in_=pool,
                in_offset=bass.IndirectOffsetOnAxis(ap=idxu[:, rt:rt + 1], axis=0)),
                reads=[pool, idxu], writes=[XR], dma=True, tag="xgat")
            for c in range(NC):
                ps = psb[6 + (j * NC + c) % 2]
                p.do("tensor", lambda t_, c=c, XR=XR, ps=ps: t_.transpose(
                    ps.bitcast(BF16)[:, 0:128], XR[:, c * 128:(c + 1) * 128], iden_b[:]),
                    reads=[XR, iden_b], writes=[ps], tag="xtr")
                p.do("vector", lambda v, c=c, j=j, ps=ps, XF=XF: v.tensor_copy(
                    out=XF[c][:, j * 128:(j + 1) * 128], in_=ps.bitcast(BF16)[:, 0:128]),
                    reads=[ps], writes=[XF[c]], tag="xev")
        nr = nrt * 128
        for fc in range(32):
            ps = psb[fc % 2]
            for c in range(NC):
                p.do("tensor", lambda t_, c=c, fc=fc, ps=ps, XF=XF, nr=nr: t_.matmul(
                    ps[:, 0:nr], W1c[c][:, fc * 128:(fc + 1) * 128], XF[c][:, 0:nr],
                    start=(c == 0), stop=(c == NC - 1)), reads=[W1c[c], XF[c]], writes=[ps], tag="h1")
            p.do("scalar", lambda s, fc=fc, ps=ps, nr=nr: s.activation(
                out=hfm[fc][:, 0:nr], in_=ps[:, 0:nr], func=AF.Gelu, bias=b1c[:, fc:fc + 1]),
                reads=[ps, b1c], writes=[hfm[fc]], tag="gelu")
        for j in range(nrt):
            rt = r0 + j
            YR = yrow[rt % 2]
            for oc in range(2):
                ps = psb[4 + oc]
                for fc in range(32):
                    p.do("tensor", lambda t_, fc=fc, oc=oc, ps=ps, j=j: t_.matmul(
                        ps[:], hfm[fc][:, j * 128:(j + 1) * 128], W2c[fc][:, oc * 512:(oc + 1) * 512],
                        start=(fc == 0), stop=False), reads=[hfm[fc], W2c[fc]], writes=[ps], tag="h2")
                p.do("tensor", lambda t_, oc=oc, ps=ps: t_.matmul(
                    ps[:], ones_row_bb[:], b2r[:, oc * 512:(oc + 1) * 512],
                    start=False, stop=True), reads=[ones_row_bb, b2r], writes=[ps], tag="b2")
            p.do("vector", lambda v, YR=YR: v.tensor_copy(out=YR[:, 0:512], in_=psb[4][:]),
                 reads=[psb[4]], writes=[YR], tag="yev0")
            p.do("scalar", lambda s, YR=YR: s.copy(out=YR[:, 512:1024], in_=psb[5][:]),
                 reads=[psb[5]], writes=[YR], tag="yev1")
            p.do("sync", lambda s, rt=rt, YR=YR: s.dma_start(
                out=ybuf[rt * 128:(rt + 1) * 128, :], in_=YR[:]),
                reads=[YR], writes=[ybuf], dma=True, tag="yst")
        if stt_ == 4:
            pass
        if stt_ == 5:
            cc3a = p.do("gpsimd", lambda g: g.collective_compute(
                "AllGather", ALU.bypass, replica_groups=[list(range(8))],
                ins=[ybuf[0:2624, :].opt()], outs=[ypool_half[0].opt()]),
                reads=[ybuf], writes=[ypool_half[0]], tag="agy1")
            cc3a.amount = 1
            cc3a.sem_key = "cc_y1"
    cc3b = p.do("gpsimd", lambda g: g.collective_compute(
        "AllGather", ALU.bypass, replica_groups=[list(range(8))],
        ins=[ybuf[2624:RTILES * 128, :].opt()], outs=[ypool_half[1][0:8 * (RTILES * 128 - 2624), :].opt()]),
        reads=[ybuf], writes=[ypool_half[1]], tag="agy2")
    cc3b.amount = 1
    cc3b.sem_key = "cc_y2"
    p.fence()
    eph.close()

    HROWS = 2624
    fph = ExitStack()
    xt2 = [sb(fph, f"x2t{i}", [128, D], F32) for i in range(2)]
    ygat = [sb(fph, f"ygat{i}", [128, D], BF16) for i in range(2)]
    rf1 = sb(fph, "rf1", [128, NT_ALL], F32)
    rf2 = sb(fph, "rf2", [128, NT_ALL], F32)
    ovA = sb(fph, "ovA", [128, NT_ALL], F32)
    ruA = sb(fph, "ruA", [128, 2 * NT_ALL], U32)
    ot_ = [sb(fph, f"fot{i}", [128, D], F32) for i in range(2)]
    p.do("vector", lambda v: v.scalar_tensor_tensor(
        out=rf1[:], in0=t1_all[:], scalar=float(HROWS), in1=rk_all[:],
        op0=ALU.mult, op1=ALU.add), reads=[t1_all, rk_all], writes=[rf1], tag="rid1")
    p.do("vector", lambda v: v.tensor_scalar(
        out=ovA[:], in0=rk_all[:], scalar1=float(HROWS) - 0.5, scalar2=1e9,
        op0=ALU.is_gt, op1=ALU.mult), reads=[rk_all], writes=[ovA], tag="ovr1")
    p.do("vector", lambda v: v.tensor_tensor(out=rf1[:], in0=rf1[:], in1=ovA[:],
         op=ALU.add), reads=[rf1, ovA], writes=[rf1])
    p.do("vector", lambda v: v.tensor_copy(out=ruA[:, 0:NT_ALL], in_=rf1[:]),
         reads=[rf1], writes=[ruA])
    p.do("vector", lambda v: v.scalar_tensor_tensor(
        out=rf2[:], in0=t1_all[:], scalar=float(RTILES * 128 - HROWS), in1=rk_all[:],
        op0=ALU.mult, op1=ALU.add), reads=[t1_all, rk_all], writes=[rf2], tag="rid2")
    p.do("vector", lambda v: v.tensor_scalar(
        out=rf2[:], in0=rf2[:], scalar1=float(HROWS), scalar2=None, op0=ALU.subtract),
        reads=[rf2], writes=[rf2])
    p.do("vector", lambda v: v.tensor_scalar(
        out=ovA[:], in0=rk_all[:], scalar1=float(HROWS) - 0.5, scalar2=1e9,
        op0=ALU.is_le, op1=ALU.mult), reads=[rk_all], writes=[ovA], tag="ovr2a")
    p.do("vector", lambda v: v.tensor_tensor(out=rf2[:], in0=rf2[:], in1=ovA[:],
         op=ALU.add), reads=[rf2, ovA], writes=[rf2])
    p.do("vector", lambda v: v.tensor_scalar(
        out=ovA[:], in0=rk_all[:], scalar1=float(CAP) - 0.5, scalar2=1e9,
        op0=ALU.is_gt, op1=ALU.mult), reads=[rk_all], writes=[ovA], tag="ovr2b")
    p.do("vector", lambda v: v.tensor_tensor(out=rf2[:], in0=rf2[:], in1=ovA[:],
         op=ALU.add), reads=[rf2, ovA], writes=[rf2])
    p.do("vector", lambda v: v.tensor_scalar(
        out=rf2[:], in0=rf2[:], scalar1=float(8 * (RTILES * 128 - HROWS)), scalar2=None,
        op0=ALU.min), reads=[rf2], writes=[rf2], tag="clmp")
    p.do("vector", lambda v: v.tensor_copy(out=ruA[:, NT_ALL:2 * NT_ALL], in_=rf2[:]),
         reads=[rf2], writes=[ruA])
    final_stores = []
    for tt in range(NT_ALL):
        i2 = tt % 2
        X, YG, OT = xt2[i2], ygat[i2], ot_[i2]
        p.do("sync", lambda s, tt=tt, X=X: s.dma_start(
            out=X[:], in_=x_ap[tt * 128:(tt + 1) * 128, :]), reads=[x_ap], writes=[X], dma=True, tag="ldx2")
        p.do("gpsimd", lambda g, tt=tt, YG=YG: g.indirect_dma_start(
            out=YG[:], out_offset=None, in_=ypool_half[1],
            in_offset=bass.IndirectOffsetOnAxis(ap=ruA[:, NT_ALL + tt:NT_ALL + tt + 1], axis=0)),
            reads=[ypool_half[1], ruA], writes=[YG], dma=True, tag="ygat2")
        p.do("gpsimd", lambda g, tt=tt, YG=YG: g.indirect_dma_start(
            out=YG[:], out_offset=None, in_=ypool_half[0],
            in_offset=bass.IndirectOffsetOnAxis(ap=ruA[:, tt:tt + 1], axis=0),
            bounds_check=8 * HROWS - 1, oob_is_err=False),
            reads=[ypool_half[0], ruA], writes=[YG], dma=True, tag="ygat1")
        p.do("vector", lambda v, tt=tt, X=X, YG=YG, OT=OT: v.scalar_tensor_tensor(
            out=OT[:], in0=YG[:], scalar=cw_all[:, tt:tt + 1], in1=X[:],
            op0=ALU.mult, op1=ALU.add), reads=[YG, cw_all, X], writes=[OT], tag="comb")
        stx = p.do("sync", lambda s, tt=tt, OT=OT: s.dma_start(
            out=out_ap[tt * 128:(tt + 1) * 128, :], in_=OT[:]),
            reads=[OT], writes=[out_ap], dma=True, tag="stx")
        final_stores.append(stx)
    p.fence()
    fph.close()
    return final_stores


def make_launch_b():
    nc = bass.Bass("TRN2", target_bir_lowering=False)
    st = ExitStack()
    io = {}
    io["x"] = nc.dram_tensor("x", [B_PER_CORE * T, D], F32, kind="ExternalInput").ap()
    io["out"] = nc.dram_tensor("out", [B_PER_CORE * T, D], F32, kind="ExternalOutput").ap()
    for name, shape in [("wg", [D, 8]), ("W1", [D, FEXP]), ("W2", [FEXP, D]),
                        ("b1_cols", [128, 32]), ("b2_row", [1, D]),
                        ("my1_col", [8, 1]), ("iota8m8", [128, 8]), ("st8", [8, 8]),
                        ("rep8", [8, 8 * 128])]:
        io[name] = nc.dram_tensor(name, shape, F32, kind="ExternalInput").ap()
    for name in ("iden_f", "cvals", "ones_row"):
        io[name] = nc.dram_tensor(name, list(HOST_CONST_SHAPES[name]), F32, kind="ExternalInput").ap()
    io["lsort"] = nc.dram_tensor("lsort", [B_PER_CORE * T, D], BF16).ap()
    io["pool"] = nc.dram_tensor("pool", [8 * B_PER_CORE * T, D], BF16, addr_space="Shared").ap()
    io["cnt_in"] = nc.dram_tensor("cnt_in", [1, 8], F32).ap()
    io["cnt_out"] = nc.dram_tensor("cnt_out", [8, 8], F32).ap()
    io["ybuf"] = nc.dram_tensor("ybuf", [RTILES * 128, D], BF16).ap()
    io["ypool"] = nc.dram_tensor("ypool", [8 * 2624, D], BF16, addr_space="Shared").ap()
    io["ypool2"] = nc.dram_tensor("ypool2", [8 * (RTILES * 128 - 2624) + 1, D], BF16, addr_space="Shared").ap()
    io["ustage"] = nc.dram_tensor("ustage", [B_PER_CORE * T, D], BF16).ap()
    io["dbg"] = nc.dram_tensor("dbg", [128, 104 + RTILES], F32, kind="ExternalOutput").ap()
    p = Prog(nc)
    with st:
        finals = build_l0_ffn(p, st, io)
        p.do("gpsimd", lambda g: g.engine_nop(), deps=finals, tag="fence")
        with nc.Block() as block:
            p.emit(block, st)
    return nc


def l0_ffn_maps(inputs, x_full):
    consts = host_constants()
    xs = x_full.reshape(16, T, D)
    in_maps = []
    io8 = np.broadcast_to(np.arange(8, dtype=np.float32) - 8.0, (128, 8)).copy()
    st8 = np.tril(np.ones((8, 8), np.float32), -1).T.copy()
    rep8 = np.zeros((8, 8 * 128), np.float32)
    for i in range(8):
        rep8[i, i * 128:(i + 1) * 128] = 1.0
    for c in range(NCORE):
        m = {"x": np.ascontiguousarray(xs[2 * c:2 * c + 2].reshape(B_PER_CORE * T, D))}
        m["wg"] = inputs["L0_ffn_wg"]
        m["W1"] = np.ascontiguousarray(inputs["L0_ffn_W1"][c])
        m["W2"] = np.ascontiguousarray(inputs["L0_ffn_W2"][c])
        m["b1_cols"] = np.ascontiguousarray(inputs["L0_ffn_b1"][c].reshape(32, 128).T)
        m["b2_row"] = np.ascontiguousarray(inputs["L0_ffn_b2"][c].reshape(1, D))
        m["my1_col"] = np.eye(8, dtype=np.float32)[:, c:c + 1].copy()
        m["iota8m8"] = io8
        m["st8"] = st8
        m["rep8"] = rep8
        for k in ("iden_f", "cvals", "ones_row"):
            m[k] = consts[k]
        in_maps.append(m)
    return in_maps



A_KEYS = ["qkv_w", "o_w", "in_w", "out_w", "Bp_w", "Cp_w", "dt_w", "gate_w",
          "n1_cols", "n2_cols", "A_cols", "dtb_col"]
B_KEYS = ["wg", "W1", "W2", "b1_cols", "b2_row", "my1_col", "iota8m8", "st8", "rep8"]
C_KEYS = ["Wq", "Wk", "Wv", "Wo", "bq_col", "bk_col", "bv_row", "bo_cols"]
D_KEYS = ["wg_rows", "wu_rows", "wd_rows"]
SHARED_KEYS = list(HOST_CONST_SHAPES)

A_SHAPES = {"qkv_w": [3072, D], "o_w": [D, D], "in_w": [64, D], "out_w": [D, 64],
            "Bp_w": [16, 64], "Cp_w": [16, 64], "dt_w": [64, 64], "gate_w": [64, D],
            "n1_cols": [128, 8], "n2_cols": [128, 8], "A_cols": [128, 8], "dtb_col": [64, 1]}
B_SHAPES = {"wg": [D, 8], "W1": [D, FEXP], "W2": [FEXP, D], "b1_cols": [128, 32],
            "b2_row": [1, D], "my1_col": [8, 1], "iota8m8": [128, 8], "st8": [8, 8],
            "rep8": [8, 8 * 128]}
C_SHAPES = {"Wq": [128, D], "Wk": [128, D], "Wv": [128, D], "Wo": [D, 128],
            "bq_col": [128, 1], "bk_col": [128, 1], "bv_row": [1, 128], "bo_cols": [128, 8]}
D_SHAPES = {"wg_rows": [WG_TPC * 128, D], "wu_rows": [WG_TPC * 128, D], "wd_rows": [128, I_FF]}


def make_fused():
    nc = bass.Bass("TRN2", target_bir_lowering=False)
    st = ExitStack()
    shared = {}
    for name in SHARED_KEYS:
        shared[name] = nc.dram_tensor(name, list(HOST_CONST_SHAPES[name]), F32,
                                      kind="ExternalInput").ap()
    x0 = nc.dram_tensor("x", [B_PER_CORE * T, D], F32, kind="ExternalInput").ap()
    out = nc.dram_tensor("out", [B_PER_CORE * T, D], F32, kind="ExternalOutput").ap()
    x1 = nc.dram_tensor("x1", [B_PER_CORE * T, D], F32).ap()
    x2 = nc.dram_tensor("x2", [B_PER_CORE * T, D], F32).ap()
    x3 = nc.dram_tensor("x3", [B_PER_CORE * T, D], F32).ap()

    def decl(prefix, shapes):
        d = {}
        for k, shp in shapes.items():
            d[k] = nc.dram_tensor(prefix + k, shp, F32, kind="ExternalInput").ap()
        return d

    io_a = decl("a_", A_SHAPES)
    io_b = decl("b_", B_SHAPES)
    io_c = decl("c_", C_SHAPES)
    io_d = decl("d_", D_SHAPES)
    io_b["lsort"] = nc.dram_tensor("b_lsort", [B_PER_CORE * T, D], BF16).ap()
    io_b["pool"] = nc.dram_tensor("b_pool", [8 * B_PER_CORE * T, D], BF16, addr_space="Shared").ap()
    io_b["cnt_in"] = nc.dram_tensor("b_cnt_in", [1, 8], F32).ap()
    io_b["cnt_out"] = nc.dram_tensor("b_cnt_out", [8, 8], F32).ap()
    io_b["ybuf"] = nc.dram_tensor("b_ybuf", [RTILES * 128, D], BF16).ap()
    io_b["ypool"] = nc.dram_tensor("b_ypool", [8 * 2624, D], BF16, addr_space="Shared").ap()
    io_b["ypool2"] = nc.dram_tensor("b_ypool2", [8 * (RTILES * 128 - 2624) + 1, D], BF16, addr_space="Shared").ap()
    io_b["ustage"] = nc.dram_tensor("b_ustage", [B_PER_CORE * T, D], BF16).ap()

    p = Prog(nc)
    with st:
        psb8 = [st.enter_context(nc.psum_tensor(f"psb{i}", [128, 512], F32)) for i in range(8)]
        psO = psb8[7][0:65, :]
        cvals_s = st.enter_context(nc.sbuf_tensor("cvals_s", [128, len(CONST_VALS)], F32))
        p.do("gpsimd", lambda g: g.dma_start(out=cvals_s[:], in_=shared["cvals"]),
             reads=[shared["cvals"]], writes=[cvals_s], dma=True, tag="cvals")
        for j, v in enumerate(CONST_VALS):
            nc.const_aps.register(F32, float(v), cvals_s[:, j:j + 1])
        p.fence()

        def run(builder, io_x, pfx, xin, xout):
            io_full = dict(shared)
            io_full.update(io_x)
            io_full["x"] = xin
            io_full["out"] = xout
            io_full["pfx"] = pfx
            io_full["psb"] = psb8
            io_full["psO"] = psO
            io_full["cvals_ready"] = True
            ph = ExitStack()
            finals = builder(p, ph, io_full)
            p.fence()
            ph.close()
            return finals

        io_dw = dict(shared)
        io_dw.update(io_d)
        io_dw["pfx"] = "d_"
        io_dw["psb"] = psb8
        io_dw["cvals_ready"] = True
        wq_st = ExitStack()
        io_d["wq_outs"] = build_l1_ffn_wquant(p, wq_st, io_dw)
        p.fence()
        wq_st.close()
        run(build_l0_attn, io_a, "a_", x0, x1)
        run(build_l0_ffn, io_b, "b_", x1, x2)
        run(build_l1_attn, io_c, "c_", x2, x3)
        finals = run(build_l1_ffn, io_d, "d_", x3, out)
        p.do("gpsimd", lambda g: g.engine_nop(), deps=finals, tag="final")
        with nc.Block() as block:
            p.emit(block, st)
    return nc


def fused_maps(inputs):
    ma = l0_attn_maps(inputs)
    mb = l0_ffn_maps(inputs, np.zeros((16, T, D), np.float32))
    mc = l1_attn_maps(inputs, np.zeros((16, T, D), np.float32))
    md = l1_ffn_maps(inputs, np.zeros((16, T, D), np.float32))
    consts = host_constants()
    out = []
    for c in range(NCORE):
        m = {}
        for k in SHARED_KEYS:
            m[k] = consts[k]
        m["x"] = ma[c]["x"]
        for k in A_KEYS:
            m["a_" + k] = ma[c][k]
        for k in B_KEYS:
            m["b_" + k] = mb[c][k]
        for k in C_KEYS:
            m["c_" + k] = mc[c][k]
        for k in D_KEYS:
            m["d_" + k] = md[c][k]
        out.append(m)
    return out


def kernel_fused(**inputs):
    nc = make_fused()
    res = run_bass_kernel_spmd(nc, fused_maps(inputs), core_ids=list(range(NCORE)))
    return _gather_out(res)
